# version 04
# baseline (speedup 1.0000x reference)
import os
import numpy as np
import ml_dtypes
import concourse.bass as bass
import concourse.mybir as mybir
from concourse.bass_utils import run_bass_kernel_spmd
from concourse.bass_core_rust import IndirectOffsetOnAxis

F32 = mybir.dt.float32
BF16 = mybir.dt.bfloat16
I32 = mybir.dt.int32
AF = mybir.ActivationFunctionType
ALU = mybir.AluOpType
AX = mybir.AxisListType

NCORES = 8
_PHASE_LIMIT = None
_N_SUB = None
_SUB_START = 0
_X_OVERRIDE = None
_DBG_TAPS = {}
T = 2048
D = 1024
NT = T // 512
KC = D // 128

NDMASEM = 8
_ENGS = ["pe", "act", "dve", "gpc"]
_DMAQ = {"sp": "sync", "gp": "gpsimd"}
PHYS_SECT = {"pe": "tensor", "act": "scalar", "dve": "vector", "gpc": "gpsimd"}
for _q, _sect in _DMAQ.items():
    for _i in range(NDMASEM):
        PHYS_SECT[f"{_q}{_i}"] = _sect
ENG_SECT = PHYS_SECT
SEM_SEG_LIMIT = 24000
BF_POOL = 88704
F32_POOL = 4 * 1024 + 768


class _StopEmit(Exception):
    pass


class Prog:

    def __init__(self, nc):
        self.nc = nc
        self.counts = {e: 0 for e in ENG_SECT}
        self.sems = {e: [nc.alloc_semaphore(f"S_{e}_0")] for e in ENG_SECT}
        self.seg_base = {e: [0] for e in ENG_SECT}
        self.last_tok = {e: None for e in ENG_SECT}
        self.pend = []
        self.nphase = 0
        self.dma_rr = {q: 0 for q in _DMAQ}

    def op(self, e, fn, deps=(), inc=1, chain=True):
        alldeps = [d for d in deps if d is not None]
        if chain and e != "pe" and self.last_tok[e] is not None:
            alldeps.append(self.last_tok[e])
        self.counts[e] += (inc if inc else 0)
        tok = (e, self.counts[e])
        self.pend.append((e, fn, tuple(alldeps), inc))
        if inc:
            self.last_tok[e] = tok
        return tok

    def pe(self, fn, deps=(), inc=1):
        return self.op("pe", fn, deps, inc)

    def act(self, fn, deps=()):
        return self.op("act", fn, deps)

    def dve(self, fn, deps=()):
        return self.op("dve", fn, deps)

    def gp(self, fn, deps=(), chain=True):
        return self.op("gpc", fn, deps, chain=chain)

    def dma(self, q, out_ap, in_ap, deps=()):
        e = f"{q}{self.dma_rr[q]}"
        self.dma_rr[q] = (self.dma_rr[q] + 1) % NDMASEM
        gate = [self.last_tok[e]]
        return self.op(e, lambda E: E.dma_start(out=out_ap, in_=in_ap),
                       list(deps) + gate, inc=16, chain=False)

    def load(self, out_ap, in_ap, deps=(), q="sp"):
        return self.dma(q, out_ap, in_ap, deps)

    def loadc(self, out_ap, in_ap, deps=()):
        return self.dma("gp", out_ap, in_ap, deps)

    def gpdma(self, fn, deps=()):
        e = f"gp{self.dma_rr['gp']}"
        self.dma_rr["gp"] = (self.dma_rr["gp"] + 1) % NDMASEM
        gate = [self.last_tok[e]]
        return self.op(e, fn, list(deps) + gate, inc=16, chain=False)

    def _locate(self, e, v):
        bases = self.seg_base[e]
        for i in range(len(bases) - 1, -1, -1):
            if v > bases[i]:
                return self.sems[e][i], v - bases[i]
        raise AssertionError(f"bad token ({e},{v})")

    def flush(self):
        pend, self.pend = self.pend, []
        if not pend:
            return
        per = {}
        for rec in pend:
            per.setdefault(PHYS_SECT[rec[0]], []).append(rec)
        self.nphase += 1
        with self.nc.Block(f"ph{self.nphase}") as blk:
            for e, items in per.items():
                def mk(items=items):
                    def body(E):
                        maxw = {}
                        for (le, fn, deps, inc) in items:
                            for (de, dv) in deps:
                                if maxw.get(de, 0) >= dv:
                                    continue
                                maxw[de] = dv
                                sem, lv = self._locate(de, dv)
                                E.wait_ge(sem, lv)
                            ins = fn(E)
                            if inc:
                                (ins if ins is not None else E.nop()).then_inc(
                                    self.sems[le][-1], inc)
                    return body
                getattr(blk, e)(mk())
        for e in ENG_SECT:
            if self.counts[e] - self.seg_base[e][-1] > SEM_SEG_LIMIT:
                s = self.nc.alloc_semaphore(f"S_{e}_{len(self.sems[e])}")
                self.sems[e].append(s)
                self.seg_base[e].append(self.counts[e])


class Ctx:
    def __init__(self, nc, w):
        self.nc = nc
        self.P = Prog(nc)
        self.w = w
        self.pb = [nc.alloc_psum_tensor(f"pb{i}", [128, 512], F32)
                   for i in range(7)]
        self.pt = nc.alloc_psum_tensor("pt", [128, 1024], BF16)
        self.bank_users = {}
        self.pool_bf = nc.alloc_sbuf_tensor("pool_bf", [128, BF_POOL], BF16)
        self.pool_f = nc.alloc_sbuf_tensor("pool_f", [128, F32_POOL], F32)
        self.off_bf = 0
        self.off_f = 0
        self.scopes = []
        self.hi_bf = 0
        self.hi_f = 0

    def push(self):
        self.scopes.append((self.off_bf, self.off_f))

    def pop(self):
        self.off_bf, self.off_f = self.scopes.pop()

    def bf(self, n, parts=128):
        n = (n + 15) & ~15
        off = self.off_bf
        self.off_bf += n
        assert self.off_bf <= BF_POOL, f"bf pool overflow {self.off_bf}"
        self.hi_bf = max(self.hi_bf, self.off_bf)
        return self.pool_bf[0:parts, off:off + n]

    def f32(self, n, parts=128):
        n = (n + 7) & ~7
        off = self.off_f
        self.off_f += n
        assert self.off_f <= F32_POOL, f"f32 pool overflow {self.off_f}"
        self.hi_f = max(self.hi_f, self.off_f)
        return self.pool_f[0:parts, off:off + n]

    def flush(self, dma_barrier=True):
        P = self.P
        if dma_barrier:
            for q in _DMAQ:
                toks = [(f"{q}{i}", P.counts[f"{q}{i}"]) for i in range(NDMASEM)
                        if P.counts[f"{q}{i}"] > 0]
                if toks:
                    P.op(f"{q}0", lambda E: E.nop(), deps=toks, inc=0,
                         chain=False)
        P.flush()
        self.bank_users = {}
        if _PHASE_LIMIT is not None and self.P.nphase >= _PHASE_LIMIT:
            raise _StopEmit()

    def bank_deps(self, i):
        return list(self.bank_users.get(i, ()))

    def bank_use(self, i, tok):
        self.bank_users[i] = [tok]

    def bank_use_add(self, i, tok):
        self.bank_users.setdefault(i, []).append(tok)

    CONST_VALS = (1e-5, -12.0, 0.5, -0.5, 127.0, 1.0 / 1024.0, 1e-6)

    def cst(self, v, parts=128):
        i = self.CONST_VALS.index(v)
        return self.cmisc[0:parts, i:i + 1]

    def get_sel16(self):
        if not hasattr(self, "_sel16"):
            P, nc = self.P, self.nc
            self._sel16 = nc.alloc_sbuf_tensor("c_sel16", [16, 16 * 128], BF16)
            self.push()
            io = self.f32(2048, parts=16)
            iov = io.rearrange("p (n j) -> p n j", n=16)
            t3 = P.gp(lambda E: E.iota(iov, pattern=[[1, 16], [0, 128]],
                                       base=0, channel_multiplier=-1,
                                       allow_small_or_imprecise_dtypes=True))
            t4 = P.dve(lambda E, io=io: E.tensor_single_scalar(
                io, io, 0.0, ALU.is_equal), deps=[t3])
            P.dve(lambda E, io=io: E.tensor_copy(self._sel16[:], io),
                  deps=[t4])
            self.flush()
            self.pop()
        return self._sel16

    def emit_consts(self):
        P, nc = self.P, self.nc
        self.cmisc = nc.alloc_sbuf_tensor("c_misc", [128, len(self.CONST_VALS)],
                                          F32)
        for i, v in enumerate(self.CONST_VALS):
            P.dve(lambda E, i=i, v=v: E.memset(self.cmisc[:, i:i + 1], v))
        self.ident = nc.alloc_sbuf_tensor("c_ident", [128, 128], BF16)
        self.ones_col = nc.alloc_sbuf_tensor("c_ones_col", [128, 1], BF16)
        self.mean_col = nc.alloc_sbuf_tensor("c_mean_col", [128, 1], BF16)
        self.ones_row = nc.alloc_sbuf_tensor("c_ones_row", [1, 128], BF16)
        self.identf = nc.alloc_sbuf_tensor("c_identf", [128, 128], F32)
        self.ones_rowf = nc.alloc_sbuf_tensor("c_ones_rowf", [1, 128], F32)
        self.i32t = nc.alloc_sbuf_tensor("c_i32", [128, 512], I32)
        self.push()
        io = self.f32(2048)
        t1 = P.gp(lambda E: E.iota(io[:, 0:128], pattern=[[1, 128]], base=0,
                                   channel_multiplier=-1,
                                   allow_small_or_imprecise_dtypes=True))
        t2 = P.dve(lambda E: E.tensor_single_scalar(io[:, 0:128], io[:, 0:128],
                                                    0.0, ALU.is_equal),
                   deps=[t1])
        t2b = P.dve(lambda E: E.tensor_copy(self.ident[:], io[:, 0:128]),
                    deps=[t2])
        t2c = P.dve(lambda E: E.tensor_copy(self.identf[:], io[:, 0:128]),
                    deps=[t2])
        P.dve(lambda E: E.memset(self.ones_col[:], 1.0))
        P.dve(lambda E: E.memset(self.mean_col[:], 1.0 / 1024.0))
        P.dve(lambda E: E.memset(self.ones_row[:], 1.0))
        P.dve(lambda E: E.memset(self.ones_rowf[:], 1.0))
        self.flush()
        self.pop()


def emit_ln(ctx, x_dram, xln, xbf_scr=None, sq_scr=None, f32_out_dram=None):
    P = ctx.P
    ctx.push()
    xbf = xbf_scr if xbf_scr is not None else ctx.bf(KC * T)
    sqbf = sq_scr if sq_scr is not None else ctx.bf(KC * T)
    stat = ctx.f32(4 * 512, parts=1)
    m_row, q_row = stat[:, 0:512], stat[:, 512:1024]
    v_row, r_row = stat[:, 1024:1536], stat[:, 1536:2048]
    mbf = ctx.bf(2 * 512, parts=1)
    rep = ctx.f32(2 * 512)
    m_rep, r_rep = rep[:, 0:512], rep[:, 512:1024]
    tmp = ctx.f32(512)

    tload = P.loadc(xbf.rearrange("p (c t) -> p c t", c=KC),
                    x_dram.rearrange("(c p) t -> p c t", p=128))
    tsq = []
    for c in range(KC):
        tsq.append(P.act(lambda E, c=c, sqbf=sqbf, xbf=xbf: E.activation(
            sqbf[:, c * T:(c + 1) * T], xbf[:, c * T:(c + 1) * T], AF.Square),
            deps=[tload]))
    out_toks = []
    for tc in range(NT):
        ts = tc * 512
        ev = []
        for half, (src, dst) in enumerate(((xbf, m_row), (sqbf, q_row))):
            bank = half
            pbank = ctx.pb[bank]
            dep0 = ([tload] if half == 0 else tsq) + ctx.bank_deps(bank)
            mm = None
            for c in range(KC):
                mm = P.pe(lambda E, c=c, ts=ts, src=src, pbank=pbank: E.matmul(
                    pbank[0:1, :], ctx.mean_col[:],
                    src[:, c * T + ts: c * T + ts + 512],
                    start=(c == 0), stop=(c == KC - 1)),
                    deps=dep0, inc=(c == KC - 1))
            e = P.dve(lambda E, dst=dst, pbank=pbank: E.tensor_copy(
                dst, pbank[0:1, :]), deps=[mm])
            ctx.bank_use(bank, e)
            ev.append(e)
        t1 = P.dve(lambda E: E.tensor_tensor(v_row, m_row, m_row, ALU.mult),
                   deps=ev)
        t2 = P.dve(lambda E: E.tensor_sub(v_row, q_row, v_row), deps=[t1])
        t3 = P.act(lambda E: E.activation(v_row, v_row, AF.Ln,
                                          bias=ctx.cst(1e-5, parts=1)),
                   deps=[t2])
        t4 = P.act(lambda E: E.activation(r_row, v_row, AF.Exp, scale=-0.5),
                   deps=[t3])
        t5 = P.dve(lambda E: E.tensor_copy(mbf[:, 0:512], m_row), deps=[t2])
        t6 = P.dve(lambda E: E.tensor_copy(mbf[:, 512:1024], r_row), deps=[t4])
        reps = []
        for half in range(2):
            bank = 2 + half
            pbank = ctx.pb[bank]
            mm = P.pe(lambda E, half=half, pbank=pbank: E.matmul(
                pbank[:], ctx.ones_row[:], mbf[0:1, half * 512:(half + 1) * 512],
                start=True, stop=True), deps=[t5, t6] + ctx.bank_deps(bank))
            e = P.dve(lambda E, half=half, pbank=pbank: E.tensor_copy(
                rep[:, half * 512:(half + 1) * 512], pbank[:]), deps=[mm])
            ctx.bank_use(bank, e)
            reps.append(e)
        for c in range(KC):
            ta = P.dve(lambda E, c=c, ts=ts, xbf=xbf: E.tensor_sub(
                tmp, xbf[:, c * T + ts: c * T + ts + 512], m_rep),
                deps=reps + [tload])
            tb = P.dve(lambda E, c=c, ts=ts, xln=xln: E.tensor_mul(
                xln[:, c * T + ts: c * T + ts + 512], tmp, r_rep), deps=[ta])
            out_toks.append(tb)
            if f32_out_dram is not None:
                tb2 = P.dve(lambda E, c=c, ts=ts: E.tensor_mul(
                    tmp, tmp, r_rep), deps=[ta])
                out_toks.append(P.load(
                    f32_out_dram.rearrange("(c p) t -> c p t", p=128)
                    [c, :, ts:ts + 512], tmp, deps=[tb2]))
                reps = reps + [out_toks[-1]]
    ctx.flush()
    ctx.pop()
    return out_toks


def emit_l0a(ctx, x_in, x_out):
    P, w = ctx.P, ctx.w
    ctx.push()
    Brow = ctx.bf(T, parts=16)
    Crow = ctx.bf(T, parts=16)
    hlast = ctx.bf(KC * 16)
    A_sb = ctx.f32(KC * 16)
    D_sb = ctx.f32(KC)
    dtb = ctx.f32(KC)
    xz2 = ctx.bf(2 * KC * T)
    x_conv = xz2[:, 0:KC * T]
    zsil = xz2[:, KC * T:2 * KC * T]

    ctx.push()
    xln = ctx.bf(KC * T)
    ctx.push()
    xbf = ctx.bf(KC * T)
    emit_ln(ctx, x_in, xln, xbf_scr=xbf, sq_scr=x_conv)
    ctx.pop()

    ctx.push()
    wbuf = ctx.bf(2 * KC * 128)
    wv4 = w["L0_w_inprojT"].rearrange("(kc p) (oc j) -> oc p kc j", p=128, j=128)
    wv = [wv4[i] for i in range(16)]
    wslot = [wbuf[:, 0:KC * 128].rearrange("p (kc j) -> p kc j", j=128),
             wbuf[:, KC * 128:2 * KC * 128].rearrange("p (kc j) -> p kc j", j=128)]
    tw = {0: P.loadc(wslot[0], wv[0]), 1: P.loadc(wslot[1], wv[1])}
    for oc in range(16):
        dst = x_conv if oc < KC else zsil
        c = oc % KC
        last_mm = None
        for tc in range(NT):
            bank = tc % 2
            pbank = ctx.pb[bank]
            mm = None
            for kc in range(KC):
                mm = P.pe(lambda E, oc=oc, kc=kc, tc=tc, pbank=pbank,
                          wbuf=wbuf, xln=xln: E.matmul(
                    pbank[:], wbuf[:, (oc % 2) * KC * 128 + kc * 128:
                                   (oc % 2) * KC * 128 + (kc + 1) * 128],
                    xln[:, kc * T + tc * 512: kc * T + tc * 512 + 512],
                    start=(kc == 0), stop=(kc == KC - 1)),
                    deps=[tw[oc]] + ctx.bank_deps(bank), inc=(kc == KC - 1))
            e = P.act(lambda E, c=c, tc=tc, dst=dst, pbank=pbank: E.activation(
                dst[:, c * T + tc * 512: c * T + tc * 512 + 512], pbank[:],
                AF.Copy), deps=[mm])
            ctx.bank_use(bank, e)
            last_mm = mm
        if oc + 2 < 16:
            tw[oc + 2] = P.loadc(wslot[oc % 2], wv[oc + 2], deps=[last_mm])
    ctx.flush()
    ctx.pop()

    ctx.push()
    acc = ctx.f32(2 * T)
    cw = ctx.f32(2 * KC * 4)
    tcw = P.load(cw[:, 0:KC * 4], w["L0_conv_xw"])
    tcw2 = P.load(cw[:, KC * 4:2 * KC * 4], w["L0_conv_zw"])
    prev_act = {}
    for half, (buf, woff, twd) in enumerate(
            ((x_conv, 0, tcw), (zsil, KC * 4, tcw2))):
        for c in range(KC):
            gi = half * KC + c
            s = buf[:, c * T:(c + 1) * T]
            a = acc[:, (gi % 2) * T:(gi % 2 + 1) * T]
            w0 = cw[:, woff + c * 4 + 0: woff + c * 4 + 1]
            w1 = cw[:, woff + c * 4 + 1: woff + c * 4 + 2]
            w2 = cw[:, woff + c * 4 + 2: woff + c * 4 + 3]
            w3 = cw[:, woff + c * 4 + 3: woff + c * 4 + 4]
            t0 = P.dve(lambda E, s=s, a=a, w1=w1: E.tensor_scalar(
                a, s, w1, None, ALU.mult),
                deps=[twd, prev_act.get(gi)])
            t0 = P.dve(lambda E, s=s, a=a, w0=w0: E.scalar_tensor_tensor(
                a[:, 1:T], s[:, 0:T - 1], w0, a[:, 1:T], ALU.mult, ALU.add),
                deps=[t0])
            t0 = P.dve(lambda E, s=s, a=a, w2=w2: E.scalar_tensor_tensor(
                a[:, 0:T - 1], s[:, 1:T], w2, a[:, 0:T - 1], ALU.mult, ALU.add),
                deps=[t0])
            t0 = P.dve(lambda E, s=s, a=a, w3=w3: E.scalar_tensor_tensor(
                a[:, 0:T - 2], s[:, 2:T], w3, a[:, 0:T - 2], ALU.mult, ALU.add),
                deps=[t0])
            prev_act[gi + 2] = P.act(
                lambda E, s=s, a=a: E.activation(s, a, AF.Silu), deps=[t0])
    ctx.flush()
    ctx.pop()
    ctx.pop()

    delta = ctx.bf(KC * T)

    ctx.push()
    dtr = ctx.bf(T, parts=64)
    wxp = ctx.bf(KC * 96)
    twx = P.loadc(wxp.rearrange("p (kc o) -> p kc o", kc=KC),
                  w["L0_w_xprojT"].rearrange("(kc p) o -> p kc o", p=128))
    for tc in range(NT):
        pb = ctx.pb[4]
        mm = None
        for kc in range(KC):
            mm = P.pe(lambda E, kc=kc, tc=tc, pb=pb, wxp=wxp: E.matmul(
                pb[0:80, :], wxp[:, kc * 96:kc * 96 + 80],
                x_conv[:, kc * T + tc * 512: kc * T + tc * 512 + 512],
                start=(kc == 0), stop=(kc == KC - 1)),
                deps=[twx] + ctx.bank_deps(4), inc=(kc == KC - 1))
        e1 = P.act(lambda E, tc=tc, pb=pb, dtr=dtr: E.activation(
            dtr[:, tc * 512:(tc + 1) * 512], pb[0:64, :], AF.Copy), deps=[mm])
        e2 = P.dve(lambda E, tc=tc, pb=pb: E.tensor_copy(
            Brow[:, tc * 512:(tc + 1) * 512], pb[64:80, :]), deps=[mm, e1])
        ctx.bank_use(4, e1)
        ctx.bank_use_add(4, e2)
        pb2 = ctx.pb[5]
        mm2 = None
        for kc in range(KC):
            mm2 = P.pe(lambda E, kc=kc, tc=tc, pb2=pb2, wxp=wxp: E.matmul(
                pb2[0:16, :], wxp[:, kc * 96 + 80:kc * 96 + 96],
                x_conv[:, kc * T + tc * 512: kc * T + tc * 512 + 512],
                start=(kc == 0), stop=(kc == KC - 1)),
                deps=[twx] + ctx.bank_deps(5), inc=(kc == KC - 1))
        e3 = P.dve(lambda E, tc=tc, pb2=pb2: E.tensor_copy(
            Crow[:, tc * 512:(tc + 1) * 512], pb2[0:16, :]), deps=[mm2])
        ctx.bank_use(5, e3)
    ctx.flush()

    wdt = ctx.bf(KC * 128, parts=64)
    u1 = ctx.f32(512)
    u3 = ctx.f32(512)
    rr = ctx.f32(512)
    twd = P.loadc(wdt, w["L0_w_dtprojT"])
    tbb = P.load(dtb, w["L0_dt_bias"])
    prev_add = None
    dbg = int(os.environ.get("DBG_DELTA", "3"))
    for dc in range(KC):
        for tc in range(NT):
            bank = tc % 2
            pb = ctx.pb[bank]
            bap = dtb[:, dc:dc + 1]
            mm = P.pe(lambda E, dc=dc, tc=tc, pb=pb, wdt=wdt, dtr=dtr: E.matmul(
                pb[:], wdt[:, dc * 128:(dc + 1) * 128],
                dtr[:, tc * 512:(tc + 1) * 512], start=True, stop=True),
                deps=[twd] + ctx.bank_deps(bank))
            if dbg == 1:
                prev_add = P.dve(lambda E, dc=dc, tc=tc, pb=pb: E.tensor_copy(
                    delta[:, dc * T + tc * 512: dc * T + tc * 512 + 512],
                    pb[:]), deps=[mm])
                ctx.bank_use(bank, prev_add)
                continue
            a1 = P.act(lambda E, pb=pb, bap=bap, u1=u1: E.activation(
                u1, pb[:], AF.Abs, bias=bap), deps=[mm, tbb, prev_add])
            a2 = P.act(lambda E, u1=u1: E.activation(u1, u1, AF.Exp, scale=-1.0),
                       deps=[a1])
            a3 = P.act(lambda E, u1=u1, u3=u3: E.activation(u3, u1, AF.Ln,
                                                            bias=1.0), deps=[a2])
            if dbg == 2:
                prev_add = P.dve(lambda E, dc=dc, tc=tc, u3=u3: E.tensor_copy(
                    delta[:, dc * T + tc * 512: dc * T + tc * 512 + 512],
                    u3), deps=[a3])
                ctx.bank_use(bank, a1)
                continue
            d1 = P.dve(lambda E, pb=pb, bap=bap, rr=rr: E.tensor_scalar(
                rr, pb[:], bap, None, ALU.add), deps=[mm, tbb, a1])
            d1 = P.dve(lambda E, rr=rr: E.tensor_single_scalar(
                rr, rr, 0.0, ALU.max), deps=[d1])
            prev_add = P.dve(lambda E, dc=dc, tc=tc, rr=rr, u3=u3: E.tensor_add(
                delta[:, dc * T + tc * 512: dc * T + tc * 512 + 512], rr, u3),
                deps=[d1, a3])
            ctx.bank_use(bank, a1)
            ctx.bank_use_add(bank, d1)
    ctx.flush()
    ctx.pop()

    sel16 = ctx.get_sel16()

    ta = P.load(A_sb, w["L0_A_log"])
    ta = P.act(lambda E: E.activation(A_sb, A_sb, AF.Exp), deps=[ta])
    ta = P.dve(lambda E: E.tensor_scalar(A_sb, A_sb, -1.0, None, ALU.mult),
               deps=[ta])
    td = P.load(D_sb, w["L0_D"])

    ctx.push()
    Bt = ctx.bf(16 * 512)
    Ct = ctx.bf(16 * 512)
    hbuf = ctx.bf(16 * 512)
    dAb = ctx.f32(2 * 512)
    ub = ctx.bf(2 * 512)
    gb = ctx.bf(2 * 512)
    dx = ctx.bf(512)
    ytc = ctx.bf(KC * 512)
    wout = ctx.bf(2 * 16 * 128)
    xio = ctx.f32(4 * 512)
    wov4 = w["L0_w_outprojT"].rearrange("(kc p) (oc j) -> oc p kc j",
                                        p=128, j=128)
    wov = [wov4[i] for i in range(KC)]
    woslot = [wout[:, 0:16 * 128].rearrange("p (kc j) -> p kc j", j=128),
              wout[:, 16 * 128:2 * 16 * 128].rearrange("p (kc j) -> p kc j", j=128)]
    x_in_v = x_in.rearrange("(c p) t -> c p t", p=128)
    x_out_v = x_out.rearrange("(c p) t -> c p t", p=128)
    hview = hbuf.rearrange("p (n t) -> p n t", n=16)

    for tc in range(NT):
        ts = tc * 512
        scan_prev = {}
        tbs = []
        for n in range(16):
            mmB = P.pe(lambda E, n=n, ts=ts: E.matmul(
                ctx.pb[4][:], sel16[:, n * 128:(n + 1) * 128],
                Brow[:, ts:ts + 512], start=True, stop=True),
                deps=ctx.bank_deps(4))
            eB = P.dve(lambda E, n=n, Bt=Bt: E.tensor_copy(
                Bt[:, n * 512:(n + 1) * 512], ctx.pb[4][:]), deps=[mmB])
            ctx.bank_use(4, eB)
            mmC = P.pe(lambda E, n=n, ts=ts: E.matmul(
                ctx.pb[5][:], sel16[:, n * 128:(n + 1) * 128],
                Crow[:, ts:ts + 512], start=True, stop=True),
                deps=ctx.bank_deps(5))
            eC = P.dve(lambda E, n=n, Ct=Ct: E.tensor_copy(
                Ct[:, n * 512:(n + 1) * 512], ctx.pb[5][:]), deps=[mmC])
            ctx.bank_use(5, eC)
            tbs.extend((eB, eC))
        two = {0: P.loadc(woslot[0], wov[0]), 1: P.loadc(woslot[1], wov[1])}
        for c in range(KC):
            gmm_prev = {}
            dsl = delta[:, c * T + ts: c * T + ts + 512]
            xsl = x_conv[:, c * T + ts: c * T + ts + 512]
            tdx = P.dve(lambda E, dsl=dsl, xsl=xsl, dx=dx: E.tensor_mul(
                dx, dsl, xsl), deps=tbs)
            for n in range(16):
                par = n % 2
                tda = P.act(lambda E, dsl=dsl, c=c, n=n, par=par, dAb=dAb:
                            E.activation(
                    dAb[:, par * 512:(par + 1) * 512], dsl, AF.Exp,
                    scale=A_sb[:, c * 16 + n: c * 16 + n + 1]),
                    deps=[ta, scan_prev.get(par)])
                tu = P.dve(lambda E, n=n, par=par, ub=ub, dx=dx, Bt=Bt:
                           E.tensor_mul(
                    ub[:, par * 512:(par + 1) * 512], dx,
                    Bt[:, n * 512:(n + 1) * 512]), deps=[tdx])
                init = (0.0 if tc == 0
                        else hlast[:, c * 16 + n: c * 16 + n + 1])
                th = P.dve(lambda E, n=n, par=par, init=init, hbuf=hbuf,
                           dAb=dAb, ub=ub: E.tensor_tensor_scan(
                    hbuf[:, n * 512:(n + 1) * 512],
                    dAb[:, par * 512:(par + 1) * 512],
                    ub[:, par * 512:(par + 1) * 512],
                    init, ALU.mult, ALU.add), deps=[tda, tu])
                scan_prev[par] = th
                tg = P.dve(lambda E, n=n, par=par, gb=gb, hbuf=hbuf, Ct=Ct:
                           E.tensor_mul(
                    gb[:, par * 512:(par + 1) * 512],
                    hbuf[:, n * 512:(n + 1) * 512],
                    Ct[:, n * 512:(n + 1) * 512]),
                    deps=[th, gmm_prev.get(par)])
                gmm_prev[par] = P.pe(lambda E, n=n, par=par, gb=gb: E.matmul(
                    ctx.pb[6][:], ctx.ident[:],
                    gb[:, par * 512:(par + 1) * 512],
                    start=(n == 0), stop=(n == 15)),
                    deps=[tg] + (ctx.bank_deps(6) if n == 0 else []), inc=1)
            mm_y = P.last_tok["pe"]
            P.dve(lambda E, c=c, hview=hview: E.tensor_copy(
                hlast[:, c * 16:(c + 1) * 16],
                hview[:, :, 511:512].rearrange("p n o -> p (n o)")), deps=[])
            ty = P.dve(lambda E, c=c, xsl=xsl, ytc=ytc: E.scalar_tensor_tensor(
                ytc[:, c * 512:(c + 1) * 512], xsl, D_sb[:, c:c + 1],
                ctx.pb[6][:], ALU.mult, ALU.add), deps=[mm_y, td])
            ctx.bank_use(6, ty)
        ytok = P.last_tok["dve"]
        store_prev = {}
        for oc in range(KC):
            par = oc % 2
            bank = par
            pb = ctx.pb[bank]
            xo = xio[:, par * 1024: par * 1024 + 512]
            xn = xio[:, par * 1024 + 512: par * 1024 + 1024]
            sprev = store_prev.get(par, (None, None))
            tx = P.load(xo, x_in_v[oc, :, ts:ts + 512], deps=[sprev[1]])
            mm = None
            for kc in range(16):
                rhs = (ytc[:, kc * 512:(kc + 1) * 512] if kc < KC
                       else zsil[:, (kc - KC) * T + ts:(kc - KC) * T + ts + 512])
                mm = P.pe(lambda E, oc=oc, kc=kc, rhs=rhs, pb=pb, wout=wout:
                          E.matmul(
                    pb[:], wout[:, (oc % 2) * 16 * 128 + kc * 128:
                                (oc % 2) * 16 * 128 + (kc + 1) * 128],
                    rhs, start=(kc == 0), stop=(kc == 15)),
                    deps=[two[oc], ytok] + ctx.bank_deps(bank), inc=(kc == 15))
            tn = P.dve(lambda E, pb=pb, xo=xo, xn=xn: E.tensor_add(xn, pb[:], xo),
                       deps=[mm, tx, sprev[0]])
            ctx.bank_use(bank, tn)
            st = P.load(x_out_v[oc, :, ts:ts + 512], xn, deps=[tn])
            store_prev[par] = (st, tn)
            if oc + 2 < KC:
                two[oc + 2] = P.loadc(woslot[par], wov[oc + 2], deps=[mm])
        ctx.flush()
    ctx.pop()
    ctx.pop()




def emit_l0f(ctx, x_in, x_out):
    P, w = ctx.P, ctx.w
    ctx.push()
    HC = 32
    h1 = ctx.bf(HC * 512)
    h2 = ctx.bf(HC * 512)
    xln = ctx.bf(KC * T)
    sc = ctx.f32(5 * HC)
    stmp = ctx.bf(512)
    ts1 = P.load(sc[:, 0:HC], w["L0f_s1"])
    ts2 = P.load(sc[:, HC:2 * HC], w["L0f_s2"])
    ts3 = P.load(sc[:, 2 * HC:3 * HC], w["L0f_sh1"])
    ts4 = P.load(sc[:, 3 * HC:4 * HC], w["L0f_sh2"])
    ts5 = P.load(sc[:, 4 * HC:4 * HC + KC], w["L0f_s3"])
    tsc = [ts1, ts2, ts3, ts4, ts5]

    emit_ln(ctx, x_in, xln, xbf_scr=h1[:, 0:KC * T], sq_scr=h2[:, 0:KC * T])
    xio = ctx.f32(4 * 512)

    x_in_v = x_in.rearrange("(c p) t -> c p t", p=128)
    x_out_v = x_out.rearrange("(c p) t -> c p t", p=128)

    for tc in range(NT):
        ts = tc * 512
        ctx.push()
        w1v4 = w["L0f_w1T"].rearrange("(kc p) (oc j) -> oc p kc j", p=128, j=128)
        w2v4 = w["L0f_w2T"].rearrange("(kc p) (oc j) -> oc p kc j", p=128, j=128)
        wb = ctx.bf(4 * KC * 128)
        wsl = [wb[:, i * KC * 128:(i + 1) * KC * 128]
               .rearrange("p (kc j) -> p kc j", j=128) for i in range(4)]
        tw = {(0, 0): P.loadc(wsl[0], w1v4[0], deps=tsc),
              (1, 0): P.loadc(wsl[2], w2v4[0], deps=tsc),
              (0, 1): P.loadc(wsl[1], w1v4[1]),
              (1, 1): P.loadc(wsl[3], w2v4[1])}
        prev_e2 = None
        for oc in range(HC):
            par = oc % 2
            mm1 = None
            mm2 = None
            for kc in range(KC):
                mm1 = P.pe(lambda E, kc=kc, oc=oc, par=par, ts=ts: E.matmul(
                    ctx.pb[0][:], wsl[par][:, kc, :],
                    xln[:, kc * T + ts: kc * T + ts + 512],
                    start=(kc == 0), stop=(kc == KC - 1)),
                    deps=[tw[(0, oc)]] + ctx.bank_deps(0), inc=(kc == KC - 1))
            for kc in range(KC):
                mm2 = P.pe(lambda E, kc=kc, oc=oc, par=par, ts=ts: E.matmul(
                    ctx.pb[1][:], wsl[2 + par][:, kc, :],
                    xln[:, kc * T + ts: kc * T + ts + 512],
                    start=(kc == 0), stop=(kc == KC - 1)),
                    deps=[tw[(1, oc)]] + ctx.bank_deps(1), inc=(kc == KC - 1))
            e1 = P.act(lambda E, oc=oc: E.activation(
                stmp, ctx.pb[0][:], AF.Silu, scale=sc[:, oc:oc + 1]),
                deps=[mm1, prev_e2])
            e2 = P.dve(lambda E, oc=oc: E.scalar_tensor_tensor(
                h1[:, oc * 512:(oc + 1) * 512], ctx.pb[1][:],
                sc[:, HC + oc: HC + oc + 1], stmp, ALU.mult, ALU.mult),
                deps=[mm2, e1])
            prev_e2 = e2
            ctx.bank_use(0, e1)
            ctx.bank_use(1, e2)
            if oc + 2 < HC:
                tw[(0, oc + 2)] = P.loadc(wsl[par], w1v4[oc + 2], deps=[mm1])
                tw[(1, oc + 2)] = P.loadc(wsl[2 + par], w2v4[oc + 2],
                                          deps=[mm2])
        ctx.flush()
        ctx.pop()

        ctx.push()
        def ev_h2(oc, tcc, pbk, mm):
            return P.act(lambda E, oc=oc, pbk=pbk: E.activation(
                h2[:, oc * 512:(oc + 1) * 512], pbk[:], AF.Gelu,
                scale=sc[:, 2 * HC + oc: 2 * HC + oc + 1]), deps=[mm])
        mm_fm(ctx, w["L0f_h1T"], 4096, 4096,
              lambda kc, tcc: h1[:, kc * 512:(kc + 1) * 512], ev_h2,
              t_lo=0, t_hi=1)
        ctx.flush()
        ctx.pop()

        ctx.push()
        def ev_h3(oc, tcc, pbk, mm):
            return P.act(lambda E, oc=oc, pbk=pbk: E.activation(
                h1[:, oc * 512:(oc + 1) * 512], pbk[:], AF.Gelu,
                scale=sc[:, 3 * HC + oc: 3 * HC + oc + 1]), deps=[mm])
        mm_fm(ctx, w["L0f_h2T"], 4096, 4096,
              lambda kc, tcc: h2[:, kc * 512:(kc + 1) * 512], ev_h3,
              t_lo=0, t_hi=1)
        ctx.flush()
        ctx.pop()

        ctx.push()
        store_prev = {}

        def ev_out(oc, tcc, pbk, mm, ts=ts):
            par = oc % 2
            xo = xio[:, par * 1024: par * 1024 + 512]
            xn = xio[:, par * 1024 + 512: par * 1024 + 1024]
            sprev = store_prev.get(par, (None, None))
            tx = P.load(xo, x_in_v[oc, :, ts:ts + 512], deps=[sprev[1]])
            tn = P.dve(lambda E, oc=oc, pbk=pbk, xo=xo, xn=xn:
                       E.scalar_tensor_tensor(
                xn, pbk[:], sc[:, 4 * HC + oc: 4 * HC + oc + 1], xo,
                ALU.mult, ALU.add), deps=[mm, tx, sprev[0]])
            st = P.load(x_out_v[oc, :, ts:ts + 512], xn, deps=[tn])
            store_prev[par] = (st, tn)
            return tn
        mm_fm(ctx, w["L0f_w3T"], 4096, 1024,
              lambda kc, tcc: h1[:, kc * 512:(kc + 1) * 512], ev_out,
              t_lo=0, t_hi=1)
        ctx.flush()
        ctx.pop()
    ctx.pop()



def emit_l1a(ctx, x_in, x_out):
    P, w = ctx.P, ctx.w
    ctx.push()
    Qf = ctx.bf(KC * T)
    Kf = ctx.bf(KC * T)
    Vtok = ctx.bf(16 * 1024)
    bqk = ctx.f32(2 * KC)
    tbq = P.load(bqk[:, 0:KC], w["L1_bq"])
    tbk = P.load(bqk[:, KC:2 * KC], w["L1_bk"])

    ctx.push()
    xln = ctx.bf(KC * T)
    ctx.push()
    xbf = ctx.bf(KC * T)
    emit_ln(ctx, x_in, xln, xbf_scr=xbf, sq_scr=Qf[:, 0:KC * T])
    ctx.pop()

    def mk_ev(dst, boff):
        def ev(oc, tc, pbk, mm):
            return P.act(lambda E, oc=oc, tc=tc, pbk=pbk, dst=dst: E.activation(
                dst[:, oc * T + tc * 512: oc * T + tc * 512 + 512], pbk[:],
                AF.Identity, bias=bqk[:, boff + oc: boff + oc + 1]), deps=[mm, tbq,
                                                                      tbk])
        return ev
    ctx.push()
    mm_fm(ctx, w["L1_wqT"], D, D,
          lambda kc, tc: xln[:, kc * T + tc * 512: kc * T + tc * 512 + 512],
          mk_ev(Qf, 0))
    ctx.flush()
    ctx.pop()
    ctx.push()
    mm_fm(ctx, w["L1_wkT"], D, D,
          lambda kc, tc: xln[:, kc * T + tc * 512: kc * T + tc * 512 + 512],
          mk_ev(Kf, KC))
    ctx.flush()
    ctx.pop()

    ctx.push()
    wvf = ctx.bf(KC * 1024)
    twv = P.loadc(wvf.rearrange("p (kc o) -> p kc o", kc=KC),
                  w["L1_wvT"].rearrange("(kc p) o -> p kc o", p=128))
    for tt in range(16):
        for oc2 in range(2):
            bank = (tt * 2 + oc2) % 2
            pbk = ctx.pb[bank]
            mm = None
            for kc in range(KC):
                mm = P.pe(lambda E, tt=tt, oc2=oc2, kc=kc, pbk=pbk: E.matmul(
                    pbk[:], xln[:, kc * T + tt * 128: kc * T + (tt + 1) * 128],
                    wvf[:, kc * 1024 + oc2 * 512: kc * 1024 + oc2 * 512 + 512],
                    start=(kc == 0), stop=(kc == KC - 1)),
                    deps=[twv] + ctx.bank_deps(bank), inc=(kc == KC - 1))
            e = P.dve(lambda E, tt=tt, oc2=oc2, pbk=pbk: E.tensor_copy(
                Vtok[:, tt * 1024 + oc2 * 512: tt * 1024 + oc2 * 512 + 512],
                pbk[:]), deps=[mm])
            ctx.bank_use(bank, e)
    ctx.flush()
    ctx.pop()
    ctx.pop()

    ctx.push()
    Pt = ctx.bf(16 * 512)
    Dtb = ctx.bf(16 * 512)
    xio = ctx.f32(4 * 512)
    x_in_v = x_in.rearrange("(c p) t -> c p t", p=128)
    x_out_v = x_out.rearrange("(c p) t -> c p t", p=128)
    Dt_v = ctx.w["L1_Dt"].rearrange("(kt p) q -> p kt q", p=128)
    for tqc in range(NT):
        ts = tqc * 512
        nkt = (ts + 512) // 128
        td = P.loadc(Dtb.rearrange("p (kt q) -> p kt q", q=512)[:, 0:nkt, :],
                     Dt_v[:, 0:nkt, ts:ts + 512])
        pts = []
        for kt in range(nkt):
            bank = kt % 2
            pbk = ctx.pb[bank]
            mm = None
            for hc in range(KC):
                mm = P.pe(lambda E, kt=kt, hc=hc, ts=ts, pbk=pbk: E.matmul(
                    pbk[:], Kf[:, hc * T + kt * 128: hc * T + (kt + 1) * 128],
                    Qf[:, hc * T + ts: hc * T + ts + 512],
                    start=(hc == 0), stop=(hc == KC - 1)),
                    deps=ctx.bank_deps(bank), inc=(hc == KC - 1))
            e = P.dve(lambda E, kt=kt, pbk=pbk: E.tensor_mul(
                Pt[:, kt * 512:(kt + 1) * 512], pbk[:],
                Dtb[:, kt * 512:(kt + 1) * 512]), deps=[mm, td])
            ctx.bank_use(bank, e)
            pts.append(e)
        store_prev = {}
        for oc in range(KC):
            par = oc % 2
            bank = 2 + par
            pbk = ctx.pb[bank]
            xo = xio[:, par * 1024: par * 1024 + 512]
            xn = xio[:, par * 1024 + 512: par * 1024 + 1024]
            sprev = store_prev.get(par, (None, None))
            tx = P.load(xo, x_in_v[oc, :, ts:ts + 512], deps=[sprev[1]])
            mm = None
            for kt in range(nkt):
                mm = P.pe(lambda E, oc=oc, kt=kt, pbk=pbk: E.matmul(
                    pbk[:],
                    Vtok[:, kt * 1024 + oc * 128: kt * 1024 + oc * 128 + 128],
                    Pt[:, kt * 512:(kt + 1) * 512],
                    start=(kt == 0), stop=(kt == nkt - 1)),
                    deps=pts + ctx.bank_deps(bank), inc=(kt == nkt - 1))
            tn = P.dve(lambda E, pbk=pbk, xo=xo, xn=xn: E.tensor_add(
                xn, pbk[:], xo), deps=[mm, tx, sprev[0]])
            ctx.bank_use(bank, tn)
            st = P.load(x_out_v[oc, :, ts:ts + 512], xn, deps=[tn])
            store_prev[par] = (st, tn)
        ctx.flush()
    ctx.pop()
    ctx.pop()



def emit_round(ctx, dst_bf, src_f32, itmp, ctmp, deps=()):
    P = ctx.P
    t1 = P.dve(lambda E: E.tensor_copy(itmp, src_f32), deps=deps)
    t2 = P.dve(lambda E: E.tensor_copy(ctmp, itmp), deps=[t1])
    t3 = P.dve(lambda E: E.tensor_sub(src_f32, src_f32, ctmp), deps=[t2])
    t4 = P.dve(lambda E: E.tensor_single_scalar(dst_bf, src_f32, 0.5,
                                                ALU.is_ge), deps=[t3])
    t5 = P.dve(lambda E: E.tensor_single_scalar(src_f32, src_f32, -0.5,
                                                ALU.is_lt), deps=[t4])
    t6 = P.dve(lambda E: E.tensor_sub(ctmp, ctmp, src_f32), deps=[t5])
    t7 = P.dve(lambda E: E.tensor_add(dst_bf, ctmp, dst_bf), deps=[t6])
    return t7


def emit_pmax(ctx, acc, ncols, out_pp, pbank_key, pbank, scr, deps=()):
    P = ctx.P
    t0 = P.dve(lambda E: E.tensor_copy(scr[0:64, 0:ncols], acc[64:128, :]),
               deps=deps)
    t1 = P.dve(lambda E: E.tensor_max(acc[0:64, :], acc[0:64, :],
                                      scr[0:64, 0:ncols]), deps=[t0])
    t1 = P.dve(lambda E: E.tensor_copy(scr[0:32, 0:ncols], acc[32:64, :]),
               deps=[t1])
    t2 = P.dve(lambda E: E.tensor_max(acc[0:32, :], acc[0:32, :],
                                      scr[0:32, 0:ncols]), deps=[t1])
    toks = []
    nt = (ncols + 127) // 128
    for i in range(nt):
        cw = min(128, ncols - i * 128)
        mm = P.pe(lambda E, i=i, cw=cw: E.transpose(
            pbank[0:cw, 0:32], acc[0:32, i * 128:i * 128 + cw],
            ctx.identf[0:32, 0:32]), deps=[t2] + ctx.bank_deps(pbank_key),
            inc=1)
        e = P.dve(lambda E, i=i, cw=cw: E.reduce_max(
            out_pp[0:cw, i:i + 1], pbank[0:cw, 0:32], axis=AX.X), deps=[mm])
        ctx.bank_use(pbank_key, e)
        toks.append(e)
    return toks


def emit_pp_to_row(ctx, pp, nt, row, scr_dram, pbank_key, pbank, deps=(),
                   row_slices=False):
    P = ctx.P
    mm = P.pe(lambda E: E.transpose(pbank[0:nt, 0:128], pp[:, 0:nt],
                                    ctx.identf[:]),
              deps=list(deps) + ctx.bank_deps(pbank_key), inc=1)
    st = ctx.f32(128, parts=nt)
    e = P.dve(lambda E: E.tensor_copy(st, pbank[0:nt, 0:128]), deps=[mm])
    ctx.bank_use(pbank_key, e)
    t1 = P.load(scr_dram[0:nt, :], st, deps=[e])
    if row_slices or row is None:
        return t1
    t2 = P.load(row, scr_dram.rearrange("a b -> (a b)")[None, 0:nt * 128],
                deps=[t1])
    return t2



def emit_l1f(ctx, x_in, x_out):
    P, w = ctx.P, ctx.w
    ctx.push()
    xln = ctx.bf(KC * T)
    xq = ctx.bf(KC * T)
    rhsq = ctx.bf(KC * 1024)
    sr_pp = ctx.f32(KC)
    atok = ctx.f32(16)
    scr_row = ctx.nc.dram_tensor("l1f_scr", [16, 128], F32, kind="Internal").ap()
    scr_row2 = ctx.nc.dram_tensor("l1f_scr2", [8, 128], F32,
                                  kind="Internal").ap()
    i_scr = ctx.i32t

    ctx.push()
    invr = ctx.f32(1024, parts=1)
    racc = ctx.f32(1024)
    rbuf = ctx.f32(1024)
    rnb = ctx.f32(1024)
    rv = w["L1_rhs"].rearrange("(kc p) o -> kc p o", p=128)
    tprev = None
    for kc in range(KC):
        tl = P.load(rbuf, rv[kc], deps=[tprev])
        t1 = P.dve(lambda E: E.tensor_scalar(rnb, rbuf, -1.0, None, ALU.mult),
                   deps=[tl])
        if kc == 0:
            t2 = P.dve(lambda E: E.tensor_max(racc, rbuf, rnb), deps=[t1])
        else:
            t2 = P.dve(lambda E: E.tensor_max(racc, racc, rbuf), deps=[t1])
            t2 = P.dve(lambda E: E.tensor_max(racc, racc, rnb), deps=[t2])
        tprev = t2
    emit_pmax(ctx, racc, 1024, sr_pp, ("t", 6), ctx.pb[6], rnb,
              deps=[tprev])
    trow = emit_pp_to_row(ctx, sr_pp, KC, rbuf[0:1, 0:1024], scr_row2,
                          ("t", 6), ctx.pb[6])
    t1 = P.dve(lambda E: E.tensor_single_scalar(rbuf[0:1, 0:1024],
                                                rbuf[0:1, 0:1024], 1e-12,
                                                ALU.max), deps=[trow])
    t2 = P.dve(lambda E: E.reciprocal(rnb[0:1, 0:1024], rbuf[0:1, 0:1024]),
               deps=[t1])
    t3 = P.dve(lambda E: E.tensor_scalar(invr, rnb[0:1, 0:1024], 127.0, None,
                                         ALU.mult), deps=[t2])
    t4 = P.dve(lambda E: E.tensor_single_scalar(sr_pp, sr_pp, 1e-12, ALU.max),
               deps=trow and [t3])
    t5 = P.dve(lambda E: E.tensor_scalar(sr_pp, sr_pp, 1.0 / 127.0, None,
                                         ALU.mult), deps=[t4])
    ctx.flush()
    irep = racc
    rtmp = rnb[:, 0:512]
    ctmp = rnb[:, 512:1024]
    reps = []
    for h in range(2):
        bank = 4 + h
        mm = P.pe(lambda E, h=h, bank=bank: E.matmul(
            ctx.pb[bank][:], ctx.ones_rowf[:], invr[:, h * 512:(h + 1) * 512],
            start=True, stop=True), deps=ctx.bank_deps(bank))
        e = P.dve(lambda E, h=h, bank=bank: E.tensor_copy(
            irep[:, h * 512:(h + 1) * 512], ctx.pb[bank][:]), deps=[mm])
        ctx.bank_use(bank, e)
        reps.append(e)
    tprev = None
    for kc in range(KC):
        tl = P.load(rbuf, rv[kc], deps=[tprev])
        for h in range(2):
            t1 = P.dve(lambda E, h=h: E.tensor_mul(
                rtmp, rbuf[:, h * 512:(h + 1) * 512],
                irep[:, h * 512:(h + 1) * 512]), deps=[tl] + reps)
            tprev = emit_round(
                ctx, rhsq[:, kc * 1024 + h * 512: kc * 1024 + h * 512 + 512],
                rtmp, i_scr[:, 0:512], ctmp)
    ctx.flush()
    if "rhsq" in _DBG_TAPS:
        dd = [(e, ctx.P.counts[e]) for e in ("dve", "act", "pe")]
        ctx.P.loadc(ctx.taps["rhsq"].rearrange("(kc p) o -> p kc o", p=128),
                    rhsq.rearrange("p (kc o) -> p kc o", kc=KC), deps=dd)
        ctx.P.load(ctx.taps["srp"], sr_pp, deps=dd)
        ctx.flush()
    ctx.pop()

    ctx.push()
    xbf = ctx.bf(KC * T)
    emit_ln(ctx, x_in, xln, xbf_scr=xbf, sq_scr=xq[:, 0:KC * T])
    ctx.pop()
    ctx.push()
    acc = ctx.f32(512)
    nbuf = ctx.f32(512)
    for tc in range(NT):
        ts = tc * 512
        tprev = None
        for c in range(KC):
            sl = xln[:, c * T + ts: c * T + ts + 512]
            t1 = P.dve(lambda E, sl=sl: E.tensor_scalar(
                nbuf, sl, -1.0, None, ALU.mult), deps=[tprev])
            if c == 0:
                tprev = P.dve(lambda E, sl=sl: E.tensor_max(acc, nbuf, nbuf),
                              deps=[t1])
                tprev = P.dve(lambda E, sl=sl: E.tensor_max(acc, acc, sl),
                              deps=[tprev])
            else:
                tprev = P.dve(lambda E, sl=sl: E.tensor_max(acc, acc, sl),
                              deps=[t1])
                tprev = P.dve(lambda E: E.tensor_max(acc, acc, nbuf),
                              deps=[tprev])
        emit_pmax(ctx, acc, 512, atok[:, tc * 4:(tc + 1) * 4], ("t", 6),
                  ctx.pb[6], nbuf, deps=[tprev])
    arow = ctx.f32(512, parts=1)
    tr = emit_pp_to_row(ctx, atok, 16, None, scr_row, ("t", 6), ctx.pb[6],
                        row_slices=True)
    ctx.flush()
    irep2 = ctx.f32(512)
    srep = ctx.f32(512)
    rtmp2 = ctx.f32(512)
    ctmp2 = ctx.f32(512)
    ibf = ctx.f32(512, parts=1)
    srow_v = scr_row.rearrange("a b -> (a b)")[None, :]
    prev_rd = None
    for tc in range(NT):
        ts = tc * 512
        ta = P.load(arow, srow_v[:, ts:ts + 512], deps=[tr, prev_rd])
        t1 = P.dve(lambda E: E.tensor_single_scalar(arow, arow, 1e-12,
                                                    ALU.max), deps=[ta])
        t2 = P.dve(lambda E: E.reciprocal(srep[0:1, :], arow), deps=[t1])
        prev_rd = t2
        t3 = P.dve(lambda E: E.tensor_scalar(ibf, srep[0:1, :], 127.0, None,
                                             ALU.mult), deps=[t2])
        bank = 4 + tc % 2
        mm = P.pe(lambda E, bank=bank: E.matmul(
            ctx.pb[bank][:], ctx.ones_rowf[:], ibf, start=True, stop=True),
            deps=[t3] + ctx.bank_deps(bank))
        e = P.dve(lambda E, bank=bank: E.tensor_copy(irep2, ctx.pb[bank][:]),
                  deps=[mm])
        ctx.bank_use(bank, e)
        if "irep" in _DBG_TAPS:
            e = P.load(ctx.taps["irep"][:, tc * 512:(tc + 1) * 512], irep2,
                       deps=[e])
        for c in range(KC):
            sl = xln[:, c * T + ts: c * T + ts + 512]
            t4 = P.dve(lambda E, sl=sl: E.tensor_mul(rtmp2, sl, irep2),
                       deps=[e])
            rdep = ()
            if "rt" in _DBG_TAPS and tc == 0 and c == 1:
                rdep = (P.load(ctx.taps["rt"], rtmp2, deps=[t4]),)
            emit_round(ctx, xq[:, c * T + ts: c * T + ts + 512], rtmp2,
                       i_scr[:, 0:512], ctmp2, deps=list(rdep))
    if "xq" in _DBG_TAPS:
        dd = [(e, ctx.P.counts[e]) for e in ("dve", "act", "pe")]
        ctx.P.loadc(ctx.taps["xq"].rearrange("(c p) t -> p c t", p=128),
                    xq.rearrange("p (c t) -> p c t", c=KC), deps=dd)
        if "xln" in _DBG_TAPS:
            ctx.P.loadc(ctx.taps["xln"].rearrange("(c p) t -> p c t", p=128),
                        xln.rearrange("p (c t) -> p c t", c=KC), deps=dd)
        ctx.P.load(ctx.taps["srow"],
                   scr_row.rearrange("a b -> (a b)")[None, :], deps=dd)
    ctx.flush()
    ctx.pop()

    ctx.push()
    xio = ctx.f32(4 * 512)
    srep2 = ctx.f32(512)
    otmp = ctx.f32(512)
    arow2 = ctx.f32(512, parts=1)
    sbf = ctx.f32(512, parts=1)
    x_in_v = x_in.rearrange("(c p) t -> c p t", p=128)
    x_out_v = x_out.rearrange("(c p) t -> c p t", p=128)
    prev_rd2 = None
    for tc in range(NT):
        ts = tc * 512
        ta = P.load(arow2, srow_v[:, ts:ts + 512], deps=[tr, prev_rd2])
        t1 = P.dve(lambda E: E.tensor_single_scalar(arow2, arow2, 1e-12,
                                                    ALU.max), deps=[ta])
        t2 = P.dve(lambda E: E.tensor_scalar(sbf, arow2, 1.0 / 127.0, None,
                                             ALU.mult), deps=[t1])
        prev_rd2 = t2
        bank = 4 + tc % 2
        mm = P.pe(lambda E, bank=bank: E.matmul(
            ctx.pb[bank][:], ctx.ones_rowf[:], sbf, start=True, stop=True),
            deps=[t2] + ctx.bank_deps(bank))
        es = P.dve(lambda E, bank=bank: E.tensor_copy(srep2, ctx.pb[bank][:]),
                   deps=[mm])
        ctx.bank_use(bank, es)
        store_prev = {}
        for oc in range(KC):
            par = oc % 2
            bank2 = par
            pbk = ctx.pb[bank2]
            xo = xio[:, par * 1024: par * 1024 + 512]
            xn = xio[:, par * 1024 + 512: par * 1024 + 1024]
            sprev = store_prev.get(par, (None, None))
            tx = P.load(xo, x_in_v[oc, :, ts:ts + 512], deps=[sprev[1]])
            mm2 = None
            for kc in range(KC):
                mm2 = P.pe(lambda E, oc=oc, kc=kc, ts=ts, pbk=pbk: E.matmul(
                    pbk[:],
                    rhsq[:, kc * 1024 + oc * 128: kc * 1024 + oc * 128 + 128],
                    xq[:, kc * T + ts: kc * T + ts + 512],
                    start=(kc == 0), stop=(kc == KC - 1)),
                    deps=ctx.bank_deps(bank2), inc=(kc == KC - 1))
            t5 = P.dve(lambda E, pbk=pbk: E.tensor_mul(otmp, pbk[:], srep2),
                       deps=[mm2, es])
            tn = P.dve(lambda E, oc=oc, xo=xo, xn=xn: E.scalar_tensor_tensor(
                xn, otmp, sr_pp[:, oc:oc + 1], xo, ALU.mult, ALU.add),
                deps=[t5, tx, sprev[0]])
            ctx.bank_use(bank2, t5)
            st = P.load(x_out_v[oc, :, ts:ts + 512], xn, deps=[tn])
            store_prev[par] = (st, tn)
        ctx.flush()
    ctx.pop()
    ctx.pop()



def emit_l2a(ctx, x_in, x_out):
    P, w = ctx.P, ctx.w
    NH, HS = 16, 64
    ctx.push()
    Qf = ctx.bf(KC * T)
    Kf = ctx.bf(KC * T)
    Vt = ctx.bf(16 * NH * 65)
    av = ctx.bf(KC * T)
    bpc = ctx.f32(24)
    bvr = ctx.bf(1024, parts=1)
    tb1 = P.load(bpc, w["L2_b_attn_pc"])
    tb2 = P.loadc(bvr, w["L2_b_attn"][:, 2048:3072])
    Vt3 = Vt.rearrange("p (tt h d) -> p (tt h) d", h=NH, d=65)
    tves = P.dve(lambda E: E.memset(Vt3[:, :, 64:65], 1.0))

    ctx.push()
    xln = ctx.bf(KC * T)
    emit_ln(ctx, x_in, xln, xbf_scr=Kf[:, 0:KC * T], sq_scr=Qf[:, 0:KC * T])

    def mk_ev(dst, boff):
        def ev(oc, tc, pbk, mm):
            return P.act(lambda E, oc=oc, tc=tc, pbk=pbk, dst=dst:
                         E.activation(
                dst[:, oc * T + tc * 512: oc * T + tc * 512 + 512], pbk[:],
                AF.Identity, bias=bpc[:, boff + oc: boff + oc + 1]),
                deps=[mm, tb1])
        return ev
    ctx.push()
    mm_fm(ctx, w["L2_w_attn"][:, 0:1024], D, D,
          lambda kc, tc: xln[:, kc * T + tc * 512: kc * T + tc * 512 + 512],
          mk_ev(Qf, 0))
    ctx.flush()
    ctx.pop()
    ctx.push()
    mm_fm(ctx, w["L2_w_attn"][:, 1024:2048], D, D,
          lambda kc, tc: xln[:, kc * T + tc * 512: kc * T + tc * 512 + 512],
          mk_ev(Kf, KC))
    ctx.flush()
    ctx.pop()

    ctx.push()
    wvf = ctx.bf(KC * 512)
    for oc2 in range(2):
        twv = P.loadc(
            wvf.rearrange("p (kc o) -> p kc o", kc=KC),
            w["L2_w_attn"][:, 2048 + oc2 * 512: 2048 + oc2 * 512 + 512]
            .rearrange("(kc p) o -> p kc o", p=128),
            deps=[P.last_tok["pe"]] if oc2 else ())
        for tt in range(16):
            bank = tt % 2
            pbk = ctx.pb[bank]
            mm = None
            for kc in range(KC):
                mm = P.pe(lambda E, tt=tt, oc2=oc2, kc=kc, pbk=pbk: E.matmul(
                    pbk[:], xln[:, kc * T + tt * 128: kc * T + (tt + 1) * 128],
                    wvf[:, kc * 512:(kc + 1) * 512],
                    start=(kc == 0), stop=False),
                    deps=[twv] + ctx.bank_deps(bank), inc=0)
            mm = P.pe(lambda E, tt=tt, oc2=oc2, pbk=pbk: E.matmul(
                pbk[:], ctx.ones_row[:],
                bvr[:, oc2 * 512:(oc2 + 1) * 512],
                start=False, stop=True), deps=[tb2], inc=1)
            dst = Vt3[:, tt * NH + oc2 * 8: tt * NH + oc2 * 8 + 8, 0:64]
            e = P.dve(lambda E, dst=dst, pbk=pbk: E.tensor_copy(
                dst, pbk[:].rearrange("p (h d) -> p h d", d=64)), deps=[mm,
                                                                       tves])
            ctx.bank_use(bank, e)
    ctx.flush()
    ctx.pop()
    ctx.pop()

    ctx.push()
    Pt = ctx.bf(16 * 512)
    rl = ctx.f32(512, parts=1)
    rlrep = ctx.f32(512, parts=64)
    xio = ctx.f32(4 * 512)
    x_in_v = x_in.rearrange("(c p) t -> c p t", p=128)
    x_out_v = x_out.rearrange("(c p) t -> c p t", p=128)
    for tqc in range(NT):
        ts = tqc * 512
        nkt = (ts + 512) // 128
        for h in range(NH):
            hb = (h % 2) * 64
            hc = h // 2
            pts = []
            for kt in range(nkt):
                bank = kt % 2
                pbk = ctx.pb[bank]
                mm = P.pe(lambda E, kt=kt, hb=hb, hc=hc, ts=ts, pbk=pbk:
                          E.matmul(
                    pbk[:],
                    Kf[hb:hb + 64, hc * T + kt * 128: hc * T + (kt + 1) * 128],
                    Qf[hb:hb + 64, hc * T + ts: hc * T + ts + 512],
                    start=True, stop=True), deps=ctx.bank_deps(bank), inc=1)
                e = P.act(lambda E, kt=kt, pbk=pbk: E.activation(
                    Pt[:, kt * 512:(kt + 1) * 512], pbk[:], AF.Exp,
                    scale=0.125, bias=ctx.cst(-12.0)), deps=[mm])
                ctx.bank_use(bank, e)
                if kt * 128 + 127 >= ts:
                    e = P.gp(lambda E, kt=kt, ts=ts: E.affine_select(
                        Pt[:, kt * 512:(kt + 1) * 512],
                        Pt[:, kt * 512:(kt + 1) * 512],
                        pattern=[[1, 512]], compare_op=ALU.is_ge, fill=0.0,
                        base=(ts - kt * 128), channel_multiplier=-1),
                        deps=[e])
                pts.append(e)
            bank = 2
            pbk = ctx.pb[bank]
            mm = None
            for kt in range(nkt):
                mm = P.pe(lambda E, kt=kt, h=h, pbk=pbk: E.matmul(
                    pbk[0:65, :],
                    Vt3[:, kt * NH + h: kt * NH + h + 1, :]
                    .rearrange("p a d -> p (a d)"),
                    Pt[:, kt * 512:(kt + 1) * 512],
                    start=(kt == 0), stop=(kt == nkt - 1)),
                    deps=pts + ctx.bank_deps(bank), inc=(kt == nkt - 1))
            t1 = P.dve(lambda E, pbk=pbk: E.reciprocal(rl, pbk[64:65, :]),
                       deps=[mm])
            bank2 = 3
            mm2 = P.pe(lambda E, bank2=bank2: E.matmul(
                ctx.pb[bank2][0:64, :], ctx.ones_rowf[0:1, 0:64], rl,
                start=True, stop=True), deps=[t1] + ctx.bank_deps(bank2),
                inc=1)
            t2 = P.dve(lambda E, bank2=bank2: E.tensor_copy(
                rlrep, ctx.pb[bank2][0:64, :]), deps=[mm2])
            ctx.bank_use(bank2, t2)
            dst = av[hb:hb + 64, hc * T + ts: hc * T + ts + 512]
            t3 = P.dve(lambda E, dst=dst, pbk=pbk: E.tensor_mul(
                dst, pbk[0:64, :], rlrep), deps=[t2])
            ctx.bank_use(bank, t3)
        ctx.flush()

    store_prev = {}

    def ev_out(oc, tc, pbk, mm):
        par = oc % 2
        xo = xio[:, par * 1024: par * 1024 + 512]
        xn = xio[:, par * 1024 + 512: par * 1024 + 1024]
        sprev = store_prev.get(par, (None, None))
        tx = P.load(xo, x_in_v[oc, :, tc * 512:(tc + 1) * 512],
                    deps=[sprev[1]])
        tt_ = P.dve(lambda E, oc=oc, pbk=pbk, xo=xo, xn=xn:
                    E.scalar_tensor_tensor(
            xn, pbk[:], bpc[:, 16 + oc: 16 + oc + 1], xo, ALU.add, ALU.add),
            deps=[mm, tx, sprev[0], tb1])
        st = P.load(x_out_v[oc, :, tc * 512:(tc + 1) * 512], xn, deps=[tt_])
        store_prev[par] = (st, tt_)
        return tt_
    mm_fm(ctx, w["L2_w_proj"], D, D,
          lambda kc, tc: av[:, kc * T + tc * 512: kc * T + tc * 512 + 512],
          ev_out, banks=(4, 5))
    ctx.flush()
    ctx.pop()
    ctx.pop()



def emit_l3a(ctx, x_in, x_out):
    P, w = ctx.P, ctx.w
    ctx.push()
    gz = ctx.bf(KC * T)
    sDl = ctx.bf(T, parts=16)
    sB = ctx.bf(T, parts=16)
    sC = ctx.bf(T, parts=16)
    A_sb = ctx.f32(16, parts=16)
    hl = ctx.bf(16, parts=16)
    bpc = ctx.f32(12)
    tb = P.load(bpc[:, 0:9], w["L3_b_in2"])
    tb = P.load(bpc[0:16, 9:12], w["L3_b_tail"], deps=[tb])
    ta = P.load(A_sb[0:16, 0:1], w["L3_A_log"])
    ta = P.act(lambda E: E.activation(A_sb[0:16, 0:1], A_sb[0:16, 0:1],
                                      AF.Exp), deps=[ta])
    ta = P.dve(lambda E: E.tensor_scalar(A_sb[0:16, 0:1], A_sb[0:16, 0:1],
                                         -1.0, None, ALU.mult), deps=[ta])

    ctx.push()
    xln = ctx.bf(KC * T)
    ctx.push()
    xbf = ctx.bf(KC * T)
    emit_ln(ctx, x_in, xln, xbf_scr=xbf, sq_scr=gz[:, 0:KC * T])
    ctx.pop()

    def ev_z(oc, tc, pbk, mm):
        return P.act(lambda E, oc=oc, tc=tc, pbk=pbk: E.activation(
            gz[:, oc * T + tc * 512: oc * T + tc * 512 + 512], pbk[:],
            AF.Gelu, bias=bpc[:, oc:oc + 1]), deps=[mm, tb])
    ctx.push()
    mm_fm(ctx, w["L3_W_in2"][:, 0:1024], D, D,
          lambda kc, tc: xln[:, kc * T + tc * 512: kc * T + tc * 512 + 512],
          ev_z)
    ctx.flush()
    ctx.pop()

    ctx.push()
    wtl = ctx.bf(KC * 96)
    u1 = ctx.f32(512, parts=16)
    u3 = ctx.f32(512, parts=16)
    rr = ctx.f32(512, parts=16)
    twl = P.loadc(wtl.rearrange("p (kc o) -> p kc o", kc=KC),
                  w["L3_W_in2"][:, 1024:1120].rearrange("(kc p) o -> p kc o",
                                                        p=128))
    prev_add = None
    for tc in range(NT):
        bank = 4 + tc % 2
        pbk = ctx.pb[bank]
        mm = None
        for kc in range(KC):
            mm = P.pe(lambda E, kc=kc, tc=tc, pbk=pbk: E.matmul(
                pbk[0:96, :], wtl[:, kc * 96:(kc + 1) * 96],
                xln[:, kc * T + tc * 512: kc * T + tc * 512 + 512],
                start=(kc == 0), stop=(kc == KC - 1)),
                deps=[twl] + ctx.bank_deps(bank), inc=(kc == KC - 1))
        a1 = P.act(lambda E, pbk=pbk: E.activation(
            u1[0:16, :], pbk[0:16, :], AF.Abs, bias=bpc[0:16, 9:10]),
            deps=[mm, tb, prev_add])
        a2 = P.act(lambda E: E.activation(u1[0:16, :], u1[0:16, :], AF.Exp,
                                          scale=-1.0), deps=[a1])
        a3 = P.act(lambda E: E.activation(u3[0:16, :], u1[0:16, :], AF.Ln,
                                          bias=1.0), deps=[a2])
        d1 = P.dve(lambda E, pbk=pbk: E.tensor_scalar(
            rr[0:16, :], pbk[0:16, :], bpc[0:16, 9:10], None, ALU.add),
            deps=[mm, tb, a1])
        d1 = P.dve(lambda E: E.tensor_single_scalar(rr[0:16, :], rr[0:16, :],
                                                    0.0, ALU.max), deps=[d1])
        prev_add = P.dve(lambda E, tc=tc: E.tensor_add(
            sDl[0:16, tc * 512:(tc + 1) * 512], rr[0:16, :], u3[0:16, :]),
            deps=[d1, a3])
        eB = P.dve(lambda E, tc=tc, pbk=pbk: E.tensor_scalar(
            sB[0:16, tc * 512:(tc + 1) * 512], pbk[32:48, :],
            bpc[0:16, 10:11], None, ALU.add), deps=[mm, tb, prev_add])
        eC = P.dve(lambda E, tc=tc, pbk=pbk: E.tensor_scalar(
            sC[0:16, tc * 512:(tc + 1) * 512], pbk[64:80, :],
            bpc[0:16, 11:12], None, ALU.add), deps=[eB])
        ctx.bank_use(bank, prev_add)
        ctx.bank_use_add(bank, a1)
        ctx.bank_use_add(bank, eC)
    ctx.flush()
    ctx.pop()
    ctx.pop()

    ctx.push()
    dA = ctx.f32(512, parts=16)
    hb = ctx.bf(512, parts=16)
    gg = ctx.bf(512, parts=16)
    yrow = ctx.f32(512, parts=1)
    yrep = ctx.f32(512)
    yz = ctx.bf(KC * 512)
    xio = ctx.f32(4 * 512)
    x_in_v = x_in.rearrange("(c p) t -> c p t", p=128)
    x_out_v = x_out.rearrange("(c p) t -> c p t", p=128)
    bout = ctx.f32(KC)
    tbo = P.load(bout, w["L3_b_out_pc"])
    for tc in range(NT):
        ts = tc * 512
        tda = P.act(lambda E, ts=ts: E.activation(
            dA[0:16, :], sDl[0:16, ts:ts + 512], AF.Exp,
            scale=A_sb[0:16, 0:1]), deps=[ta])
        init = 0.0 if tc == 0 else hl[0:16, 0:1]
        th = P.dve(lambda E, ts=ts, init=init: E.tensor_tensor_scan(
            hb[0:16, :], dA[0:16, :], sB[0:16, ts:ts + 512], init,
            ALU.mult, ALU.add), deps=[tda])
        th2 = P.dve(lambda E: E.tensor_copy(hl[0:16, 0:1], hb[0:16, 511:512]),
                    deps=[th])
        tg = P.dve(lambda E, ts=ts: E.tensor_mul(
            gg[0:16, :], hb[0:16, :], sC[0:16, ts:ts + 512]), deps=[th2])
        bank = 2
        mm = P.pe(lambda E: E.matmul(ctx.pb[bank][0:1, :], ctx.ones_col[0:16, :],
                                     gg[0:16, :], start=True, stop=True),
                  deps=[tg] + ctx.bank_deps(bank), inc=1)
        ty = P.dve(lambda E: E.tensor_copy(yrow, ctx.pb[bank][0:1, :]),
                   deps=[mm])
        ctx.bank_use(bank, ty)
        bank2 = 3
        mm2 = P.pe(lambda E: E.matmul(ctx.pb[bank2][:], ctx.ones_rowf[:],
                                      yrow, start=True, stop=True),
                   deps=[ty] + ctx.bank_deps(bank2), inc=1)
        tyr = P.dve(lambda E: E.tensor_copy(yrep, ctx.pb[bank2][:]),
                    deps=[mm2])
        ctx.bank_use(bank2, tyr)
        tzs = []
        for c in range(KC):
            tzs.append(P.dve(lambda E, c=c, ts=ts: E.tensor_mul(
                yz[:, c * 512:(c + 1) * 512],
                gz[:, c * T + ts: c * T + ts + 512], yrep), deps=[tyr]))
        store_prev = {}

        def ev_out(oc, tcc, pbk, mm, ts=ts, store_prev=store_prev):
            par = oc % 2
            xo = xio[:, par * 1024: par * 1024 + 512]
            xn = xio[:, par * 1024 + 512: par * 1024 + 1024]
            sprev = store_prev.get(par, (None, None))
            tx = P.load(xo, x_in_v[oc, :, ts:ts + 512], deps=[sprev[1]])
            tt_ = P.dve(lambda E, oc=oc, pbk=pbk, xo=xo, xn=xn:
                        E.scalar_tensor_tensor(
                xn, pbk[:], bout[:, oc:oc + 1], xo, ALU.add, ALU.add),
                deps=[mm, tx, sprev[0], tbo])
            st = P.load(x_out_v[oc, :, ts:ts + 512], xn, deps=[tt_])
            store_prev[par] = (st, tt_)
            return tt_
        ctx.push()
        mm_fm(ctx, w["L3_W_out"], D, D,
              lambda kc, tcc: yz[:, kc * 512:(kc + 1) * 512], ev_out,
              banks=(0, 1), t_lo=0, t_hi=1, extra_mm_deps=tzs)
        ctx.flush()
        ctx.pop()
    ctx.pop()
    ctx.pop()



def _bcast_col(ctx, val11, bank, deps=()):
    P = ctx.P
    col = ctx.f32(8)
    mm = P.pe(lambda E: E.matmul(ctx.pb[bank][:, 0:1], ctx.ones_rowf[:],
                                 val11, start=True, stop=True),
              deps=list(deps) + ctx.bank_deps(bank), inc=1)
    e = P.dve(lambda E: E.tensor_copy(col[:, 0:1], ctx.pb[bank][:, 0:1]),
              deps=[mm])
    ctx.bank_use(bank, e)
    return col[:, 0:1], e


def _global_amax_tile(ctx, acc, ncols, out11_dram, scr, deps=()):
    P = ctx.P
    a1 = ctx.f32(8)
    t1 = P.dve(lambda E: E.reduce_max(a1[:, 0:1], acc[:, 0:ncols], axis=AX.X),
               deps=deps)
    pp = ctx.f32(8)
    emit_pmax(ctx, a1[:, 0:1], 1, pp[:, 0:1], ("t", 6), ctx.pb[6], scr,
              deps=[t1])
    return P.load(out11_dram, pp[0:1, 0:1])


def emit_l3f1(ctx, x_in, xln8_out, lamax_out):
    P = ctx.P
    ctx.push()
    xln = ctx.bf(KC * T)
    ctx.push()
    xbf = ctx.bf(KC * T)
    sq = ctx.bf(KC * T)
    emit_ln(ctx, x_in, xln, xbf_scr=xbf, sq_scr=sq, f32_out_dram=xln8_out)
    ctx.pop()
    fb = ctx.f32(512)
    acc = ctx.f32(512)
    nb = ctx.f32(512)
    xv = xln8_out.rearrange("(c p) t -> c p t", p=128)
    tprev = None
    first = True
    for c in range(KC):
        for tc in range(NT):
            tl = P.load(fb, xv[c][:, tc * 512:(tc + 1) * 512], deps=[tprev])
            t1 = P.dve(lambda E: E.tensor_scalar(nb, fb, -1.0, None,
                                                 ALU.mult), deps=[tl])
            if first:
                t2 = P.dve(lambda E: E.tensor_max(acc, fb, nb), deps=[t1])
                first = False
            else:
                t2 = P.dve(lambda E: E.tensor_max(acc, acc, fb), deps=[t1])
                t2 = P.dve(lambda E: E.tensor_max(acc, acc, nb), deps=[t2])
            tprev = t2
    _global_amax_tile(ctx, acc, 512, lamax_out, nb, deps=[tprev])
    ctx.flush()
    ctx.pop()


def _quant_weight(ctx, wT_dram, K, O, wq, inv_col, deps=()):
    P = ctx.P
    KCd = K // 128
    nOC = O // 1024
    wv = wT_dram.rearrange("(kc p) o -> kc p o", p=128)
    rt = ctx.f32(512)
    ct = ctx.f32(512)
    fb = ctx.f32(1024)
    tprev = None
    for kc in range(KCd):
        for o4 in range(nOC):
            tl = P.load(fb, wv[kc][:, o4 * 1024:(o4 + 1) * 1024],
                        deps=[tprev] + list(deps))
            for h in range(2):
                t1 = P.dve(lambda E, h=h: E.tensor_scalar(
                    rt, fb[:, h * 512:(h + 1) * 512], inv_col, None,
                    ALU.mult), deps=[tl])
                tprev = emit_round(
                    ctx, wq[:, kc * O + o4 * 1024 + h * 512:
                            kc * O + o4 * 1024 + h * 512 + 512],
                    rt, ctx.i32t[:, 0:512], ct)
    return tprev


def _amax_weight(ctx, wT_dram, K, O, out11_dram, deps=()):
    P = ctx.P
    KCd = K // 128
    nOC = O // 1024
    wv = wT_dram.rearrange("(kc p) o -> kc p o", p=128)
    fb = ctx.f32(1024)
    nb = ctx.f32(1024)
    acc = ctx.f32(1024)
    tprev = None
    first = True
    for kc in range(KCd):
        for o4 in range(nOC):
            tl = P.load(fb, wv[kc][:, o4 * 1024:(o4 + 1) * 1024],
                        deps=[tprev] + list(deps))
            t1 = P.dve(lambda E: E.tensor_scalar(nb, fb, -1.0, None,
                                                 ALU.mult), deps=[tl])
            if first:
                t2 = P.dve(lambda E: E.tensor_max(acc, fb, nb), deps=[t1])
                first = False
            else:
                t2 = P.dve(lambda E: E.tensor_max(acc, acc, fb), deps=[t1])
                t2 = P.dve(lambda E: E.tensor_max(acc, acc, nb), deps=[t2])
            tprev = t2
    tk = _global_amax_tile(ctx, acc, 1024, out11_dram, nb, deps=[tprev])
    return tk


def _scale_from_amax8(ctx, amax8_dram, scr11_dram, minr):
    P = ctx.P
    row = ctx.f32(16, parts=1)
    t1 = P.load(row[:, 0:8], amax8_dram)
    t2 = P.dve(lambda E: E.reduce_max(row[:, 8:9], row[:, 0:8], axis=AX.X),
               deps=[t1])
    t2 = P.dve(lambda E: E.tensor_single_scalar(row[:, 8:9], row[:, 8:9],
                                                minr, ALU.max), deps=[t2])
    t3 = P.dve(lambda E: E.tensor_scalar(row[:, 9:10], row[:, 8:9],
                                         1.0 / 127.0, None, ALU.mult),
               deps=[t2])
    t4 = P.dve(lambda E: E.reciprocal(row[:, 10:11], row[:, 9:10]), deps=[t3])
    scol, e1 = _bcast_col(ctx, row[0:1, 9:10], 4, deps=[t4])
    icol, e2 = _bcast_col(ctx, row[0:1, 10:11], 5, deps=[t4, e1])
    return scol, icol, [e1, e2]


def _scale_from_11(ctx, val11_tok, minr):
    P = ctx.P
    val, tok = val11_tok
    r = ctx.f32(8, parts=1)
    t1 = P.dve(lambda E: E.tensor_single_scalar(r[:, 0:1], val, minr,
                                                ALU.max), deps=[tok])
    t2 = P.dve(lambda E: E.tensor_scalar(r[:, 1:2], r[:, 0:1], 1.0 / 127.0,
                                         None, ALU.mult), deps=[t1])
    t3 = P.dve(lambda E: E.reciprocal(r[:, 2:3], r[:, 1:2]), deps=[t2])
    return r[:, 1:2], r[:, 2:3], t3


def emit_l3f2(ctx, dram, outs):
    P, w = ctx.P, ctx.w
    HC = 32
    ctx.push()
    w1q = ctx.bf(KC * 4096)
    xq = ctx.bf(KC * T)
    b1s = ctx.f32(HC)
    tb = P.load(b1s, w["L3f_b1_pc"])
    scr_w = ctx.nc.dram_tensor("l3f_scrw", [1, 1], F32, kind="Internal").ap()
    ctx.push()
    tka = _amax_weight(ctx, w["L3f_w1T"], D, 4096, scr_w)
    ctx.flush()
    ctx.pop()
    wrow = ctx.f32(8, parts=1)
    tw1 = P.load(wrow[:, 0:1], scr_w, deps=[tka])
    sw11, iw11, twk = _scale_from_11(ctx, (wrow[:, 0:1], tw1), 1e-6)
    iwcol, ei = _bcast_col(ctx, iw11, 2, deps=[twk])
    swcol, ei2 = _bcast_col(ctx, sw11, 3, deps=[twk, ei])
    ctx.push()
    _quant_weight(ctx, w["L3f_w1T"], D, 4096, w1q, iwcol, deps=[ei])
    ctx.flush()
    ctx.pop()
    sxcol, ixcol, toks = _scale_from_amax8(ctx, dram["amax8"], None, 1e-6)
    swscol = ctx.f32(8)
    t5 = P.dve(lambda E: E.tensor_mul(swscol[:, 0:1], sxcol, swcol),
               deps=toks + [ei2])
    t6 = P.dve(lambda E: E.tensor_scalar(b1s, b1s, swscol[:, 0:1], None,
                                         ALU.mult), deps=[t5, tb])
    ctx.push()
    fb = ctx.f32(512)
    rt = ctx.f32(512)
    ct = ctx.f32(512)
    xv = dram["xln8"].rearrange("(c p) t -> c p t", p=128)
    tprev = None
    for c in range(KC):
        for tc in range(NT):
            tl = P.load(fb, xv[c][:, tc * 512:(tc + 1) * 512], deps=[tprev])
            t1 = P.dve(lambda E: E.tensor_scalar(rt, fb, ixcol, None,
                                                 ALU.mult),
                       deps=[tl] + toks)
            tprev = emit_round(ctx, xq[:, c * T + tc * 512:
                                       c * T + tc * 512 + 512],
                               rt, ctx.i32t[:, 0:512], ct)
    ctx.flush()
    ctx.pop()
    ctx.push()
    ht = ctx.f32(512)
    nb2 = ctx.f32(512)
    acc2 = ctx.f32(512)
    h_v = outs["h"].rearrange("(oc p) t -> oc p t", p=128)
    first = [True]
    aprev = None
    stprev = None
    for oc in range(HC):
        for tc in range(NT):
            bank = (oc * NT + tc) % 2
            pbk = ctx.pb[bank]
            mm = None
            for kc in range(KC):
                mm = P.pe(lambda E, oc=oc, kc=kc, tc=tc, pbk=pbk: E.matmul(
                    pbk[:],
                    w1q[:, kc * 4096 + oc * 128: kc * 4096 + oc * 128 + 128],
                    xq[:, kc * T + tc * 512: kc * T + tc * 512 + 512],
                    start=(kc == 0), stop=(kc == KC - 1)),
                    deps=ctx.bank_deps(bank), inc=(kc == KC - 1))
            e1 = P.act(lambda E, oc=oc, pbk=pbk: E.activation(
                ht, pbk[:], AF.Gelu, scale=swscol[:, 0:1],
                bias=b1s[:, oc:oc + 1]), deps=[mm, t6, aprev, stprev])
            ctx.bank_use(bank, e1)
            stprev = P.load(h_v[oc][:, tc * 512:(tc + 1) * 512], ht,
                            deps=[e1])
            t1 = P.dve(lambda E: E.tensor_scalar(nb2, ht, -1.0, None,
                                                 ALU.mult), deps=[e1])
            if first[0]:
                t2 = P.dve(lambda E: E.tensor_max(acc2, ht, nb2), deps=[t1])
                first[0] = False
            else:
                t2 = P.dve(lambda E: E.tensor_max(acc2, acc2, ht), deps=[t1])
                t2 = P.dve(lambda E: E.tensor_max(acc2, acc2, nb2),
                           deps=[t2])
            aprev = t2
    _global_amax_tile(ctx, acc2, 512, outs["lamax2"], nb2, deps=[aprev])
    ctx.flush()
    ctx.pop()
    ctx.pop()


def emit_l3f3(ctx, dram, outs):
    P, w = ctx.P, ctx.w
    HC = 32
    ctx.push()
    w2q = ctx.bf(HC * 1024)
    hq = ctx.bf(HC * 512)
    b2s = ctx.f32(KC)
    tb = P.load(b2s, w["L3f_b2_pc"])
    scr_w = ctx.nc.dram_tensor("l3f_scrw2", [1, 1], F32, kind="Internal").ap()
    ctx.push()
    tka = _amax_weight(ctx, w["L3f_w2T"], 4096, 1024, scr_w)
    ctx.flush()
    ctx.pop()
    wrow = ctx.f32(8, parts=1)
    tw1 = P.load(wrow[:, 0:1], scr_w, deps=[tka])
    sw11, iw11, twk = _scale_from_11(ctx, (wrow[:, 0:1], tw1), 1e-6)
    iwcol, ei = _bcast_col(ctx, iw11, 2, deps=[twk])
    swcol, ei2 = _bcast_col(ctx, sw11, 3, deps=[twk, ei])
    ctx.push()
    _quant_weight(ctx, w["L3f_w2T"], 4096, 1024, w2q, iwcol, deps=[ei])
    ctx.flush()
    ctx.pop()
    shcol, ihcol, toks = _scale_from_amax8(ctx, dram["hamax8"], None, 1e-6)
    swscol = ctx.f32(8)
    t5 = P.dve(lambda E: E.tensor_mul(swscol[:, 0:1], shcol, swcol),
               deps=toks + [ei2])
    t6 = P.dve(lambda E: E.tensor_scalar(b2s, b2s, swscol[:, 0:1], None,
                                         ALU.mult), deps=[t5, tb])
    ctx.push()
    fb = ctx.f32(512)
    rt = ctx.f32(512)
    ct = ctx.f32(512)
    otmp = ctx.f32(512)
    xio = ctx.f32(4 * 512)
    h_v = dram["h"].rearrange("(kc p) t -> kc p t", p=128)
    x_in_v = dram["x_st"].rearrange("(c p) t -> c p t", p=128)
    x_out_v = outs["x_out"].rearrange("(c p) t -> c p t", p=128)
    for tc in range(NT):
        ts = tc * 512
        tprev = None
        for kc in range(HC):
            tl = P.load(fb, h_v[kc][:, ts:ts + 512], deps=[tprev])
            t1 = P.dve(lambda E: E.tensor_scalar(rt, fb, ihcol, None,
                                                 ALU.mult),
                       deps=[tl] + toks)
            tprev = emit_round(ctx, hq[:, kc * 512:(kc + 1) * 512], rt,
                               ctx.i32t[:, 0:512], ct)
        store_prev = {}
        for oc in range(KC):
            par = oc % 2
            bank = par
            pbk = ctx.pb[bank]
            xo = xio[:, par * 1024: par * 1024 + 512]
            xn = xio[:, par * 1024 + 512: par * 1024 + 1024]
            sprev = store_prev.get(par, (None, None))
            tx = P.load(xo, x_in_v[oc, :, ts:ts + 512], deps=[sprev[1]])
            mm = None
            for kc in range(HC):
                mm = P.pe(lambda E, oc=oc, kc=kc, pbk=pbk: E.matmul(
                    pbk[:],
                    w2q[:, kc * 1024 + oc * 128: kc * 1024 + oc * 128 + 128],
                    hq[:, kc * 512:(kc + 1) * 512],
                    start=(kc == 0), stop=(kc == HC - 1)),
                    deps=[tprev] + ctx.bank_deps(bank), inc=(kc == HC - 1))
            t7 = P.dve(lambda E, pbk=pbk: E.tensor_scalar(
                otmp, pbk[:], swscol[:, 0:1], None, ALU.mult),
                deps=[mm, t6])
            tn = P.dve(lambda E, oc=oc, xo=xo, xn=xn: E.scalar_tensor_tensor(
                xn, otmp, b2s[:, oc:oc + 1], xo, ALU.add, ALU.add),
                deps=[t7, tx, sprev[0]])
            ctx.bank_use(bank, t7)
            st = P.load(x_out_v[oc, :, ts:ts + 512], xn, deps=[tn])
            store_prev[par] = (st, tn)
        ctx.flush()
    ctx.pop()
    ctx.pop()



def emit_l2f(ctx, x_in, x_out):
    P, w = ctx.P, ctx.w
    NH, HS, NS = 8, 64, 256
    ctx.push()
    qkvt = ctx.bf(16 * 1536)
    QK = ctx.bf(8 * T)
    Qf, Kf = QK[:, 0:4 * T], QK[:, 4 * T:8 * T]
    Vta = ctx.bf(16 * NH * 65)
    avd = ctx.bf(4 * T)
    msc = ctx.bf(4 * T)
    offi = ctx.i32t
    offd = ctx.nc.dram_tensor("l2f_offd", [8, T], F32, kind="Internal").ap()
    gated = ctx.nc.dram_tensor("l2f_gated", [8, T], F32, kind="Internal").ap()
    idxd = [ctx.nc.dram_tensor(f"l2f_idxd{i}", [NS + 1, 1], F32,
                               kind="Internal").ap() for i in range(NH)]
    xtokd = ctx.nc.dram_tensor("l2f_xtok", [T, D], BF16, kind="Internal").ap()
    Vt3 = Vta.rearrange("p (tt h d) -> p (tt h) d", h=NH, d=65)
    tves = P.dve(lambda E: E.memset(Vt3[:, :, 64:65], 1.0))

    ctx.push()
    xln = ctx.bf(KC * T)
    emit_ln(ctx, x_in, xln, xbf_scr=qkvt[:, 0:KC * T], sq_scr=QK[:, 0:KC * T])
    ctx.push()
    gtb = qkvt[0:8, 0:2048]
    selb = qkvt[0:8, 2048:4096]
    gates = qkvt[0:8, 4096:6144]
    offr = qkvt[0:8, 6144:8192]
    offtok = ctx.f32(16 * 8)
    lohi = ctx.f32(8, parts=8)
    postok = ctx.f32(16)
    Srow = ctx.f32(2048, parts=8)
    eqc = ctx.f32(2048, parts=8)

    ctx.push()
    rw = ctx.bf(KC * 8)
    stok = ctx.f32(8)
    trw = P.loadc(rw.rearrange("p (kc o) -> p kc o", kc=KC),
                  w["L2f_router"].rearrange("(kc p) o -> p kc o", p=128))
    for tt in range(16):
        bank = tt % 2
        pbk = ctx.pb[bank]
        mm = None
        for kc in range(KC):
            mm = P.pe(lambda E, tt=tt, kc=kc, pbk=pbk: E.matmul(
                pbk[:, 0:8],
                xln[:, kc * T + tt * 128: kc * T + (tt + 1) * 128],
                rw[:, kc * 8:(kc + 1) * 8], start=(kc == 0),
                stop=(kc == KC - 1)),
                deps=[trw] + ctx.bank_deps(bank), inc=(kc == KC - 1))
        e = P.dve(lambda E, pbk=pbk: E.tensor_copy(stok, pbk[:, 0:8]),
                  deps=[mm])
        ctx.bank_use(bank, e)
        mm2 = P.pe(lambda E: E.transpose(ctx.pb[2][0:8, 0:128], stok,
                                         ctx.identf[:]),
                   deps=[e] + ctx.bank_deps(2), inc=1)
        e2 = P.dve(lambda E, tt=tt: E.tensor_copy(
            Srow[0:8, tt * 128:(tt + 1) * 128], ctx.pb[2][0:8, 0:128]),
            deps=[mm2])
        ctx.bank_use(2, e2)
    ctx.flush()
    ctx.pop()

    lo, hi, mid, cnt, cc, sc1, sc2, need = (lohi[0:8, i:i + 1]
                                            for i in range(8))
    mkb = gtb
    o8 = _ones_row8(ctx)
    t1 = P.dve(lambda E: E.tensor_reduce(lo, Srow[0:8, :], AX.X, ALU.min))
    t1 = P.dve(lambda E: E.tensor_single_scalar(lo, lo, -1.0, ALU.add),
               deps=[t1])
    t1 = P.dve(lambda E: E.reduce_max(hi, Srow[0:8, :], axis=AX.X), deps=[t1])
    t1 = P.dve(lambda E: E.tensor_single_scalar(hi, hi, 1.0, ALU.add),
               deps=[t1])
    for it in range(72):
        t1 = P.dve(lambda E: E.tensor_add(mid, lo, hi), deps=[t1])
        t1 = P.dve(lambda E: E.tensor_single_scalar(mid, mid, 0.5, ALU.mult),
                   deps=[t1])
        t1 = P.dve(lambda E: E.tensor_scalar(
            mkb[0:8, :], Srow[0:8, :], mid, None, ALU.is_ge, ALU.add,
            accum_out=cnt), deps=[t1])
        t1 = P.dve(lambda E: E.tensor_single_scalar(cc, cnt, float(NS),
                                                    ALU.is_ge), deps=[t1])
        t1 = P.dve(lambda E: E.tensor_sub(sc1, mid, lo), deps=[t1])
        t1 = P.dve(lambda E: E.tensor_mul(sc1, sc1, cc), deps=[t1])
        t1 = P.dve(lambda E: E.tensor_add(lo, lo, sc1), deps=[t1])
        t1 = P.dve(lambda E: E.tensor_sub(sc2, hi, mid), deps=[t1])
        t1 = P.dve(lambda E: E.tensor_mul(sc2, sc2, cc), deps=[t1])
        t1 = P.dve(lambda E: E.tensor_add(hi, mid, sc2), deps=[t1])
    t1 = P.dve(lambda E: E.tensor_scalar(gtb[0:8, :], Srow[0:8, :], lo, None,
                                         ALU.is_gt, ALU.add,
                                         accum_out=cnt), deps=[t1])
    t1 = P.dve(lambda E: E.tensor_scalar(selb[0:8, :], Srow[0:8, :], lo, None,
                                         ALU.is_equal), deps=[t1])
    t1 = P.dve(lambda E: E.tensor_scalar(need, cnt, -1.0, float(NS), ALU.mult,
                                         ALU.add), deps=[t1])
    t1 = P.dve(lambda E, o8=o8: E.tensor_tensor_scan(eqc[0:8, :], o8[0:8, :],
                                                     selb[0:8, :], 0.0,
                                                     ALU.mult, ALU.add),
               deps=[t1])
    t1 = P.dve(lambda E: E.tensor_scalar(eqc[0:8, :], eqc[0:8, :], need, None,
                                         ALU.is_le), deps=[t1])
    t1 = P.dve(lambda E: E.tensor_mul(selb[0:8, :], selb[0:8, :], eqc[0:8, :]),
               deps=[t1])
    t1 = P.dve(lambda E: E.tensor_add(selb[0:8, :], gtb[0:8, :], selb[0:8, :]),
               deps=[t1])
    t1 = P.dve(lambda E, o8=o8: E.tensor_tensor_scan(eqc[0:8, :], o8[0:8, :],
                                                     selb[0:8, :], 0.0,
                                                     ALU.mult, ALU.add),
               deps=[t1])
    t1 = P.dve(lambda E: E.tensor_single_scalar(eqc[0:8, :], eqc[0:8, :],
                                                -1.0, ALU.add), deps=[t1])
    t1 = P.dve(lambda E: E.tensor_mul(eqc[0:8, :], eqc[0:8, :], selb[0:8, :]),
               deps=[t1])
    t1 = P.dve(lambda E: E.tensor_scalar(offr[0:8, :], selb[0:8, :], -256.0,
                                         256.0, ALU.mult, ALU.add), deps=[t1])
    t1 = P.dve(lambda E: E.tensor_add(offr[0:8, :], offr[0:8, :], eqc[0:8, :]),
               deps=[t1])
    tg = P.act(lambda E: E.activation(gates[0:8, :], Srow[0:8, :], AF.Sigmoid),
               deps=[t1])
    tg = P.dve(lambda E: E.tensor_mul(gates[0:8, :], gates[0:8, :],
                                      selb[0:8, :]), deps=[tg])
    st_off = P.loadc(offd, offr[0:8, :], deps=[t1])
    st_gat = P.loadc(gated, gates[0:8, :], deps=[tg])
    tio = P.gp(lambda E: E.iota(postok[:, 0:16], pattern=[[128, 16]], base=0,
                                channel_multiplier=1,
                                allow_small_or_imprecise_dtypes=True))
    for tt in range(16):
        mm = P.pe(lambda E, tt=tt: E.transpose(
            ctx.pt[0:128, 0:8], offr[0:8, tt * 128:(tt + 1) * 128],
            ctx.ident[0:8, 0:8]),
            deps=[t1] + ctx.bank_deps(("pt",)), inc=1)
        e = P.dve(lambda E, tt=tt: E.tensor_copy(
            offtok[:, tt * 8:(tt + 1) * 8], ctx.pt[0:128, 0:8]), deps=[mm])
        ctx.bank_use(("pt",), e)
        e2 = P.dve(lambda E, tt=tt: E.tensor_copy(
            offi[:, tt * 8:(tt + 1) * 8], offtok[:, tt * 8:(tt + 1) * 8]),
            deps=[e])
        for h in range(NH):
            P.gpdma(lambda E, tt=tt, h=h: E.indirect_dma_start(
                idxd[h], IndirectOffsetOnAxis(
                    offi[:, tt * 8 + h: tt * 8 + h + 1], 0),
                postok[:, tt:tt + 1], None), deps=[e2, tio])
    ctx.flush()
    ctx.pop()

    ctx.push()
    wq = ctx.bf(KC * 512)
    for oc3 in range(3):
        twq = P.loadc(
            wq.rearrange("p (kc o) -> p kc o", kc=KC),
            w["L2f_qkvT"][:, oc3 * 512:(oc3 + 1) * 512]
            .rearrange("(kc p) o -> p kc o", p=128),
            deps=[P.last_tok["pe"]] if oc3 else ())
        for tt in range(16):
            bank = tt % 2
            pbk = ctx.pb[bank]
            mm = None
            for kc in range(KC):
                mm = P.pe(lambda E, tt=tt, oc3=oc3, kc=kc, pbk=pbk: E.matmul(
                    pbk[:],
                    xln[:, kc * T + tt * 128: kc * T + (tt + 1) * 128],
                    wq[:, kc * 512:(kc + 1) * 512],
                    start=(kc == 0), stop=(kc == KC - 1)),
                    deps=[twq] + ctx.bank_deps(bank), inc=(kc == KC - 1))
            e = P.dve(lambda E, tt=tt, oc3=oc3, pbk=pbk: E.tensor_copy(
                qkvt[:, tt * 1536 + oc3 * 512: tt * 1536 + oc3 * 512 + 512],
                pbk[:]), deps=[mm])
            ctx.bank_use(bank, e)
    ctx.flush()
    ctx.pop()

    ctx.push()
    cs = ctx.f32(256)
    rp = ctx.f32(4 * 128)
    csv = w["L2f_costok"].rearrange("(tt p) k -> tt p k", p=128)
    prev = None
    for tt in range(16):
        tcs = P.load(cs, csv[tt], deps=[prev])
        qv3 = qkvt[:, tt * 1536:(tt + 1) * 1536].rearrange(
            "p (h o) -> p h o", o=192)
        cosv = cs[:, 0:128].rearrange("p (h k) -> p h k", k=16)
        sinv = cs[:, 128:256].rearrange("p (h k) -> p h k", k=16)
        for part in range(2):
            x1 = qv3[:, :, part * 64: part * 64 + 16]
            x2 = qv3[:, :, part * 64 + 16: part * 64 + 32]
            r1v = rp[:, 0:128].rearrange("p (h k) -> p h k", k=16)
            r2v = rp[:, 128:256].rearrange("p (h k) -> p h k", k=16)
            r3v = rp[:, 256:384].rearrange("p (h k) -> p h k", k=16)
            r4v = rp[:, 384:512].rearrange("p (h k) -> p h k", k=16)
            ta_ = P.dve(lambda E, x1=x1, cosv=cosv, r1v=r1v: E.tensor_mul(
                r1v, x1, cosv), deps=[tcs])
            ta_ = P.dve(lambda E, x2=x2, sinv=sinv, r2v=r2v: E.tensor_mul(
                r2v, x2, sinv), deps=[ta_])
            ta_ = P.dve(lambda E, x1=x1, sinv=sinv, r3v=r3v: E.tensor_mul(
                r3v, x1, sinv), deps=[ta_])
            ta_ = P.dve(lambda E, x2=x2, cosv=cosv, r4v=r4v: E.tensor_mul(
                r4v, x2, cosv), deps=[ta_])
            ta_ = P.dve(lambda E, x1=x1, r1v=r1v, r2v=r2v: E.tensor_sub(
                x1, r1v, r2v), deps=[ta_])
            prev = P.dve(lambda E, x2=x2, r3v=r3v, r4v=r4v: E.tensor_add(
                x2, r3v, r4v), deps=[ta_])
    ctx.flush()
    ctx.pop()

    ctx.push()
    xstage = ctx.bf(1024)
    qv4 = qkvt.rearrange("p (tt h o) -> p (tt h) o", h=NH, o=192)
    tprev = None
    for tt in range(16):
        for h in range(NH):
            for part, dst in ((0, Qf), (1, Kf)):
                mm = P.pe(lambda E, tt=tt, h=h, part=part: E.transpose(
                    ctx.pt[0:64, 0:128],
                    qv4[:, tt * NH + h, part * 64:(part + 1) * 64],
                    ctx.ident[:]),
                    deps=[tprev] + ctx.bank_deps(("pt",)), inc=1)
                tprev = P.dve(lambda E, tt=tt, h=h, dst=dst: E.tensor_copy(
                    dst[(h % 2) * 64:(h % 2) * 64 + 64,
                        (h // 2) * T + tt * 128:
                        (h // 2) * T + (tt + 1) * 128],
                    ctx.pt[0:64, 0:128]), deps=[mm])
                ctx.bank_use(("pt",), tprev)
        tprev = P.dve(lambda E, tt=tt: E.tensor_copy(
            Vt3[:, tt * NH:(tt + 1) * NH, 0:64],
            qv4[:, tt * NH:(tt + 1) * NH, 128:192]), deps=[tves, tprev])
    for tt in range(16):
        stw = None
        for kc in range(KC):
            mm = P.pe(lambda E, tt=tt, kc=kc: E.transpose(
                ctx.pt[:, 0:128],
                xln[:, kc * T + tt * 128: kc * T + (tt + 1) * 128],
                ctx.ident[:]), deps=[tprev] + ctx.bank_deps(("pt",)), inc=1)
            tprev = P.dve(lambda E, kc=kc: E.tensor_copy(
                xstage[:, kc * 128:(kc + 1) * 128], ctx.pt[:, 0:128]),
                deps=[mm, stw])
            ctx.bank_use(("pt",), tprev)
        stw = P.load(xtokd[tt * 128:(tt + 1) * 128, :], xstage, deps=[tprev])
        tprev = stw
    ctx.flush()
    ctx.pop()
    ctx.pop()

    ctx.push()
    Pt = ctx.bf(16 * 512)
    rl = ctx.f32(512, parts=1)
    rlrep = ctx.f32(512, parts=64)
    for tqc in range(NT):
        ts = tqc * 512
        nkt = (ts + 512) // 128
        for h in range(NH):
            hb = (h % 2) * 64
            hc = h // 2
            pts = []
            for kt in range(nkt):
                bank = kt % 2
                pbk = ctx.pb[bank]
                mm = P.pe(lambda E, kt=kt, hb=hb, hc=hc, ts=ts, pbk=pbk:
                          E.matmul(
                    pbk[:],
                    Kf[hb:hb + 64,
                       hc * T + kt * 128: hc * T + (kt + 1) * 128],
                    Qf[hb:hb + 64, hc * T + ts: hc * T + ts + 512],
                    start=True, stop=True), deps=ctx.bank_deps(bank), inc=1)
                e = P.act(lambda E, kt=kt, pbk=pbk: E.activation(
                    Pt[:, kt * 512:(kt + 1) * 512], pbk[:], AF.Exp,
                    scale=0.125, bias=ctx.cst(-12.0)), deps=[mm])
                ctx.bank_use(bank, e)
                if kt * 128 + 127 >= ts:
                    e = P.gp(lambda E, kt=kt, ts=ts: E.affine_select(
                        Pt[:, kt * 512:(kt + 1) * 512],
                        Pt[:, kt * 512:(kt + 1) * 512],
                        pattern=[[1, 512]], compare_op=ALU.is_ge, fill=0.0,
                        base=(ts - kt * 128), channel_multiplier=-1),
                        deps=[e])
                pts.append(e)
            bank = 2
            pbk = ctx.pb[bank]
            mm = None
            for kt in range(nkt):
                mm = P.pe(lambda E, kt=kt, h=h, pbk=pbk: E.matmul(
                    pbk[0:65, :],
                    Vt3[:, kt * NH + h: kt * NH + h + 1, :]
                    .rearrange("p a d -> p (a d)"),
                    Pt[:, kt * 512:(kt + 1) * 512],
                    start=(kt == 0), stop=(kt == nkt - 1)),
                    deps=pts + ctx.bank_deps(bank), inc=(kt == nkt - 1))
            t1_ = P.dve(lambda E, pbk=pbk: E.reciprocal(rl, pbk[64:65, :]),
                        deps=[mm])
            mm2 = P.pe(lambda E: E.matmul(
                ctx.pb[3][0:64, :], ctx.ones_rowf[0:1, 0:64], rl,
                start=True, stop=True), deps=[t1_] + ctx.bank_deps(3), inc=1)
            t2_ = P.dve(lambda E: E.tensor_copy(rlrep, ctx.pb[3][0:64, :]),
                        deps=[mm2])
            ctx.bank_use(3, t2_)
            dst = avd[hb:hb + 64, hc * T + ts: hc * T + ts + 512]
            t3_ = P.dve(lambda E, dst=dst, pbk=pbk: E.tensor_mul(
                dst, pbk[0:64, :], rlrep), deps=[t2_])
            ctx.bank_use(bank, t3_)
        ctx.flush()
    ctx.pop()

    ctx.push()
    Xgt = ctx.bf(2 * 1024)
    Xgf = ctx.bf(KC * 256)
    qkvg = ctx.bf(2 * 192)
    csg = ctx.f32(2 * 32)
    idxrow = ctx.f32(256, parts=1)
    idxvt2 = ctx.f32(2)
    idxg = ctx.f32(128, parts=2)
    qg = ctx.bf(256, parts=64)
    kg = ctx.bf(256, parts=64)
    vgt = ctx.bf(160)[:, 0:130]
    Ptg = ctx.bf(2 * 256)
    avm = ctx.bf(256, parts=64)
    avmT = ctx.bf(2 * 64)
    Gt = ctx.bf(2 * 512)
    orow = ctx.f32(512, parts=1)
    orep = ctx.f32(768)
    grow = ctx.f32(512, parts=1)
    grep = ctx.f32(512, parts=64)
    ropg = ctx.f32(64)
    iocol = ctx.f32(2)
    wqm = ctx.bf(KC * 192)
    tic = P.gp(lambda E: E.iota(iocol[:, 0:2], pattern=[[128, 2]], base=0,
                                channel_multiplier=1,
                                allow_small_or_imprecise_dtypes=True))
    vg3 = vgt.rearrange("p (g d) -> p g d", d=65)
    tvm = P.dve(lambda E: E.memset(vg3[:, :, 64:65], 1.0))
    prev_head = None
    for h in range(NH):
        tidx = P.load(idxg, idxd[h][0:NS].rearrange("(g p) o -> g (p o)",
                                                    p=128),
                      deps=[prev_head])
        mm = P.pe(lambda E: E.transpose(ctx.pb[4][0:128, 0:2], idxg,
                                        ctx.identf[0:2, 0:2]),
                  deps=[tidx] + ctx.bank_deps(4), inc=1)
        tvx = P.dve(lambda E: E.tensor_copy(idxvt2[:, 0:2],
                                            ctx.pb[4][0:128, 0:2]),
                    deps=[mm])
        ctx.bank_use(4, tvx)
        ti32 = P.dve(lambda E: E.tensor_copy(offi[:, 300:302], idxvt2[:, 0:2]),
                     deps=[tvx])
        g_toks = [ti32]
        for g in range(2):
            g_toks.append(P.gpdma(lambda E, g=g: E.indirect_dma_start(
                Xgt[:, g * 1024:(g + 1) * 1024], None, xtokd,
                IndirectOffsetOnAxis(offi[:, 300 + g: 301 + g], 0)),
                deps=[ti32]))
            g_toks.append(P.gpdma(lambda E, g=g: E.indirect_dma_start(
                csg.rearrange("p (g k) -> p g k", k=32)[:, g, :], None,
                w["L2f_csg"],
                IndirectOffsetOnAxis(offi[:, 300 + g: 301 + g], 0)),
                deps=[ti32]))
        tir = P.load(idxrow, idxd[h][0:NS].rearrange("a b -> (a b)")[None, :],
                     deps=[prev_head])
        tprev2 = prev_head
        for g in range(2):
            for kc in range(KC):
                mm = P.pe(lambda E, g=g, kc=kc: E.transpose(
                    ctx.pt[:, 0:128],
                    Xgt[:, g * 1024 + kc * 128: g * 1024 + (kc + 1) * 128],
                    ctx.ident[:]),
                    deps=g_toks + [tprev2] + ctx.bank_deps(("pt",)), inc=1)
                tprev2 = P.dve(lambda E, g=g, kc=kc: E.tensor_copy(
                    Xgf[:, kc * 256 + g * 128: kc * 256 + g * 128 + 128],
                    ctx.pt[:, 0:128]), deps=[mm])
                ctx.bank_use(("pt",), tprev2)
        twm = P.loadc(wqm.rearrange("p (kc o) -> p kc o", kc=KC),
                      w["L2f_wqkv_m"][:, h * 192:(h + 1) * 192]
                      .rearrange("(kc p) o -> p kc o", p=128),
                      deps=[prev_head])
        for g in range(2):
            bank = 4 + g
            pbk = ctx.pb[bank]
            mm = None
            for kc in range(KC):
                mm = P.pe(lambda E, g=g, kc=kc, pbk=pbk: E.matmul(
                    pbk[:, 0:192],
                    Xgf[:, kc * 256 + g * 128: kc * 256 + g * 128 + 128],
                    wqm[:, kc * 192:(kc + 1) * 192],
                    start=(kc == 0), stop=(kc == KC - 1)),
                    deps=[twm, tprev2] + ctx.bank_deps(bank),
                    inc=(kc == KC - 1))
            e = P.dve(lambda E, g=g, pbk=pbk: E.tensor_copy(
                qkvg[:, g * 192:(g + 1) * 192], pbk[:, 0:192]), deps=[mm])
            ctx.bank_use(bank, e)
        prevr = P.last_tok["dve"]
        for g in range(2):
            for part in range(2):
                x1 = qkvg[:, g * 192 + part * 64: g * 192 + part * 64 + 16]
                x2 = qkvg[:, g * 192 + part * 64 + 16:
                          g * 192 + part * 64 + 32]
                cosv = csg[:, g * 32: g * 32 + 16]
                sinv = csg[:, g * 32 + 16: g * 32 + 32]
                ta_ = P.dve(lambda E, x1=x1, cosv=cosv: E.tensor_mul(
                    ropg[:, 0:16], x1, cosv), deps=[prevr] + g_toks)
                ta_ = P.dve(lambda E, x2=x2, sinv=sinv: E.tensor_mul(
                    ropg[:, 16:32], x2, sinv), deps=[ta_])
                ta_ = P.dve(lambda E, x1=x1, sinv=sinv: E.tensor_mul(
                    ropg[:, 32:48], x1, sinv), deps=[ta_])
                ta_ = P.dve(lambda E, x2=x2, cosv=cosv: E.tensor_mul(
                    ropg[:, 48:64], x2, cosv), deps=[ta_])
                ta_ = P.dve(lambda E, x1=x1: E.tensor_sub(
                    x1, ropg[:, 0:16], ropg[:, 16:32]), deps=[ta_])
                prevr = P.dve(lambda E, x2=x2: E.tensor_add(
                    x2, ropg[:, 32:48], ropg[:, 48:64]), deps=[ta_])
        tprev3 = prevr
        for g in range(2):
            for part, dst in ((0, qg), (1, kg)):
                mm = P.pe(lambda E, g=g, part=part: E.transpose(
                    ctx.pt[0:64, 0:128],
                    qkvg[:, g * 192 + part * 64: g * 192 + (part + 1) * 64],
                    ctx.ident[:]),
                    deps=[tprev3] + ctx.bank_deps(("pt",)), inc=1)
                tprev3 = P.dve(lambda E, g=g, dst=dst: E.tensor_copy(
                    dst[0:64, g * 128:(g + 1) * 128], ctx.pt[0:64, 0:128]),
                    deps=[mm])
                ctx.bank_use(("pt",), tprev3)
            tprev3 = P.dve(lambda E, g=g: E.tensor_copy(
                vg3[:, g, 0:64], qkvg[:, g * 192 + 128: g * 192 + 192]),
                deps=[tprev3, tvm])
        mm = P.pe(lambda E: E.matmul(ctx.pb[6][:, 0:256], ctx.ones_rowf[:],
                                     idxrow, start=True, stop=True),
                  deps=[tir] + ctx.bank_deps(6), inc=1)
        tqr = P.dve(lambda E: E.tensor_copy(orep[:, 0:256],
                                            ctx.pb[6][:, 0:256]), deps=[mm])
        ctx.bank_use(6, tqr)
        spts = []
        for g in range(2):
            bank = g
            pbk = ctx.pb[bank]
            mm = P.pe(lambda E, g=g, pbk=pbk: E.matmul(
                pbk[:, 0:256], kg[0:64, g * 128:(g + 1) * 128],
                qg[0:64, 0:256], start=True, stop=True),
                deps=[tprev3] + ctx.bank_deps(bank), inc=1)
            e = P.act(lambda E, g=g, pbk=pbk: E.activation(
                Ptg[:, g * 256:(g + 1) * 256], pbk[:, 0:256], AF.Exp,
                scale=0.125, bias=ctx.cst(-12.0)), deps=[mm])
            ctx.bank_use(bank, e)
            e2 = P.dve(lambda E, g=g: E.tensor_scalar(
                orep[:, 256 + g * 256: 256 + (g + 1) * 256], orep[:, 0:256],
                idxvt2[:, g:g + 1], None, ALU.is_ge), deps=[tqr, tvx])
            spts.append(P.dve(lambda E, g=g: E.tensor_mul(
                Ptg[:, g * 256:(g + 1) * 256], Ptg[:, g * 256:(g + 1) * 256],
                orep[:, 256 + g * 256: 256 + (g + 1) * 256]), deps=[e, e2]))
        bank = 2
        pbk = ctx.pb[bank]
        mm = None
        for g in range(2):
            mm = P.pe(lambda E, g=g, pbk=pbk: E.matmul(
                pbk[0:65, 0:256], vg3[:, g, :],
                Ptg[:, g * 256:(g + 1) * 256],
                start=(g == 0), stop=(g == 1)),
                deps=spts + ctx.bank_deps(bank), inc=(g == 1))
        rlg = rl_g(ctx)
        t1_ = P.dve(lambda E, pbk=pbk: E.reciprocal(rlg, pbk[64:65, 0:256]),
                    deps=[mm])
        mm2 = P.pe(lambda E: E.matmul(ctx.pb[3][0:64, 0:256],
                                      ctx.ones_rowf[0:1, 0:64], rlg,
                                      start=True, stop=True),
                   deps=[t1_] + ctx.bank_deps(3), inc=1)
        t2_ = P.dve(lambda E: E.tensor_copy(grep[0:64, 0:256],
                                            ctx.pb[3][0:64, 0:256]),
                    deps=[mm2])
        ctx.bank_use(3, t2_)
        t3_ = P.dve(lambda E, pbk=pbk: E.tensor_mul(
            avm[0:64, :], pbk[0:64, 0:256], grep[0:64, 0:256]), deps=[t2_])
        ctx.bank_use(bank, t3_)
        tprev4 = t3_
        for g in range(2):
            mm = P.pe(lambda E, g=g: E.transpose(
                ctx.pt[:, 0:64], avm[0:64, g * 128:(g + 1) * 128],
                ctx.ident[0:64, 0:64]),
                deps=[tprev4] + ctx.bank_deps(("pt",)), inc=1)
            tprev4 = P.dve(lambda E, g=g: E.tensor_copy(
                avmT[:, g * 64:(g + 1) * 64], ctx.pt[:, 0:64]), deps=[mm])
            ctx.bank_use(("pt",), tprev4)
        hb2 = (h % 2) * 64
        hc2 = h // 2
        last_ms = None
        for tc in range(NT):
            ts2 = tc * 512
            tor = P.load(orow, offd[h][None, ts2:ts2 + 512],
                         deps=[st_off, last_ms])
            tgr = P.load(grow, gated[h][None, ts2:ts2 + 512],
                         deps=[st_gat, last_ms])
            mmo = P.pe(lambda E: E.matmul(
                ctx.pb[6][:], ctx.ones_rowf[:], orow, start=True, stop=True),
                deps=[tor] + ctx.bank_deps(6), inc=1)
            teo = P.dve(lambda E: E.tensor_copy(orep[:, 0:512],
                                                ctx.pb[6][:]), deps=[mmo])
            ctx.bank_use(6, teo)
            gts = []
            for g in range(2):
                gts.append(P.dve(lambda E, g=g: E.tensor_scalar(
                    Gt[:, g * 512:(g + 1) * 512], orep[:, 0:512],
                    iocol[:, g:g + 1], None, ALU.is_equal), deps=[teo, tic]))
            mmg = P.pe(lambda E: E.matmul(
                ctx.pb[5][0:64, :], ctx.ones_rowf[0:1, 0:64], grow,
                start=True, stop=True), deps=[tgr] + ctx.bank_deps(5), inc=1)
            teg = P.dve(lambda E: E.tensor_copy(grep[0:64, 0:512],
                                                ctx.pb[5][0:64, :]),
                        deps=[mmg])
            ctx.bank_use(5, teg)
            bank2 = 4
            pbk2 = ctx.pb[bank2]
            mm3 = None
            for g in range(2):
                mm3 = P.pe(lambda E, g=g, pbk2=pbk2: E.matmul(
                    pbk2[0:64, :], avmT[:, g * 64:(g + 1) * 64],
                    Gt[:, g * 512:(g + 1) * 512],
                    start=(g == 0), stop=(g == 1)),
                    deps=gts + [tprev4] + ctx.bank_deps(bank2),
                    inc=(g == 1))
            last_ms = P.dve(lambda E, hb2=hb2, hc2=hc2, ts2=ts2, pbk2=pbk2:
                            E.tensor_mul(
                msc[hb2:hb2 + 64, hc2 * T + ts2: hc2 * T + ts2 + 512],
                pbk2[0:64, :], grep[0:64, 0:512]), deps=[mm3, teg])
            ctx.bank_use(bank2, last_ms)
        prev_head = last_ms
    ctx.flush()
    ctx.pop()

    ctx.push()
    xio = ctx.f32(4 * 512)
    wod = ctx.bf(2 * 512)
    wom = ctx.bf(2 * 512)
    wodv = w["L2f_wo_d"].rearrange("(kc p) (oc j) -> oc p kc j", p=128, j=128)
    womv = w["L2f_wo_m"].rearrange("(kc p) (oc j) -> oc p kc j", p=128, j=128)
    wods = [wod[:, 0:512].rearrange("p (kc j) -> p kc j", j=128),
            wod[:, 512:1024].rearrange("p (kc j) -> p kc j", j=128)]
    woms = [wom[:, 0:512].rearrange("p (kc j) -> p kc j", j=128),
            wom[:, 512:1024].rearrange("p (kc j) -> p kc j", j=128)]
    twd = {0: P.loadc(wods[0], wodv[0]), 1: P.loadc(wods[1], wodv[1])}
    twm2 = {0: P.loadc(woms[0], womv[0]), 1: P.loadc(woms[1], womv[1])}
    x_in_v = x_in.rearrange("(c p) t -> c p t", p=128)
    x_out_v = x_out.rearrange("(c p) t -> c p t", p=128)
    store_prev = {}
    for oc in range(KC):
        par = oc % 2
        last_mm = None
        for tc in range(NT):
            ts2 = tc * 512
            bank = tc % 2
            pbk = ctx.pb[bank]
            mm = None
            for kc in range(4):
                mm = P.pe(lambda E, kc=kc, ts2=ts2, pbk=pbk, par=par:
                          E.matmul(
                    pbk[:], wods[par][:, kc, :],
                    avd[:, kc * T + ts2: kc * T + ts2 + 512],
                    start=(kc == 0), stop=False),
                    deps=[twd[oc]] + ctx.bank_deps(bank), inc=0)
            for kc in range(4):
                mm = P.pe(lambda E, kc=kc, ts2=ts2, pbk=pbk, par=par:
                          E.matmul(
                    pbk[:], woms[par][:, kc, :],
                    msc[:, kc * T + ts2: kc * T + ts2 + 512],
                    start=False, stop=(kc == 3)),
                    deps=[twm2[oc]], inc=(kc == 3))
            xo = xio[:, par * 1024: par * 1024 + 512]
            xn = xio[:, par * 1024 + 512: par * 1024 + 1024]
            sprev = store_prev.get(par, (None, None))
            tx = P.load(xo, x_in_v[oc, :, ts2:ts2 + 512], deps=[sprev[1]])
            tn = P.dve(lambda E, pbk=pbk, xo=xo, xn=xn: E.tensor_add(
                xn, pbk[:], xo), deps=[mm, tx, sprev[0]])
            ctx.bank_use(bank, tn)
            st = P.load(x_out_v[oc, :, ts2:ts2 + 512], xn, deps=[tn])
            store_prev[par] = (st, tn)
            last_mm = mm
        if oc + 2 < KC:
            twd[oc + 2] = P.loadc(wods[par], wodv[oc + 2], deps=[last_mm])
            twm2[oc + 2] = P.loadc(woms[par], womv[oc + 2], deps=[last_mm])
    ctx.flush()
    ctx.pop()
    ctx.pop()


def _ones_row8(ctx):
    if not hasattr(ctx, "_ones8"):
        t = ctx.nc.alloc_sbuf_tensor("c_ones8", [8, T], BF16)
        ctx._ones8 = t
        ctx.P.dve(lambda E: E.memset(t[:], 1.0))
    return ctx._ones8[:]


def rl_g(ctx):
    if not hasattr(ctx, "_rlg"):
        ctx._rlg = ctx.nc.alloc_sbuf_tensor("c_rlg", [1, 256], F32)
    return ctx._rlg[:]


def mm_fm(ctx, wT_dram, K, O, rhs_fn, evac_fn, deps=(), banks=(0, 1),
          t_lo=0, t_hi=NT, extra_mm_deps=None):
    P = ctx.P
    KCd = K // 128
    OC = (O + 127) // 128
    wv4 = wT_dram.rearrange("(kc p) (oc j) -> oc p kc j", p=128, j=128)
    wv = [wv4[i] for i in range(OC)]
    wbuf = ctx.bf(2 * KCd * 128)
    wslot = [wbuf[:, 0:KCd * 128].rearrange("p (kc j) -> p kc j", j=128),
             wbuf[:, KCd * 128:2 * KCd * 128].rearrange("p (kc j) -> p kc j",
                                                        j=128)]
    tw = {0: P.loadc(wslot[0], wv[0], deps=deps)}
    if OC > 1:
        tw[1] = P.loadc(wslot[1], wv[1], deps=deps)
    toks = []
    for oc in range(OC):
        ow = min(128, O - oc * 128)
        last_mm = None
        for tc in range(t_lo, t_hi):
            bank = banks[tc % 2] if len(banks) == 2 else banks[0]
            pbk = ctx.pb[bank]
            mm = None
            for kc in range(KCd):
                mm = P.pe(lambda E, oc=oc, kc=kc, tc=tc, pbk=pbk, ow=ow:
                          E.matmul(
                    pbk[0:ow, :],
                    wbuf[:, (oc % 2) * KCd * 128 + kc * 128:
                         (oc % 2) * KCd * 128 + kc * 128 + ow],
                    rhs_fn(kc, tc), start=(kc == 0), stop=(kc == KCd - 1)),
                    deps=[tw[oc]] + ctx.bank_deps(bank) +
                         (extra_mm_deps or []), inc=(kc == KCd - 1))
            e = evac_fn(oc, tc, pbk, mm)
            ctx.bank_use(bank, e)
            toks.append(e)
            last_mm = mm
        if oc + 2 < OC:
            tw[oc + 2] = P.loadc(wslot[oc % 2], wv[oc + 2], deps=[last_mm])
    return toks


def _t(a):
    return np.ascontiguousarray(np.asarray(a).T)


def _prep_weights(inp):
    w = {}
    w["L0_w_inprojT"] = _t(inp["L0_attn_in_proj_w"])
    w["L0_conv_xw"] = np.ascontiguousarray(
        inp["L0_attn_conv_x_w"][:, 0, :].reshape(KC, 128, 4).transpose(1, 0, 2)
        .reshape(128, KC * 4))
    w["L0_conv_zw"] = np.ascontiguousarray(
        inp["L0_attn_conv_z_w"][:, 0, :].reshape(KC, 128, 4).transpose(1, 0, 2)
        .reshape(128, KC * 4))
    w["L0_w_xprojT"] = _t(inp["L0_attn_x_proj_w"])
    w["L0_w_dtprojT"] = _t(inp["L0_attn_dt_proj_w"])
    w["L0_dt_bias"] = np.ascontiguousarray(
        inp["L0_attn_dt_proj_b"].reshape(KC, 128).T)
    w["L0_A_log"] = np.ascontiguousarray(
        inp["L0_attn_A_log"].reshape(KC, 128, 16).transpose(1, 0, 2)
        .reshape(128, KC * 16))
    w["L0_D"] = np.ascontiguousarray(inp["L0_attn_D"].reshape(KC, 128).T)
    w["L0_w_outprojT"] = _t(inp["L0_attn_out_proj_w"])
    w["L0f_w1T"] = _t(inp["L0_ffn_w1_w"])
    w["L0f_w2T"] = _t(inp["L0_ffn_w2_w"])
    w["L0f_h1T"] = _t(inp["L0_ffn_h1_w"])
    w["L0f_h2T"] = _t(inp["L0_ffn_h2_w"])
    w["L0f_w3T"] = _t(inp["L0_ffn_w3_w"])
    w["L0f_s1"] = np.ascontiguousarray(inp["L0_ffn_w1_s"].reshape(32, 128).T)
    w["L0f_s2"] = np.ascontiguousarray(inp["L0_ffn_w2_s"].reshape(32, 128).T)
    w["L0f_sh1"] = np.ascontiguousarray(inp["L0_ffn_h1_s"].reshape(32, 128).T)
    w["L0f_sh2"] = np.ascontiguousarray(inp["L0_ffn_h2_s"].reshape(32, 128).T)
    w["L0f_s3"] = np.ascontiguousarray(inp["L0_ffn_w3_s"].reshape(8, 128).T)
    w["L1_wqT"] = _t(inp["L1_attn_Wq"])
    w["L1_wkT"] = _t(inp["L1_attn_Wk"])
    w["L1_wvT"] = _t(inp["L1_attn_Wv"])
    w["L1_bq"] = np.ascontiguousarray(inp["L1_attn_bq"].reshape(8, 128).T)
    w["L1_bk"] = np.ascontiguousarray(inp["L1_attn_bk"].reshape(8, 128).T)
    GAMMA = np.float64(0.96875)
    i = np.arange(T, dtype=np.float64)
    Dm = np.where(i[:, None] >= i[None, :], GAMMA ** (i[:, None] - i[None, :]),
                  0.0)
    w["L1_Dt"] = np.ascontiguousarray(Dm.T.astype(np.float32))
    w["L1_rhs"] = np.ascontiguousarray(inp["L1_ffn_rhs"])
    w["L2_w_attn"] = np.ascontiguousarray(inp["L2_attn_w_attn"])
    w["L2_b_attn"] = np.ascontiguousarray(inp["L2_attn_b_attn"])[None, :]
    w["L2_b_attn_pc"] = np.ascontiguousarray(
        inp["L2_attn_b_attn"][0:2048].reshape(16, 128).T)
    w["L2_b_attn_pc"] = np.concatenate(
        [w["L2_b_attn_pc"],
         inp["L2_attn_b_proj"].reshape(8, 128).T], axis=1)
    w["L2_w_proj"] = np.ascontiguousarray(inp["L2_attn_w_proj"])
    w["L2f_router"] = np.ascontiguousarray(inp["L2_ffn_mosa_router_w"])
    w["L2f_qkvT"] = np.ascontiguousarray(
        inp["L2_ffn_dense_wqkv"].reshape(1024, 1536))
    w["L2f_wqkv_m"] = np.ascontiguousarray(
        inp["L2_ffn_mosa_wqkv"].reshape(1024, 1536))
    w["L2f_wo_d"] = np.ascontiguousarray(inp["L2_ffn_dense_wo"])
    w["L2f_wo_m"] = np.ascontiguousarray(inp["L2_ffn_mosa_wo"])
    half = 16
    inv = (10000.0 ** (-np.arange(half, dtype=np.float64) / half))
    ang = np.arange(T, dtype=np.float64)[:, None] * inv[None, :]
    cosr = np.cos(ang).astype(np.float32)
    sinr = np.sin(ang).astype(np.float32)
    w["L2f_costok"] = np.ascontiguousarray(
        np.concatenate([np.tile(cosr, (1, 8)), np.tile(sinr, (1, 8))], 1))
    w["L2f_csg"] = np.ascontiguousarray(np.concatenate([cosr, sinr], 1))
    Wi = inp["L3_attn_W_in"]
    bi = inp["L3_attn_b_in"]
    W2 = np.zeros((1024, 1024 + 96), np.float32)
    W2[:, 0:1024] = Wi[:, 1024:2048]
    W2[:, 1024:1040] = Wi[:, 2080:2096]
    W2[:, 1056:1072] = Wi[:, 2048:2064]
    W2[:, 1088:1104] = Wi[:, 2064:2080]
    w["L3_W_in2"] = W2
    b2 = np.zeros((128, 9), np.float32)
    b2[:, 0:8] = bi[1024:2048].reshape(8, 128).T
    w["L3_b_in2"] = np.ascontiguousarray(b2)
    bt = np.zeros((16, 3), np.float32)
    bt[:, 0] = bi[2080:2096]
    bt[:, 1] = bi[2048:2064]
    bt[:, 2] = bi[2064:2080]
    w["L3_b_tail"] = bt
    w["L3_A_log"] = np.ascontiguousarray(inp["L3_attn_A_log"][:, None])
    w["L3_W_out"] = np.ascontiguousarray(inp["L3_attn_W_out"])
    w["L3_b_out_pc"] = np.ascontiguousarray(
        inp["L3_attn_b_out"].reshape(8, 128).T)
    w["L3f_w1T"] = _t(inp["L3_ffn_w1"])
    w["L3f_b1_pc"] = np.ascontiguousarray(inp["L3_ffn_b1"].reshape(32, 128).T)
    w["L3f_w2T"] = _t(inp["L3_ffn_w2"])
    w["L3f_b2_pc"] = np.ascontiguousarray(inp["L3_ffn_b2"].reshape(8, 128).T)
    return w


def _final_wait(ctx):
    P = ctx.P
    toks = []
    for q in _DMAQ:
        toks += [(f"{q}{i}", P.counts[f"{q}{i}"]) for i in range(NDMASEM)
                 if P.counts[f"{q}{i}"] > 0]
    P.op("sp0", lambda E: E.nop(), deps=toks, inc=0, chain=False)
    P.flush()


def _launch(build_fn, w_names, w, per_core_ins, out_shapes, nl):
    nc = bass.Bass("TRN2", target_bir_lowering=False)
    dram = {}
    for name in w_names:
        dram[name] = nc.dram_tensor(name, list(w[name].shape), F32,
                                    kind="ExternalInput").ap()
    for name, arr0 in per_core_ins[0].items():
        dram[name] = nc.dram_tensor(name, list(arr0.shape), F32,
                                    kind="ExternalInput").ap()
    outs = {}
    for name, shape in list(out_shapes.items()) + list(_DBG_TAPS.items()):
        outs[name] = nc.dram_tensor(name, list(shape), F32,
                                    kind="ExternalOutput").ap()
    build_fn(nc, dram, outs)
    nc.finalize()
    in_maps = []
    for c in range(NCORES):
        m = {n: w[n] for n in w_names}
        m.update(per_core_ins[c])
        in_maps.append(m)
    res = run_bass_kernel_spmd(nc, in_maps, core_ids=list(range(NCORES)))
    nl[0] += 1
    return res.results


L0A_W = ["L0_w_inprojT", "L0_conv_xw", "L0_conv_zw", "L0_w_xprojT",
         "L0_w_dtprojT", "L0_dt_bias", "L0_A_log", "L0_D", "L0_w_outprojT"]
L0F_W = ["L0f_w1T", "L0f_w2T", "L0f_h1T", "L0f_h2T", "L0f_w3T",
         "L0f_s1", "L0f_s2", "L0f_sh1", "L0f_sh2", "L0f_s3"]
L1A_W = ["L1_wqT", "L1_wkT", "L1_wvT", "L1_bq", "L1_bk", "L1_Dt"]


def _mk_build(emit_fn):
    def build(nc, dram, outs):
        ctx = Ctx(nc, dram)
        ctx.taps = outs
        try:
            ctx.emit_consts()
            emit_fn(ctx, dram["x_st"], outs["x_out"])
        except _StopEmit:
            ctx.P.pend = []
        _final_wait(ctx)
    return build


def build_l3f1(nc, dram, outs):
    ctx = Ctx(nc, dram)
    ctx.taps = outs
    try:
        ctx.emit_consts()
        emit_l3f1(ctx, dram["x_st"], outs["xln8"], outs["lamax"])
    except _StopEmit:
        ctx.P.pend = []
    _final_wait(ctx)


def build_l3f2(nc, dram, outs):
    ctx = Ctx(nc, dram)
    ctx.taps = outs
    try:
        ctx.emit_consts()
        emit_l3f2(ctx, dram, outs)
    except _StopEmit:
        ctx.P.pend = []
    _final_wait(ctx)


def build_l3f3(nc, dram, outs):
    ctx = Ctx(nc, dram)
    ctx.taps = outs
    try:
        ctx.emit_consts()
        emit_l3f3(ctx, dram, outs)
    except _StopEmit:
        ctx.P.pend = []
    _final_wait(ctx)


def kernel(**inp):
    w = _prep_weights(inp)
    x0 = inp["L0_attn_hidden_states"]
    x_fm = [np.ascontiguousarray(x0[c].T.astype(np.float32))
            for c in range(NCORES)]
    nl = [0]

    chain = [(emit_l0a, L0A_W), (emit_l0f, L0F_W), (emit_l1a, L1A_W),
             (emit_l1f, ["L1_rhs"]),
             (emit_l2a, ["L2_w_attn", "L2_b_attn", "L2_b_attn_pc",
                         "L2_w_proj"]),
             (emit_l2f, ["L2f_router", "L2f_qkvT", "L2f_wqkv_m", "L2f_wo_d",
                         "L2f_wo_m", "L2f_costok", "L2f_csg"]),
             (emit_l3a, ["L3_W_in2", "L3_b_in2", "L3_b_tail", "L3_A_log",
                         "L3_W_out", "L3_b_out_pc"])]
    if _N_SUB is not None:
        chain = chain[:_N_SUB]
    chain = chain[_SUB_START:]
    if _X_OVERRIDE is not None:
        x_fm = [np.ascontiguousarray(np.asarray(_X_OVERRIDE)[c].T.astype(
            np.float32)) for c in range(NCORES)]
    for emit_fn, wn in chain:
        res = _launch(_mk_build(emit_fn), wn, w,
                      [dict(x_st=x_fm[c]) for c in range(NCORES)],
                      dict(x_out=(D, T)), nl)
        x_fm = [res[c]["x_out"] for c in range(NCORES)]

    if _N_SUB is None or _N_SUB > 7:
        r1 = _launch(build_l3f1, [], w,
                     [dict(x_st=x_fm[c]) for c in range(NCORES)],
                     dict(xln8=(D, T), lamax=(1, 1)), nl)
        amax8 = np.concatenate([r1[c]["lamax"].reshape(1)
                                for c in range(NCORES)])[None, :]
        r2 = _launch(build_l3f2, ["L3f_w1T", "L3f_b1_pc"], w,
                     [dict(xln8=r1[c]["xln8"], amax8=amax8)
                      for c in range(NCORES)],
                     dict(h=(4096, T), lamax2=(1, 1)), nl)
        hamax8 = np.concatenate([r2[c]["lamax2"].reshape(1)
                                 for c in range(NCORES)])[None, :]
        r3 = _launch(build_l3f3, ["L3f_w2T", "L3f_b2_pc"], w,
                     [dict(h=r2[c]["h"], hamax8=hamax8, x_st=x_fm[c])
                      for c in range(NCORES)],
                     dict(x_out=(D, T)), nl)
        x_fm = [r3[c]["x_out"] for c in range(NCORES)]

    y = np.stack([x_fm[c].T for c in range(NCORES)], axis=0)
    return np.ascontiguousarray(y.astype(np.float32))


# version 06 (final)
# speedup vs baseline: 1.0325x; 1.0325x over previous
import os
import numpy as np
import ml_dtypes
import concourse.bass as bass
import concourse.mybir as mybir
from concourse.bass_utils import run_bass_kernel_spmd
from concourse.bass_core_rust import IndirectOffsetOnAxis

F32 = mybir.dt.float32
BF16 = mybir.dt.bfloat16
I32 = mybir.dt.int32
AF = mybir.ActivationFunctionType
ALU = mybir.AluOpType
AX = mybir.AxisListType

NCORES = 8
_PHASE_LIMIT = None
_N_SUB = None
_SUB_START = 0
_X_OVERRIDE = None
_DBG_TAPS = {}
T = 2048
D = 1024
NT = T // 512
KC = D // 128

NDMASEM = 8
_ENGS = ["pe", "act", "dve", "gpc"]
_DMAQ = {"sp": "sync", "gp": "gpsimd"}
PHYS_SECT = {"pe": "tensor", "act": "scalar", "dve": "vector", "gpc": "gpsimd"}
for _q, _sect in _DMAQ.items():
    for _i in range(NDMASEM):
        PHYS_SECT[f"{_q}{_i}"] = _sect
ENG_SECT = PHYS_SECT
SEM_SEG_LIMIT = 24000
BF_POOL = 87680
F32_POOL = 4 * 1024 + 704


class _StopEmit(Exception):
    pass


class Prog:

    def __init__(self, nc):
        self.nc = nc
        self.counts = {e: 0 for e in ENG_SECT}
        self.sems = {e: [nc.alloc_semaphore(f"S_{e}_0")] for e in ENG_SECT}
        self.seg_base = {e: [0] for e in ENG_SECT}
        self.last_tok = {e: None for e in ENG_SECT}
        self.pend = []
        self.nphase = 0
        self.dma_rr = {q: 0 for q in _DMAQ}

    def op(self, e, fn, deps=(), inc=1, chain=True):
        alldeps = [d for d in deps if d is not None]
        if chain and e != "pe" and self.last_tok[e] is not None:
            alldeps.append(self.last_tok[e])
        self.counts[e] += (inc if inc else 0)
        tok = (e, self.counts[e])
        self.pend.append((e, fn, tuple(alldeps), inc))
        if inc:
            self.last_tok[e] = tok
        return tok

    def pe(self, fn, deps=(), inc=1):
        return self.op("pe", fn, deps, inc)

    def act(self, fn, deps=()):
        return self.op("act", fn, deps)

    def dve(self, fn, deps=()):
        return self.op("dve", fn, deps)

    def gp(self, fn, deps=(), chain=True):
        return self.op("gpc", fn, deps, chain=chain)

    def dma(self, q, out_ap, in_ap, deps=()):
        e = f"{q}{self.dma_rr[q]}"
        self.dma_rr[q] = (self.dma_rr[q] + 1) % NDMASEM
        gate = [self.last_tok[e]]
        return self.op(e, lambda E: E.dma_start(out=out_ap, in_=in_ap),
                       list(deps) + gate, inc=16, chain=False)

    def load(self, out_ap, in_ap, deps=(), q="sp"):
        return self.dma(q, out_ap, in_ap, deps)

    def loadc(self, out_ap, in_ap, deps=()):
        return self.dma("gp", out_ap, in_ap, deps)

    def gpdma(self, fn, deps=()):
        e = f"gp{self.dma_rr['gp']}"
        self.dma_rr["gp"] = (self.dma_rr["gp"] + 1) % NDMASEM
        gate = [self.last_tok[e]]
        return self.op(e, fn, list(deps) + gate, inc=16, chain=False)

    def _locate(self, e, v):
        bases = self.seg_base[e]
        for i in range(len(bases) - 1, -1, -1):
            if v > bases[i]:
                return self.sems[e][i], v - bases[i]
        raise AssertionError(f"bad token ({e},{v})")

    def flush(self):
        pend, self.pend = self.pend, []
        if not pend:
            return
        per = {}
        for rec in pend:
            per.setdefault(PHYS_SECT[rec[0]], []).append(rec)
        self.nphase += 1
        with self.nc.Block(f"ph{self.nphase}") as blk:
            for e, items in per.items():
                def mk(items=items):
                    def body(E):
                        maxw = {}
                        for (le, fn, deps, inc) in items:
                            for (de, dv) in deps:
                                if maxw.get(de, 0) >= dv:
                                    continue
                                maxw[de] = dv
                                sem, lv = self._locate(de, dv)
                                E.wait_ge(sem, lv)
                            ins = fn(E)
                            if inc:
                                (ins if ins is not None else E.nop()).then_inc(
                                    self.sems[le][-1], inc)
                    return body
                getattr(blk, e)(mk())
        for e in ENG_SECT:
            if self.counts[e] - self.seg_base[e][-1] > SEM_SEG_LIMIT:
                s = self.nc.alloc_semaphore(f"S_{e}_{len(self.sems[e])}")
                self.sems[e].append(s)
                self.seg_base[e].append(self.counts[e])


class Ctx:
    def __init__(self, nc, w):
        self.nc = nc
        self.P = Prog(nc)
        self.w = w
        self.pb = [nc.alloc_psum_tensor(f"pb{i}", [128, 512], F32)
                   for i in range(7)]
        self.pt = nc.alloc_psum_tensor("pt", [128, 1024], BF16)
        self.bank_users = {}
        self.pool_bf = nc.alloc_sbuf_tensor("pool_bf", [128, BF_POOL], BF16)
        self.pool_f = nc.alloc_sbuf_tensor("pool_f", [128, F32_POOL], F32)
        self.off_bf = 0
        self.off_f = 0
        self.scopes = []
        self.hi_bf = 0
        self.hi_f = 0

    def push(self):
        self.scopes.append((self.off_bf, self.off_f))

    def pop(self):
        self.off_bf, self.off_f = self.scopes.pop()

    def bf(self, n, parts=128):
        n = (n + 15) & ~15
        off = self.off_bf
        self.off_bf += n
        assert self.off_bf <= BF_POOL, f"bf pool overflow {self.off_bf}"
        self.hi_bf = max(self.hi_bf, self.off_bf)
        return self.pool_bf[0:parts, off:off + n]

    def f32(self, n, parts=128):
        n = (n + 7) & ~7
        off = self.off_f
        self.off_f += n
        assert self.off_f <= F32_POOL, f"f32 pool overflow {self.off_f}"
        self.hi_f = max(self.hi_f, self.off_f)
        return self.pool_f[0:parts, off:off + n]

    def flush(self, dma_barrier=True):
        P = self.P
        if dma_barrier:
            for q in _DMAQ:
                toks = [(f"{q}{i}", P.counts[f"{q}{i}"]) for i in range(NDMASEM)
                        if P.counts[f"{q}{i}"] > 0]
                if toks:
                    P.op(f"{q}0", lambda E: E.nop(), deps=toks, inc=0,
                         chain=False)
        P.flush()
        self.bank_users = {}
        if _PHASE_LIMIT is not None and self.P.nphase >= _PHASE_LIMIT:
            raise _StopEmit()

    def bank_deps(self, i):
        return list(self.bank_users.get(i, ()))

    def bank_use(self, i, tok):
        self.bank_users[i] = [tok]

    def bank_use_add(self, i, tok):
        self.bank_users.setdefault(i, []).append(tok)

    CONST_VALS = (1e-5, -12.0, 0.5, -0.5, 127.0, 1.0 / 1024.0, 1e-6)

    def cst(self, v, parts=128):
        i = self.CONST_VALS.index(v)
        return self.cmisc[0:parts, i:i + 1]

    def get_sel16(self):
        if not hasattr(self, "_sel16"):
            P, nc = self.P, self.nc
            self._sel16 = nc.alloc_sbuf_tensor("c_sel16", [16, 16 * 128], BF16)
            self.push()
            io = self.f32(2048, parts=16)
            iov = io.rearrange("p (n j) -> p n j", n=16)
            t3 = P.gp(lambda E: E.iota(iov, pattern=[[1, 16], [0, 128]],
                                       base=0, channel_multiplier=-1,
                                       allow_small_or_imprecise_dtypes=True))
            t4 = P.dve(lambda E, io=io: E.tensor_single_scalar(
                io, io, 0.0, ALU.is_equal), deps=[t3])
            P.dve(lambda E, io=io: E.tensor_copy(self._sel16[:], io),
                  deps=[t4])
            self.flush()
            self.pop()
        return self._sel16

    def emit_consts(self):
        P, nc = self.P, self.nc
        self.cmisc = nc.alloc_sbuf_tensor("c_misc", [128, len(self.CONST_VALS)],
                                          F32)
        for i, v in enumerate(self.CONST_VALS):
            P.dve(lambda E, i=i, v=v: E.memset(self.cmisc[:, i:i + 1], v))
        self.ident = nc.alloc_sbuf_tensor("c_ident", [128, 128], BF16)
        self.ones_col = nc.alloc_sbuf_tensor("c_ones_col", [128, 1], BF16)
        self.mean_col = nc.alloc_sbuf_tensor("c_mean_col", [128, 1], BF16)
        self.ones_row = nc.alloc_sbuf_tensor("c_ones_row", [1, 128], BF16)
        self.identf = nc.alloc_sbuf_tensor("c_identf", [128, 128], F32)
        self.ones_rowf = nc.alloc_sbuf_tensor("c_ones_rowf", [1, 128], F32)
        self.i32t = nc.alloc_sbuf_tensor("c_i32", [128, 512], I32)
        self.push()
        io = self.f32(2048)
        t1 = P.gp(lambda E: E.iota(io[:, 0:128], pattern=[[1, 128]], base=0,
                                   channel_multiplier=-1,
                                   allow_small_or_imprecise_dtypes=True))
        t2 = P.dve(lambda E: E.tensor_single_scalar(io[:, 0:128], io[:, 0:128],
                                                    0.0, ALU.is_equal),
                   deps=[t1])
        t2b = P.dve(lambda E: E.tensor_copy(self.ident[:], io[:, 0:128]),
                    deps=[t2])
        t2c = P.dve(lambda E: E.tensor_copy(self.identf[:], io[:, 0:128]),
                    deps=[t2])
        P.dve(lambda E: E.memset(self.ones_col[:], 1.0))
        P.dve(lambda E: E.memset(self.mean_col[:], 1.0 / 1024.0))
        P.dve(lambda E: E.memset(self.ones_row[:], 1.0))
        P.dve(lambda E: E.memset(self.ones_rowf[:], 1.0))
        self.flush()
        self.pop()


def emit_ln(ctx, x_dram, xln, xbf_scr=None, sq_scr=None, f32_out_dram=None):
    P = ctx.P
    ctx.push()
    xbf = xbf_scr if xbf_scr is not None else ctx.bf(KC * T)
    sqbf = sq_scr if sq_scr is not None else ctx.bf(KC * T)
    stat = ctx.f32(4 * 512, parts=1)
    m_row, q_row = stat[:, 0:512], stat[:, 512:1024]
    v_row, r_row = stat[:, 1024:1536], stat[:, 1536:2048]
    mbf = ctx.bf(2 * 512, parts=1)
    rep = ctx.f32(2 * 512)
    m_rep, r_rep = rep[:, 0:512], rep[:, 512:1024]
    tmp = ctx.f32(512)

    tload = P.loadc(xbf.rearrange("p (c t) -> p c t", c=KC),
                    x_dram.rearrange("(c p) t -> p c t", p=128))
    tsq = []
    for c in range(KC):
        tsq.append(P.act(lambda E, c=c, sqbf=sqbf, xbf=xbf: E.activation(
            sqbf[:, c * T:(c + 1) * T], xbf[:, c * T:(c + 1) * T], AF.Square),
            deps=[tload]))
    out_toks = []
    for tc in range(NT):
        ts = tc * 512
        ev = []
        for half, (src, dst) in enumerate(((xbf, m_row), (sqbf, q_row))):
            bank = half
            pbank = ctx.pb[bank]
            dep0 = ([tload] if half == 0 else tsq) + ctx.bank_deps(bank)
            mm = None
            for c in range(KC):
                mm = P.pe(lambda E, c=c, ts=ts, src=src, pbank=pbank: E.matmul(
                    pbank[0:1, :], ctx.mean_col[:],
                    src[:, c * T + ts: c * T + ts + 512],
                    start=(c == 0), stop=(c == KC - 1)),
                    deps=dep0, inc=(c == KC - 1))
            e = P.dve(lambda E, dst=dst, pbank=pbank: E.tensor_copy(
                dst, pbank[0:1, :]), deps=[mm])
            ctx.bank_use(bank, e)
            ev.append(e)
        t1 = P.dve(lambda E: E.tensor_tensor(v_row, m_row, m_row, ALU.mult),
                   deps=ev)
        t2 = P.dve(lambda E: E.tensor_sub(v_row, q_row, v_row), deps=[t1])
        t3 = P.act(lambda E: E.activation(v_row, v_row, AF.Ln,
                                          bias=ctx.cst(1e-5, parts=1)),
                   deps=[t2])
        t4 = P.act(lambda E: E.activation(r_row, v_row, AF.Exp, scale=-0.5),
                   deps=[t3])
        t5 = P.dve(lambda E: E.tensor_copy(mbf[:, 0:512], m_row), deps=[t2])
        t6 = P.dve(lambda E: E.tensor_copy(mbf[:, 512:1024], r_row), deps=[t4])
        reps = []
        for half in range(2):
            bank = 2 + half
            pbank = ctx.pb[bank]
            mm = P.pe(lambda E, half=half, pbank=pbank: E.matmul(
                pbank[:], ctx.ones_row[:], mbf[0:1, half * 512:(half + 1) * 512],
                start=True, stop=True), deps=[t5, t6] + ctx.bank_deps(bank))
            e = P.dve(lambda E, half=half, pbank=pbank: E.tensor_copy(
                rep[:, half * 512:(half + 1) * 512], pbank[:]), deps=[mm])
            ctx.bank_use(bank, e)
            reps.append(e)
        for c in range(KC):
            ta = P.dve(lambda E, c=c, ts=ts, xbf=xbf: E.tensor_sub(
                tmp, xbf[:, c * T + ts: c * T + ts + 512], m_rep),
                deps=reps + [tload])
            tb = P.dve(lambda E, c=c, ts=ts, xln=xln: E.tensor_mul(
                xln[:, c * T + ts: c * T + ts + 512], tmp, r_rep), deps=[ta])
            out_toks.append(tb)
            if f32_out_dram is not None:
                tb2 = P.dve(lambda E, c=c, ts=ts: E.tensor_mul(
                    tmp, tmp, r_rep), deps=[ta])
                out_toks.append(P.load(
                    f32_out_dram.rearrange("(c p) t -> c p t", p=128)
                    [c, :, ts:ts + 512], tmp, deps=[tb2]))
                reps = reps + [out_toks[-1]]
    ctx.flush()
    ctx.pop()
    return out_toks


def emit_l0a(ctx, x_in, x_out):
    P, w = ctx.P, ctx.w
    ctx.push()
    Brow = ctx.bf(T, parts=16)
    Crow = ctx.bf(T, parts=16)
    hlast = ctx.bf(KC * 16)
    A_sb = ctx.f32(KC * 16)
    D_sb = ctx.f32(KC)
    dtb = ctx.f32(KC)
    xz2 = ctx.bf(2 * KC * T)
    x_conv = xz2[:, 0:KC * T]
    zsil = xz2[:, KC * T:2 * KC * T]

    ctx.push()
    xln = ctx.bf(KC * T)
    ctx.push()
    xbf = ctx.bf(KC * T)
    emit_ln(ctx, x_in, xln, xbf_scr=xbf, sq_scr=x_conv)
    ctx.pop()

    ctx.push()
    wbuf = ctx.bf(2 * KC * 128)
    wv4 = w["L0_w_inprojT"].rearrange("(kc p) (oc j) -> oc p kc j", p=128, j=128)
    wv = [wv4[i] for i in range(16)]
    wslot = [wbuf[:, 0:KC * 128].rearrange("p (kc j) -> p kc j", j=128),
             wbuf[:, KC * 128:2 * KC * 128].rearrange("p (kc j) -> p kc j", j=128)]
    tw = {0: P.loadc(wslot[0], wv[0]), 1: P.loadc(wslot[1], wv[1])}
    for oc in range(16):
        dst = x_conv if oc < KC else zsil
        c = oc % KC
        last_mm = None
        for tc in range(NT):
            bank = tc % 2
            pbank = ctx.pb[bank]
            mm = None
            for kc in range(KC):
                mm = P.pe(lambda E, oc=oc, kc=kc, tc=tc, pbank=pbank,
                          wbuf=wbuf, xln=xln: E.matmul(
                    pbank[:], wbuf[:, (oc % 2) * KC * 128 + kc * 128:
                                   (oc % 2) * KC * 128 + (kc + 1) * 128],
                    xln[:, kc * T + tc * 512: kc * T + tc * 512 + 512],
                    start=(kc == 0), stop=(kc == KC - 1)),
                    deps=[tw[oc]] + ctx.bank_deps(bank), inc=(kc == KC - 1))
            e = P.act(lambda E, c=c, tc=tc, dst=dst, pbank=pbank: E.activation(
                dst[:, c * T + tc * 512: c * T + tc * 512 + 512], pbank[:],
                AF.Copy), deps=[mm])
            ctx.bank_use(bank, e)
            last_mm = mm
        if oc + 2 < 16:
            tw[oc + 2] = P.loadc(wslot[oc % 2], wv[oc + 2], deps=[last_mm])
    ctx.flush()
    ctx.pop()

    ctx.push()
    acc = ctx.f32(2 * T)
    cw = ctx.f32(2 * KC * 4)
    tcw = P.load(cw[:, 0:KC * 4], w["L0_conv_xw"])
    tcw2 = P.load(cw[:, KC * 4:2 * KC * 4], w["L0_conv_zw"])
    prev_act = {}
    for half, (buf, woff, twd) in enumerate(
            ((x_conv, 0, tcw), (zsil, KC * 4, tcw2))):
        for c in range(KC):
            gi = half * KC + c
            s = buf[:, c * T:(c + 1) * T]
            a = acc[:, (gi % 2) * T:(gi % 2 + 1) * T]
            w0 = cw[:, woff + c * 4 + 0: woff + c * 4 + 1]
            w1 = cw[:, woff + c * 4 + 1: woff + c * 4 + 2]
            w2 = cw[:, woff + c * 4 + 2: woff + c * 4 + 3]
            w3 = cw[:, woff + c * 4 + 3: woff + c * 4 + 4]
            t0 = P.dve(lambda E, s=s, a=a, w1=w1: E.tensor_scalar(
                a, s, w1, None, ALU.mult),
                deps=[twd, prev_act.get(gi)])
            t0 = P.dve(lambda E, s=s, a=a, w0=w0: E.scalar_tensor_tensor(
                a[:, 1:T], s[:, 0:T - 1], w0, a[:, 1:T], ALU.mult, ALU.add),
                deps=[t0])
            t0 = P.dve(lambda E, s=s, a=a, w2=w2: E.scalar_tensor_tensor(
                a[:, 0:T - 1], s[:, 1:T], w2, a[:, 0:T - 1], ALU.mult, ALU.add),
                deps=[t0])
            t0 = P.dve(lambda E, s=s, a=a, w3=w3: E.scalar_tensor_tensor(
                a[:, 0:T - 2], s[:, 2:T], w3, a[:, 0:T - 2], ALU.mult, ALU.add),
                deps=[t0])
            prev_act[gi + 2] = P.act(
                lambda E, s=s, a=a: E.activation(s, a, AF.Silu), deps=[t0])
    ctx.flush()
    ctx.pop()
    ctx.pop()

    delta = ctx.bf(KC * T)

    ctx.push()
    dtr = ctx.bf(T, parts=64)
    wxp = ctx.bf(KC * 96)
    twx = P.loadc(wxp.rearrange("p (kc o) -> p kc o", kc=KC),
                  w["L0_w_xprojT"].rearrange("(kc p) o -> p kc o", p=128))
    for tc in range(NT):
        pb = ctx.pb[4]
        mm = None
        for kc in range(KC):
            mm = P.pe(lambda E, kc=kc, tc=tc, pb=pb, wxp=wxp: E.matmul(
                pb[0:80, :], wxp[:, kc * 96:kc * 96 + 80],
                x_conv[:, kc * T + tc * 512: kc * T + tc * 512 + 512],
                start=(kc == 0), stop=(kc == KC - 1)),
                deps=[twx] + ctx.bank_deps(4), inc=(kc == KC - 1))
        e1 = P.act(lambda E, tc=tc, pb=pb, dtr=dtr: E.activation(
            dtr[:, tc * 512:(tc + 1) * 512], pb[0:64, :], AF.Copy), deps=[mm])
        e2 = P.dve(lambda E, tc=tc, pb=pb: E.tensor_copy(
            Brow[:, tc * 512:(tc + 1) * 512], pb[64:80, :]), deps=[mm, e1])
        ctx.bank_use(4, e1)
        ctx.bank_use_add(4, e2)
        pb2 = ctx.pb[5]
        mm2 = None
        for kc in range(KC):
            mm2 = P.pe(lambda E, kc=kc, tc=tc, pb2=pb2, wxp=wxp: E.matmul(
                pb2[0:16, :], wxp[:, kc * 96 + 80:kc * 96 + 96],
                x_conv[:, kc * T + tc * 512: kc * T + tc * 512 + 512],
                start=(kc == 0), stop=(kc == KC - 1)),
                deps=[twx] + ctx.bank_deps(5), inc=(kc == KC - 1))
        e3 = P.dve(lambda E, tc=tc, pb2=pb2: E.tensor_copy(
            Crow[:, tc * 512:(tc + 1) * 512], pb2[0:16, :]), deps=[mm2])
        ctx.bank_use(5, e3)
    ctx.flush()

    wdt = ctx.bf(KC * 128, parts=64)
    u1 = ctx.f32(512)
    u3 = ctx.f32(512)
    rr = ctx.f32(512)
    twd = P.loadc(wdt, w["L0_w_dtprojT"])
    tbb = P.load(dtb, w["L0_dt_bias"])
    prev_add = None
    dbg = int(os.environ.get("DBG_DELTA", "3"))
    for dc in range(KC):
        for tc in range(NT):
            bank = tc % 2
            pb = ctx.pb[bank]
            bap = dtb[:, dc:dc + 1]
            mm = P.pe(lambda E, dc=dc, tc=tc, pb=pb, wdt=wdt, dtr=dtr: E.matmul(
                pb[:], wdt[:, dc * 128:(dc + 1) * 128],
                dtr[:, tc * 512:(tc + 1) * 512], start=True, stop=True),
                deps=[twd] + ctx.bank_deps(bank))
            if dbg == 1:
                prev_add = P.dve(lambda E, dc=dc, tc=tc, pb=pb: E.tensor_copy(
                    delta[:, dc * T + tc * 512: dc * T + tc * 512 + 512],
                    pb[:]), deps=[mm])
                ctx.bank_use(bank, prev_add)
                continue
            a1 = P.act(lambda E, pb=pb, bap=bap, u1=u1: E.activation(
                u1, pb[:], AF.Abs, bias=bap), deps=[mm, tbb, prev_add])
            a2 = P.act(lambda E, u1=u1: E.activation(u1, u1, AF.Exp, scale=-1.0),
                       deps=[a1])
            a3 = P.act(lambda E, u1=u1, u3=u3: E.activation(u3, u1, AF.Ln,
                                                            bias=1.0), deps=[a2])
            if dbg == 2:
                prev_add = P.dve(lambda E, dc=dc, tc=tc, u3=u3: E.tensor_copy(
                    delta[:, dc * T + tc * 512: dc * T + tc * 512 + 512],
                    u3), deps=[a3])
                ctx.bank_use(bank, a1)
                continue
            d1 = P.dve(lambda E, pb=pb, bap=bap, rr=rr: E.tensor_scalar(
                rr, pb[:], bap, None, ALU.add), deps=[mm, tbb, a1])
            d1 = P.dve(lambda E, rr=rr: E.tensor_single_scalar(
                rr, rr, 0.0, ALU.max), deps=[d1])
            prev_add = P.dve(lambda E, dc=dc, tc=tc, rr=rr, u3=u3: E.tensor_add(
                delta[:, dc * T + tc * 512: dc * T + tc * 512 + 512], rr, u3),
                deps=[d1, a3])
            ctx.bank_use(bank, a1)
            ctx.bank_use_add(bank, d1)
    ctx.flush()
    ctx.pop()

    sel16 = ctx.get_sel16()

    ta = P.load(A_sb, w["L0_A_log"])
    ta = P.act(lambda E: E.activation(A_sb, A_sb, AF.Exp), deps=[ta])
    ta = P.dve(lambda E: E.tensor_scalar(A_sb, A_sb, -1.0, None, ALU.mult),
               deps=[ta])
    td = P.load(D_sb, w["L0_D"])

    ctx.push()
    Bt = ctx.bf(16 * 512)
    Ct = ctx.bf(16 * 512)
    hbuf = ctx.bf(16 * 512)
    dAb = ctx.f32(2 * 512)
    ub = ctx.bf(2 * 512)
    gb = ctx.bf(2 * 512)
    dx = ctx.bf(512)
    ytc = ctx.bf(KC * 512)
    wout = ctx.bf(16 * 128)
    xio = ctx.f32(4 * 512)
    wov4 = w["L0_w_outprojT"].rearrange("(kc p) (oc j) -> oc p kc j",
                                        p=128, j=128)
    wov = [wov4[i] for i in range(KC)]
    woslot = [wout[:, 0:16 * 128].rearrange("p (kc j) -> p kc j", j=128)]
    x_in_v = x_in.rearrange("(c p) t -> c p t", p=128)
    x_out_v = x_out.rearrange("(c p) t -> c p t", p=128)
    hview = hbuf.rearrange("p (n t) -> p n t", n=16)

    for tc in range(NT):
        ts = tc * 512
        scan_prev = {}
        tbs = []
        for n in range(16):
            mmB = P.pe(lambda E, n=n, ts=ts: E.matmul(
                ctx.pb[4][:], sel16[:, n * 128:(n + 1) * 128],
                Brow[:, ts:ts + 512], start=True, stop=True),
                deps=ctx.bank_deps(4))
            eB = P.dve(lambda E, n=n, Bt=Bt: E.tensor_copy(
                Bt[:, n * 512:(n + 1) * 512], ctx.pb[4][:]), deps=[mmB])
            ctx.bank_use(4, eB)
            mmC = P.pe(lambda E, n=n, ts=ts: E.matmul(
                ctx.pb[5][:], sel16[:, n * 128:(n + 1) * 128],
                Crow[:, ts:ts + 512], start=True, stop=True),
                deps=ctx.bank_deps(5))
            eC = P.dve(lambda E, n=n, Ct=Ct: E.tensor_copy(
                Ct[:, n * 512:(n + 1) * 512], ctx.pb[5][:]), deps=[mmC])
            ctx.bank_use(5, eC)
            tbs.extend((eB, eC))
        two = {0: P.loadc(woslot[0], wov[0])}
        for c in range(KC):
            gmm_prev = {}
            dsl = delta[:, c * T + ts: c * T + ts + 512]
            xsl = x_conv[:, c * T + ts: c * T + ts + 512]
            tdx = P.dve(lambda E, dsl=dsl, xsl=xsl, dx=dx: E.tensor_mul(
                dx, dsl, xsl), deps=tbs)
            for n in range(16):
                par = n % 2
                tda = P.act(lambda E, dsl=dsl, c=c, n=n, par=par, dAb=dAb:
                            E.activation(
                    dAb[:, par * 512:(par + 1) * 512], dsl, AF.Exp,
                    scale=A_sb[:, c * 16 + n: c * 16 + n + 1]),
                    deps=[ta, scan_prev.get(par)])
                tu = P.dve(lambda E, n=n, par=par, ub=ub, dx=dx, Bt=Bt:
                           E.tensor_mul(
                    ub[:, par * 512:(par + 1) * 512], dx,
                    Bt[:, n * 512:(n + 1) * 512]), deps=[tdx])
                init = (0.0 if tc == 0
                        else hlast[:, c * 16 + n: c * 16 + n + 1])
                th = P.dve(lambda E, n=n, par=par, init=init, hbuf=hbuf,
                           dAb=dAb, ub=ub: E.tensor_tensor_scan(
                    hbuf[:, n * 512:(n + 1) * 512],
                    dAb[:, par * 512:(par + 1) * 512],
                    ub[:, par * 512:(par + 1) * 512],
                    init, ALU.mult, ALU.add), deps=[tda, tu])
                scan_prev[par] = th
                tg = P.dve(lambda E, n=n, par=par, gb=gb, hbuf=hbuf, Ct=Ct:
                           E.tensor_mul(
                    gb[:, par * 512:(par + 1) * 512],
                    hbuf[:, n * 512:(n + 1) * 512],
                    Ct[:, n * 512:(n + 1) * 512]),
                    deps=[th, gmm_prev.get(par)])
                gmm_prev[par] = P.pe(lambda E, n=n, par=par, gb=gb: E.matmul(
                    ctx.pb[6][:], ctx.ident[:],
                    gb[:, par * 512:(par + 1) * 512],
                    start=(n == 0), stop=(n == 15)),
                    deps=[tg] + (ctx.bank_deps(6) if n == 0 else []), inc=1)
            mm_y = P.last_tok["pe"]
            P.dve(lambda E, c=c, hview=hview: E.tensor_copy(
                hlast[:, c * 16:(c + 1) * 16],
                hview[:, :, 511:512].rearrange("p n o -> p (n o)")), deps=[])
            ty = P.dve(lambda E, c=c, xsl=xsl, ytc=ytc: E.scalar_tensor_tensor(
                ytc[:, c * 512:(c + 1) * 512], xsl, D_sb[:, c:c + 1],
                ctx.pb[6][:], ALU.mult, ALU.add), deps=[mm_y, td])
            ctx.bank_use(6, ty)
        ytok = P.last_tok["dve"]
        store_prev = {}
        for oc in range(KC):
            par = oc % 2
            bank = par
            pb = ctx.pb[bank]
            xo = xio[:, par * 1024: par * 1024 + 512]
            xn = xio[:, par * 1024 + 512: par * 1024 + 1024]
            sprev = store_prev.get(par, (None, None))
            tx = P.load(xo, x_in_v[oc, :, ts:ts + 512], deps=[sprev[1]])
            mm = None
            for kc in range(16):
                rhs = (ytc[:, kc * 512:(kc + 1) * 512] if kc < KC
                       else zsil[:, (kc - KC) * T + ts:(kc - KC) * T + ts + 512])
                mm = P.pe(lambda E, kc=kc, rhs=rhs, pb=pb, wout=wout:
                          E.matmul(
                    pb[:], wout[:, kc * 128:(kc + 1) * 128],
                    rhs, start=(kc == 0), stop=(kc == 15)),
                    deps=[two[oc], ytok] + ctx.bank_deps(bank), inc=(kc == 15))
            tn = P.dve(lambda E, pb=pb, xo=xo, xn=xn: E.tensor_add(xn, pb[:], xo),
                       deps=[mm, tx, sprev[0]])
            ctx.bank_use(bank, tn)
            st = P.load(x_out_v[oc, :, ts:ts + 512], xn, deps=[tn])
            store_prev[par] = (st, tn)
            if oc + 1 < KC:
                two[oc + 1] = P.loadc(woslot[0], wov[oc + 1], deps=[mm])
        ctx.flush()
    ctx.pop()
    ctx.pop()




def emit_l0f(ctx, x_in, x_out):
    P, w = ctx.P, ctx.w
    ctx.push()
    HC = 32
    h1 = ctx.bf(HC * 512)
    h2 = ctx.bf(HC * 512)
    xln = ctx.bf(KC * T)
    sc = ctx.f32(5 * HC)
    stmp = ctx.bf(512)
    ts1 = P.load(sc[:, 0:HC], w["L0f_s1"])
    ts2 = P.load(sc[:, HC:2 * HC], w["L0f_s2"])
    ts3 = P.load(sc[:, 2 * HC:3 * HC], w["L0f_sh1"])
    ts4 = P.load(sc[:, 3 * HC:4 * HC], w["L0f_sh2"])
    ts5 = P.load(sc[:, 4 * HC:4 * HC + KC], w["L0f_s3"])
    tsc = [ts1, ts2, ts3, ts4, ts5]

    emit_ln(ctx, x_in, xln, xbf_scr=h1[:, 0:KC * T], sq_scr=h2[:, 0:KC * T])
    xio = ctx.f32(4 * 512)

    x_in_v = x_in.rearrange("(c p) t -> c p t", p=128)
    x_out_v = x_out.rearrange("(c p) t -> c p t", p=128)

    for tc in range(NT):
        ts = tc * 512
        ctx.push()
        w1v4 = w["L0f_w1T"].rearrange("(kc p) (oc j) -> oc p kc j", p=128, j=128)
        w2v4 = w["L0f_w2T"].rearrange("(kc p) (oc j) -> oc p kc j", p=128, j=128)
        wb = ctx.bf(4 * KC * 128)
        wsl = [wb[:, i * KC * 128:(i + 1) * KC * 128]
               .rearrange("p (kc j) -> p kc j", j=128) for i in range(4)]
        tw = {(0, 0): P.loadc(wsl[0], w1v4[0], deps=tsc),
              (1, 0): P.loadc(wsl[2], w2v4[0], deps=tsc),
              (0, 1): P.loadc(wsl[1], w1v4[1]),
              (1, 1): P.loadc(wsl[3], w2v4[1])}
        prev_e2 = None
        for oc in range(HC):
            par = oc % 2
            mm1 = None
            mm2 = None
            for kc in range(KC):
                mm1 = P.pe(lambda E, kc=kc, oc=oc, par=par, ts=ts: E.matmul(
                    ctx.pb[0][:], wsl[par][:, kc, :],
                    xln[:, kc * T + ts: kc * T + ts + 512],
                    start=(kc == 0), stop=(kc == KC - 1)),
                    deps=[tw[(0, oc)]] + ctx.bank_deps(0), inc=(kc == KC - 1))
            for kc in range(KC):
                mm2 = P.pe(lambda E, kc=kc, oc=oc, par=par, ts=ts: E.matmul(
                    ctx.pb[1][:], wsl[2 + par][:, kc, :],
                    xln[:, kc * T + ts: kc * T + ts + 512],
                    start=(kc == 0), stop=(kc == KC - 1)),
                    deps=[tw[(1, oc)]] + ctx.bank_deps(1), inc=(kc == KC - 1))
            e1 = P.act(lambda E, oc=oc: E.activation(
                stmp, ctx.pb[0][:], AF.Silu, scale=sc[:, oc:oc + 1]),
                deps=[mm1, prev_e2])
            e2 = P.dve(lambda E, oc=oc: E.scalar_tensor_tensor(
                h1[:, oc * 512:(oc + 1) * 512], ctx.pb[1][:],
                sc[:, HC + oc: HC + oc + 1], stmp, ALU.mult, ALU.mult),
                deps=[mm2, e1])
            prev_e2 = e2
            ctx.bank_use(0, e1)
            ctx.bank_use(1, e2)
            if oc + 2 < HC:
                tw[(0, oc + 2)] = P.loadc(wsl[par], w1v4[oc + 2], deps=[mm1])
                tw[(1, oc + 2)] = P.loadc(wsl[2 + par], w2v4[oc + 2],
                                          deps=[mm2])
        ctx.flush()
        ctx.pop()

        ctx.push()
        def ev_h2(oc, tcc, pbk, mm):
            return P.act(lambda E, oc=oc, pbk=pbk: E.activation(
                h2[:, oc * 512:(oc + 1) * 512], pbk[:], AF.Gelu,
                scale=sc[:, 2 * HC + oc: 2 * HC + oc + 1]), deps=[mm])
        mm_fm(ctx, w["L0f_h1T"], 4096, 4096,
              lambda kc, tcc: h1[:, kc * 512:(kc + 1) * 512], ev_h2,
              t_lo=0, t_hi=1)
        ctx.flush()
        ctx.pop()

        ctx.push()
        def ev_h3(oc, tcc, pbk, mm):
            return P.act(lambda E, oc=oc, pbk=pbk: E.activation(
                h1[:, oc * 512:(oc + 1) * 512], pbk[:], AF.Gelu,
                scale=sc[:, 3 * HC + oc: 3 * HC + oc + 1]), deps=[mm])
        mm_fm(ctx, w["L0f_h2T"], 4096, 4096,
              lambda kc, tcc: h2[:, kc * 512:(kc + 1) * 512], ev_h3,
              t_lo=0, t_hi=1)
        ctx.flush()
        ctx.pop()

        ctx.push()
        store_prev = {}

        def ev_out(oc, tcc, pbk, mm, ts=ts):
            par = oc % 2
            xo = xio[:, par * 1024: par * 1024 + 512]
            xn = xio[:, par * 1024 + 512: par * 1024 + 1024]
            sprev = store_prev.get(par, (None, None))
            tx = P.load(xo, x_in_v[oc, :, ts:ts + 512], deps=[sprev[1]])
            tn = P.dve(lambda E, oc=oc, pbk=pbk, xo=xo, xn=xn:
                       E.scalar_tensor_tensor(
                xn, pbk[:], sc[:, 4 * HC + oc: 4 * HC + oc + 1], xo,
                ALU.mult, ALU.add), deps=[mm, tx, sprev[0]])
            st = P.load(x_out_v[oc, :, ts:ts + 512], xn, deps=[tn])
            store_prev[par] = (st, tn)
            return tn
        mm_fm(ctx, w["L0f_w3T"], 4096, 1024,
              lambda kc, tcc: h1[:, kc * 512:(kc + 1) * 512], ev_out,
              t_lo=0, t_hi=1)
        ctx.flush()
        ctx.pop()
    ctx.pop()



def emit_l1a(ctx, x_in, x_out):
    P, w = ctx.P, ctx.w
    ctx.push()
    Qf = ctx.bf(KC * T)
    Kf = ctx.bf(KC * T)
    Vtok = ctx.bf(16 * 1024)
    bqk = ctx.f32(2 * KC)
    tbq = P.load(bqk[:, 0:KC], w["L1_bq"])
    tbk = P.load(bqk[:, KC:2 * KC], w["L1_bk"])

    ctx.push()
    xln = ctx.bf(KC * T)
    ctx.push()
    xbf = ctx.bf(KC * T)
    emit_ln(ctx, x_in, xln, xbf_scr=xbf, sq_scr=Qf[:, 0:KC * T])
    ctx.pop()

    def mk_ev(dst, boff):
        def ev(oc, tc, pbk, mm):
            return P.act(lambda E, oc=oc, tc=tc, pbk=pbk, dst=dst: E.activation(
                dst[:, oc * T + tc * 512: oc * T + tc * 512 + 512], pbk[:],
                AF.Identity, bias=bqk[:, boff + oc: boff + oc + 1]), deps=[mm, tbq,
                                                                      tbk])
        return ev
    ctx.push()
    mm_fm(ctx, w["L1_wqT"], D, D,
          lambda kc, tc: xln[:, kc * T + tc * 512: kc * T + tc * 512 + 512],
          mk_ev(Qf, 0))
    ctx.flush()
    ctx.pop()
    ctx.push()
    mm_fm(ctx, w["L1_wkT"], D, D,
          lambda kc, tc: xln[:, kc * T + tc * 512: kc * T + tc * 512 + 512],
          mk_ev(Kf, KC))
    ctx.flush()
    ctx.pop()

    ctx.push()
    wvf = ctx.bf(KC * 1024)
    twv = P.loadc(wvf.rearrange("p (kc o) -> p kc o", kc=KC),
                  w["L1_wvT"].rearrange("(kc p) o -> p kc o", p=128))
    for tt in range(16):
        for oc2 in range(2):
            bank = (tt * 2 + oc2) % 2
            pbk = ctx.pb[bank]
            mm = None
            for kc in range(KC):
                mm = P.pe(lambda E, tt=tt, oc2=oc2, kc=kc, pbk=pbk: E.matmul(
                    pbk[:], xln[:, kc * T + tt * 128: kc * T + (tt + 1) * 128],
                    wvf[:, kc * 1024 + oc2 * 512: kc * 1024 + oc2 * 512 + 512],
                    start=(kc == 0), stop=(kc == KC - 1)),
                    deps=[twv] + ctx.bank_deps(bank), inc=(kc == KC - 1))
            e = P.dve(lambda E, tt=tt, oc2=oc2, pbk=pbk: E.tensor_copy(
                Vtok[:, tt * 1024 + oc2 * 512: tt * 1024 + oc2 * 512 + 512],
                pbk[:]), deps=[mm])
            ctx.bank_use(bank, e)
    ctx.flush()
    ctx.pop()
    ctx.pop()

    ctx.push()
    Pt = ctx.bf(16 * 512)
    Dtb = ctx.bf(16 * 512)
    xio = ctx.f32(4 * 512)
    x_in_v = x_in.rearrange("(c p) t -> c p t", p=128)
    x_out_v = x_out.rearrange("(c p) t -> c p t", p=128)
    Dt_v = ctx.w["L1_Dt"].rearrange("(kt p) q -> p kt q", p=128)
    for tqc in range(NT):
        ts = tqc * 512
        nkt = (ts + 512) // 128
        td = P.loadc(Dtb.rearrange("p (kt q) -> p kt q", q=512)[:, 0:nkt, :],
                     Dt_v[:, 0:nkt, ts:ts + 512])
        pts = []
        for kt in range(nkt):
            bank = kt % 2
            pbk = ctx.pb[bank]
            mm = None
            for hc in range(KC):
                mm = P.pe(lambda E, kt=kt, hc=hc, ts=ts, pbk=pbk: E.matmul(
                    pbk[:], Kf[:, hc * T + kt * 128: hc * T + (kt + 1) * 128],
                    Qf[:, hc * T + ts: hc * T + ts + 512],
                    start=(hc == 0), stop=(hc == KC - 1)),
                    deps=ctx.bank_deps(bank), inc=(hc == KC - 1))
            e = P.dve(lambda E, kt=kt, pbk=pbk: E.tensor_mul(
                Pt[:, kt * 512:(kt + 1) * 512], pbk[:],
                Dtb[:, kt * 512:(kt + 1) * 512]), deps=[mm, td])
            ctx.bank_use(bank, e)
            pts.append(e)
        store_prev = {}
        for oc in range(KC):
            par = oc % 2
            bank = 2 + par
            pbk = ctx.pb[bank]
            xo = xio[:, par * 1024: par * 1024 + 512]
            xn = xio[:, par * 1024 + 512: par * 1024 + 1024]
            sprev = store_prev.get(par, (None, None))
            tx = P.load(xo, x_in_v[oc, :, ts:ts + 512], deps=[sprev[1]])
            mm = None
            for kt in range(nkt):
                mm = P.pe(lambda E, oc=oc, kt=kt, pbk=pbk: E.matmul(
                    pbk[:],
                    Vtok[:, kt * 1024 + oc * 128: kt * 1024 + oc * 128 + 128],
                    Pt[:, kt * 512:(kt + 1) * 512],
                    start=(kt == 0), stop=(kt == nkt - 1)),
                    deps=pts + ctx.bank_deps(bank), inc=(kt == nkt - 1))
            tn = P.dve(lambda E, pbk=pbk, xo=xo, xn=xn: E.tensor_add(
                xn, pbk[:], xo), deps=[mm, tx, sprev[0]])
            ctx.bank_use(bank, tn)
            st = P.load(x_out_v[oc, :, ts:ts + 512], xn, deps=[tn])
            store_prev[par] = (st, tn)
        ctx.flush()
    ctx.pop()
    ctx.pop()



def emit_round(ctx, dst_bf, src_f32, itmp, ctmp, deps=()):
    P = ctx.P
    t1 = P.dve(lambda E: E.tensor_copy(itmp, src_f32), deps=deps)
    t2 = P.dve(lambda E: E.tensor_copy(ctmp, itmp), deps=[t1])
    t3 = P.dve(lambda E: E.tensor_sub(src_f32, src_f32, ctmp), deps=[t2])
    t4 = P.dve(lambda E: E.tensor_single_scalar(dst_bf, src_f32, 0.5,
                                                ALU.is_ge), deps=[t3])
    t5 = P.dve(lambda E: E.tensor_single_scalar(src_f32, src_f32, -0.5,
                                                ALU.is_lt), deps=[t4])
    t6 = P.dve(lambda E: E.tensor_sub(ctmp, ctmp, src_f32), deps=[t5])
    t7 = P.dve(lambda E: E.tensor_add(dst_bf, ctmp, dst_bf), deps=[t6])
    return t7


def emit_pmax(ctx, acc, ncols, out_pp, pbank_key, pbank, scr, deps=()):
    P = ctx.P
    t0 = P.dve(lambda E: E.tensor_copy(scr[0:64, 0:ncols], acc[64:128, :]),
               deps=deps)
    t1 = P.dve(lambda E: E.tensor_max(acc[0:64, :], acc[0:64, :],
                                      scr[0:64, 0:ncols]), deps=[t0])
    t1 = P.dve(lambda E: E.tensor_copy(scr[0:32, 0:ncols], acc[32:64, :]),
               deps=[t1])
    t2 = P.dve(lambda E: E.tensor_max(acc[0:32, :], acc[0:32, :],
                                      scr[0:32, 0:ncols]), deps=[t1])
    toks = []
    nt = (ncols + 127) // 128
    for i in range(nt):
        cw = min(128, ncols - i * 128)
        mm = P.pe(lambda E, i=i, cw=cw: E.transpose(
            pbank[0:cw, 0:32], acc[0:32, i * 128:i * 128 + cw],
            ctx.identf[0:32, 0:32]), deps=[t2] + ctx.bank_deps(pbank_key),
            inc=1)
        e = P.dve(lambda E, i=i, cw=cw: E.reduce_max(
            out_pp[0:cw, i:i + 1], pbank[0:cw, 0:32], axis=AX.X), deps=[mm])
        ctx.bank_use(pbank_key, e)
        toks.append(e)
    return toks


def emit_pp_to_row(ctx, pp, nt, row, scr_dram, pbank_key, pbank, deps=(),
                   row_slices=False):
    P = ctx.P
    mm = P.pe(lambda E: E.transpose(pbank[0:nt, 0:128], pp[:, 0:nt],
                                    ctx.identf[:]),
              deps=list(deps) + ctx.bank_deps(pbank_key), inc=1)
    st = ctx.f32(128, parts=nt)
    e = P.dve(lambda E: E.tensor_copy(st, pbank[0:nt, 0:128]), deps=[mm])
    ctx.bank_use(pbank_key, e)
    t1 = P.load(scr_dram[0:nt, :], st, deps=[e])
    if row_slices or row is None:
        return t1
    t2 = P.load(row, scr_dram.rearrange("a b -> (a b)")[None, 0:nt * 128],
                deps=[t1])
    return t2



def emit_l1f(ctx, x_in, x_out):
    P, w = ctx.P, ctx.w
    ctx.push()
    xln = ctx.bf(KC * T)
    xq = ctx.bf(KC * T)
    rhsq = ctx.bf(KC * 1024)
    sr_pp = ctx.f32(KC)
    atok = ctx.f32(16)
    scr_row = ctx.nc.dram_tensor("l1f_scr", [16, 128], F32, kind="Internal").ap()
    scr_row2 = ctx.nc.dram_tensor("l1f_scr2", [8, 128], F32,
                                  kind="Internal").ap()
    i_scr = ctx.i32t

    ctx.push()
    invr = ctx.f32(1024, parts=1)
    racc = ctx.f32(1024)
    rbuf = ctx.f32(1024)
    rnb = ctx.f32(1024)
    rv = w["L1_rhs"].rearrange("(kc p) o -> kc p o", p=128)
    tprev = None
    for kc in range(KC):
        tl = P.load(rbuf, rv[kc], deps=[tprev])
        t1 = P.dve(lambda E: E.tensor_scalar(rnb, rbuf, -1.0, None, ALU.mult),
                   deps=[tl])
        if kc == 0:
            t2 = P.dve(lambda E: E.tensor_max(racc, rbuf, rnb), deps=[t1])
        else:
            t2 = P.dve(lambda E: E.tensor_max(racc, racc, rbuf), deps=[t1])
            t2 = P.dve(lambda E: E.tensor_max(racc, racc, rnb), deps=[t2])
        tprev = t2
    emit_pmax(ctx, racc, 1024, sr_pp, ("t", 6), ctx.pb[6], rnb,
              deps=[tprev])
    trow = emit_pp_to_row(ctx, sr_pp, KC, rbuf[0:1, 0:1024], scr_row2,
                          ("t", 6), ctx.pb[6])
    t1 = P.dve(lambda E: E.tensor_single_scalar(rbuf[0:1, 0:1024],
                                                rbuf[0:1, 0:1024], 1e-12,
                                                ALU.max), deps=[trow])
    t2 = P.dve(lambda E: E.reciprocal(rnb[0:1, 0:1024], rbuf[0:1, 0:1024]),
               deps=[t1])
    t3 = P.dve(lambda E: E.tensor_scalar(invr, rnb[0:1, 0:1024], 127.0, None,
                                         ALU.mult), deps=[t2])
    t4 = P.dve(lambda E: E.tensor_single_scalar(sr_pp, sr_pp, 1e-12, ALU.max),
               deps=trow and [t3])
    t5 = P.dve(lambda E: E.tensor_scalar(sr_pp, sr_pp, 1.0 / 127.0, None,
                                         ALU.mult), deps=[t4])
    ctx.flush()
    irep = racc
    rtmp = rnb[:, 0:512]
    ctmp = rnb[:, 512:1024]
    reps = []
    for h in range(2):
        bank = 4 + h
        mm = P.pe(lambda E, h=h, bank=bank: E.matmul(
            ctx.pb[bank][:], ctx.ones_rowf[:], invr[:, h * 512:(h + 1) * 512],
            start=True, stop=True), deps=ctx.bank_deps(bank))
        e = P.dve(lambda E, h=h, bank=bank: E.tensor_copy(
            irep[:, h * 512:(h + 1) * 512], ctx.pb[bank][:]), deps=[mm])
        ctx.bank_use(bank, e)
        reps.append(e)
    tprev = None
    for kc in range(KC):
        tl = P.load(rbuf, rv[kc], deps=[tprev])
        for h in range(2):
            t1 = P.dve(lambda E, h=h: E.tensor_mul(
                rtmp, rbuf[:, h * 512:(h + 1) * 512],
                irep[:, h * 512:(h + 1) * 512]), deps=[tl] + reps)
            tprev = emit_round(
                ctx, rhsq[:, kc * 1024 + h * 512: kc * 1024 + h * 512 + 512],
                rtmp, i_scr[:, 0:512], ctmp)
    ctx.flush()
    if "rhsq" in _DBG_TAPS:
        dd = [(e, ctx.P.counts[e]) for e in ("dve", "act", "pe")]
        ctx.P.loadc(ctx.taps["rhsq"].rearrange("(kc p) o -> p kc o", p=128),
                    rhsq.rearrange("p (kc o) -> p kc o", kc=KC), deps=dd)
        ctx.P.load(ctx.taps["srp"], sr_pp, deps=dd)
        ctx.flush()
    ctx.pop()

    ctx.push()
    xbf = ctx.bf(KC * T)
    emit_ln(ctx, x_in, xln, xbf_scr=xbf, sq_scr=xq[:, 0:KC * T])
    ctx.pop()
    ctx.push()
    acc = ctx.f32(512)
    nbuf = ctx.f32(512)
    for tc in range(NT):
        ts = tc * 512
        tprev = None
        for c in range(KC):
            sl = xln[:, c * T + ts: c * T + ts + 512]
            t1 = P.dve(lambda E, sl=sl: E.tensor_scalar(
                nbuf, sl, -1.0, None, ALU.mult), deps=[tprev])
            if c == 0:
                tprev = P.dve(lambda E, sl=sl: E.tensor_max(acc, nbuf, nbuf),
                              deps=[t1])
                tprev = P.dve(lambda E, sl=sl: E.tensor_max(acc, acc, sl),
                              deps=[tprev])
            else:
                tprev = P.dve(lambda E, sl=sl: E.tensor_max(acc, acc, sl),
                              deps=[t1])
                tprev = P.dve(lambda E: E.tensor_max(acc, acc, nbuf),
                              deps=[tprev])
        emit_pmax(ctx, acc, 512, atok[:, tc * 4:(tc + 1) * 4], ("t", 6),
                  ctx.pb[6], nbuf, deps=[tprev])
    arow = ctx.f32(512, parts=1)
    tr = emit_pp_to_row(ctx, atok, 16, None, scr_row, ("t", 6), ctx.pb[6],
                        row_slices=True)
    ctx.flush()
    irep2 = ctx.f32(512)
    srep = ctx.f32(512)
    rtmp2 = ctx.f32(512)
    ctmp2 = ctx.f32(512)
    ibf = ctx.f32(512, parts=1)
    srow_v = scr_row.rearrange("a b -> (a b)")[None, :]
    prev_rd = None
    for tc in range(NT):
        ts = tc * 512
        ta = P.load(arow, srow_v[:, ts:ts + 512], deps=[tr, prev_rd])
        t1 = P.dve(lambda E: E.tensor_single_scalar(arow, arow, 1e-12,
                                                    ALU.max), deps=[ta])
        t2 = P.dve(lambda E: E.reciprocal(srep[0:1, :], arow), deps=[t1])
        prev_rd = t2
        t3 = P.dve(lambda E: E.tensor_scalar(ibf, srep[0:1, :], 127.0, None,
                                             ALU.mult), deps=[t2])
        bank = 4 + tc % 2
        mm = P.pe(lambda E, bank=bank: E.matmul(
            ctx.pb[bank][:], ctx.ones_rowf[:], ibf, start=True, stop=True),
            deps=[t3] + ctx.bank_deps(bank))
        e = P.dve(lambda E, bank=bank: E.tensor_copy(irep2, ctx.pb[bank][:]),
                  deps=[mm])
        ctx.bank_use(bank, e)
        if "irep" in _DBG_TAPS:
            e = P.load(ctx.taps["irep"][:, tc * 512:(tc + 1) * 512], irep2,
                       deps=[e])
        for c in range(KC):
            sl = xln[:, c * T + ts: c * T + ts + 512]
            t4 = P.dve(lambda E, sl=sl: E.tensor_mul(rtmp2, sl, irep2),
                       deps=[e])
            rdep = ()
            if "rt" in _DBG_TAPS and tc == 0 and c == 1:
                rdep = (P.load(ctx.taps["rt"], rtmp2, deps=[t4]),)
            emit_round(ctx, xq[:, c * T + ts: c * T + ts + 512], rtmp2,
                       i_scr[:, 0:512], ctmp2, deps=list(rdep))
    if "xq" in _DBG_TAPS:
        dd = [(e, ctx.P.counts[e]) for e in ("dve", "act", "pe")]
        ctx.P.loadc(ctx.taps["xq"].rearrange("(c p) t -> p c t", p=128),
                    xq.rearrange("p (c t) -> p c t", c=KC), deps=dd)
        if "xln" in _DBG_TAPS:
            ctx.P.loadc(ctx.taps["xln"].rearrange("(c p) t -> p c t", p=128),
                        xln.rearrange("p (c t) -> p c t", c=KC), deps=dd)
        ctx.P.load(ctx.taps["srow"],
                   scr_row.rearrange("a b -> (a b)")[None, :], deps=dd)
    ctx.flush()
    ctx.pop()

    ctx.push()
    xio = ctx.f32(4 * 512)
    srep2 = ctx.f32(512)
    otmp = ctx.f32(512)
    arow2 = ctx.f32(512, parts=1)
    sbf = ctx.f32(512, parts=1)
    x_in_v = x_in.rearrange("(c p) t -> c p t", p=128)
    x_out_v = x_out.rearrange("(c p) t -> c p t", p=128)
    prev_rd2 = None
    for tc in range(NT):
        ts = tc * 512
        ta = P.load(arow2, srow_v[:, ts:ts + 512], deps=[tr, prev_rd2])
        t1 = P.dve(lambda E: E.tensor_single_scalar(arow2, arow2, 1e-12,
                                                    ALU.max), deps=[ta])
        t2 = P.dve(lambda E: E.tensor_scalar(sbf, arow2, 1.0 / 127.0, None,
                                             ALU.mult), deps=[t1])
        prev_rd2 = t2
        bank = 4 + tc % 2
        mm = P.pe(lambda E, bank=bank: E.matmul(
            ctx.pb[bank][:], ctx.ones_rowf[:], sbf, start=True, stop=True),
            deps=[t2] + ctx.bank_deps(bank))
        es = P.dve(lambda E, bank=bank: E.tensor_copy(srep2, ctx.pb[bank][:]),
                   deps=[mm])
        ctx.bank_use(bank, es)
        store_prev = {}
        for oc in range(KC):
            par = oc % 2
            bank2 = par
            pbk = ctx.pb[bank2]
            xo = xio[:, par * 1024: par * 1024 + 512]
            xn = xio[:, par * 1024 + 512: par * 1024 + 1024]
            sprev = store_prev.get(par, (None, None))
            tx = P.load(xo, x_in_v[oc, :, ts:ts + 512], deps=[sprev[1]])
            mm2 = None
            for kc in range(KC):
                mm2 = P.pe(lambda E, oc=oc, kc=kc, ts=ts, pbk=pbk: E.matmul(
                    pbk[:],
                    rhsq[:, kc * 1024 + oc * 128: kc * 1024 + oc * 128 + 128],
                    xq[:, kc * T + ts: kc * T + ts + 512],
                    start=(kc == 0), stop=(kc == KC - 1)),
                    deps=ctx.bank_deps(bank2), inc=(kc == KC - 1))
            t5 = P.dve(lambda E, pbk=pbk: E.tensor_mul(otmp, pbk[:], srep2),
                       deps=[mm2, es])
            tn = P.dve(lambda E, oc=oc, xo=xo, xn=xn: E.scalar_tensor_tensor(
                xn, otmp, sr_pp[:, oc:oc + 1], xo, ALU.mult, ALU.add),
                deps=[t5, tx, sprev[0]])
            ctx.bank_use(bank2, t5)
            st = P.load(x_out_v[oc, :, ts:ts + 512], xn, deps=[tn])
            store_prev[par] = (st, tn)
        ctx.flush()
    ctx.pop()
    ctx.pop()



def emit_l2a(ctx, x_in, x_out):
    P, w = ctx.P, ctx.w
    NH, HS = 16, 64
    ctx.push()
    Qf = ctx.bf(KC * T)
    Kf = ctx.bf(KC * T)
    Vt = ctx.bf(16 * NH * 65)
    av = ctx.bf(KC * T)
    bpc = ctx.f32(24)
    bvr = ctx.bf(1024, parts=1)
    tb1 = P.load(bpc, w["L2_b_attn_pc"])
    tb2 = P.loadc(bvr, w["L2_b_attn"][:, 2048:3072])
    Vt3 = Vt.rearrange("p (tt h d) -> p (tt h) d", h=NH, d=65)
    tves = P.dve(lambda E: E.memset(Vt3[:, :, 64:65], 1.0))

    ctx.push()
    xln = ctx.bf(KC * T)
    emit_ln(ctx, x_in, xln, xbf_scr=Kf[:, 0:KC * T], sq_scr=Qf[:, 0:KC * T])

    def mk_ev(dst, boff):
        def ev(oc, tc, pbk, mm):
            return P.act(lambda E, oc=oc, tc=tc, pbk=pbk, dst=dst:
                         E.activation(
                dst[:, oc * T + tc * 512: oc * T + tc * 512 + 512], pbk[:],
                AF.Identity, bias=bpc[:, boff + oc: boff + oc + 1]),
                deps=[mm, tb1])
        return ev
    ctx.push()
    mm_fm(ctx, w["L2_w_attn"][:, 0:1024], D, D,
          lambda kc, tc: xln[:, kc * T + tc * 512: kc * T + tc * 512 + 512],
          mk_ev(Qf, 0))
    ctx.flush()
    ctx.pop()
    ctx.push()
    mm_fm(ctx, w["L2_w_attn"][:, 1024:2048], D, D,
          lambda kc, tc: xln[:, kc * T + tc * 512: kc * T + tc * 512 + 512],
          mk_ev(Kf, KC))
    ctx.flush()
    ctx.pop()

    ctx.push()
    wvf = ctx.bf(KC * 512)
    for oc2 in range(2):
        twv = P.loadc(
            wvf.rearrange("p (kc o) -> p kc o", kc=KC),
            w["L2_w_attn"][:, 2048 + oc2 * 512: 2048 + oc2 * 512 + 512]
            .rearrange("(kc p) o -> p kc o", p=128),
            deps=[P.last_tok["pe"]] if oc2 else ())
        for tt in range(16):
            bank = tt % 2
            pbk = ctx.pb[bank]
            mm = None
            for kc in range(KC):
                mm = P.pe(lambda E, tt=tt, oc2=oc2, kc=kc, pbk=pbk: E.matmul(
                    pbk[:], xln[:, kc * T + tt * 128: kc * T + (tt + 1) * 128],
                    wvf[:, kc * 512:(kc + 1) * 512],
                    start=(kc == 0), stop=False),
                    deps=[twv] + ctx.bank_deps(bank), inc=0)
            mm = P.pe(lambda E, tt=tt, oc2=oc2, pbk=pbk: E.matmul(
                pbk[:], ctx.ones_row[:],
                bvr[:, oc2 * 512:(oc2 + 1) * 512],
                start=False, stop=True), deps=[tb2], inc=1)
            dst = Vt3[:, tt * NH + oc2 * 8: tt * NH + oc2 * 8 + 8, 0:64]
            e = P.dve(lambda E, dst=dst, pbk=pbk: E.tensor_copy(
                dst, pbk[:].rearrange("p (h d) -> p h d", d=64)), deps=[mm,
                                                                       tves])
            ctx.bank_use(bank, e)
    ctx.flush()
    ctx.pop()
    ctx.pop()

    ctx.push()
    Pt = ctx.bf(16 * 512)
    rl = ctx.f32(512, parts=1)
    rlrep = ctx.f32(512, parts=64)
    xio = ctx.f32(4 * 512)
    x_in_v = x_in.rearrange("(c p) t -> c p t", p=128)
    x_out_v = x_out.rearrange("(c p) t -> c p t", p=128)
    for tqc in range(NT):
        ts = tqc * 512
        nkt = (ts + 512) // 128
        for h in range(NH):
            hb = (h % 2) * 64
            hc = h // 2
            pts = []
            for kt in range(nkt):
                bank = kt % 2
                pbk = ctx.pb[bank]
                mm = P.pe(lambda E, kt=kt, hb=hb, hc=hc, ts=ts, pbk=pbk:
                          E.matmul(
                    pbk[:],
                    Kf[hb:hb + 64, hc * T + kt * 128: hc * T + (kt + 1) * 128],
                    Qf[hb:hb + 64, hc * T + ts: hc * T + ts + 512],
                    start=True, stop=True), deps=ctx.bank_deps(bank), inc=1)
                e = P.act(lambda E, kt=kt, pbk=pbk: E.activation(
                    Pt[:, kt * 512:(kt + 1) * 512], pbk[:], AF.Exp,
                    scale=0.125, bias=ctx.cst(-12.0)), deps=[mm])
                ctx.bank_use(bank, e)
                if kt * 128 + 127 >= ts:
                    e = P.gp(lambda E, kt=kt, ts=ts: E.affine_select(
                        Pt[:, kt * 512:(kt + 1) * 512],
                        Pt[:, kt * 512:(kt + 1) * 512],
                        pattern=[[1, 512]], compare_op=ALU.is_ge, fill=0.0,
                        base=(ts - kt * 128), channel_multiplier=-1),
                        deps=[e])
                pts.append(e)
            bank = 2
            pbk = ctx.pb[bank]
            mm = None
            for kt in range(nkt):
                mm = P.pe(lambda E, kt=kt, h=h, pbk=pbk: E.matmul(
                    pbk[0:65, :],
                    Vt3[:, kt * NH + h: kt * NH + h + 1, :]
                    .rearrange("p a d -> p (a d)"),
                    Pt[:, kt * 512:(kt + 1) * 512],
                    start=(kt == 0), stop=(kt == nkt - 1)),
                    deps=pts + ctx.bank_deps(bank), inc=(kt == nkt - 1))
            t1 = P.dve(lambda E, pbk=pbk: E.reciprocal(rl, pbk[64:65, :]),
                       deps=[mm])
            bank2 = 3
            mm2 = P.pe(lambda E, bank2=bank2: E.matmul(
                ctx.pb[bank2][0:64, :], ctx.ones_rowf[0:1, 0:64], rl,
                start=True, stop=True), deps=[t1] + ctx.bank_deps(bank2),
                inc=1)
            t2 = P.dve(lambda E, bank2=bank2: E.tensor_copy(
                rlrep, ctx.pb[bank2][0:64, :]), deps=[mm2])
            ctx.bank_use(bank2, t2)
            dst = av[hb:hb + 64, hc * T + ts: hc * T + ts + 512]
            t3 = P.dve(lambda E, dst=dst, pbk=pbk: E.tensor_mul(
                dst, pbk[0:64, :], rlrep), deps=[t2])
            ctx.bank_use(bank, t3)
        ctx.flush()

    store_prev = {}

    def ev_out(oc, tc, pbk, mm):
        par = oc % 2
        xo = xio[:, par * 1024: par * 1024 + 512]
        xn = xio[:, par * 1024 + 512: par * 1024 + 1024]
        sprev = store_prev.get(par, (None, None))
        tx = P.load(xo, x_in_v[oc, :, tc * 512:(tc + 1) * 512],
                    deps=[sprev[1]])
        tt_ = P.dve(lambda E, oc=oc, pbk=pbk, xo=xo, xn=xn:
                    E.scalar_tensor_tensor(
            xn, pbk[:], bpc[:, 16 + oc: 16 + oc + 1], xo, ALU.add, ALU.add),
            deps=[mm, tx, sprev[0], tb1])
        st = P.load(x_out_v[oc, :, tc * 512:(tc + 1) * 512], xn, deps=[tt_])
        store_prev[par] = (st, tt_)
        return tt_
    mm_fm(ctx, w["L2_w_proj"], D, D,
          lambda kc, tc: av[:, kc * T + tc * 512: kc * T + tc * 512 + 512],
          ev_out, banks=(4, 5))
    ctx.flush()
    ctx.pop()
    ctx.pop()



def emit_l3a(ctx, x_in, x_out):
    P, w = ctx.P, ctx.w
    ctx.push()
    gz = ctx.bf(KC * T)
    sDl = ctx.bf(T, parts=16)
    sB = ctx.bf(T, parts=16)
    sC = ctx.bf(T, parts=16)
    A_sb = ctx.f32(16, parts=16)
    hl = ctx.bf(16, parts=16)
    bpc = ctx.f32(12)
    tb = P.load(bpc[:, 0:9], w["L3_b_in2"])
    tb = P.load(bpc[0:16, 9:12], w["L3_b_tail"], deps=[tb])
    ta = P.load(A_sb[0:16, 0:1], w["L3_A_log"])
    ta = P.act(lambda E: E.activation(A_sb[0:16, 0:1], A_sb[0:16, 0:1],
                                      AF.Exp), deps=[ta])
    ta = P.dve(lambda E: E.tensor_scalar(A_sb[0:16, 0:1], A_sb[0:16, 0:1],
                                         -1.0, None, ALU.mult), deps=[ta])

    ctx.push()
    xln = ctx.bf(KC * T)
    ctx.push()
    xbf = ctx.bf(KC * T)
    emit_ln(ctx, x_in, xln, xbf_scr=xbf, sq_scr=gz[:, 0:KC * T])
    ctx.pop()

    def ev_z(oc, tc, pbk, mm):
        return P.act(lambda E, oc=oc, tc=tc, pbk=pbk: E.activation(
            gz[:, oc * T + tc * 512: oc * T + tc * 512 + 512], pbk[:],
            AF.Gelu, bias=bpc[:, oc:oc + 1]), deps=[mm, tb])
    ctx.push()
    mm_fm(ctx, w["L3_W_in2"][:, 0:1024], D, D,
          lambda kc, tc: xln[:, kc * T + tc * 512: kc * T + tc * 512 + 512],
          ev_z)
    ctx.flush()
    ctx.pop()

    ctx.push()
    wtl = ctx.bf(KC * 96)
    u1 = ctx.f32(512, parts=16)
    u3 = ctx.f32(512, parts=16)
    rr = ctx.f32(512, parts=16)
    twl = P.loadc(wtl.rearrange("p (kc o) -> p kc o", kc=KC),
                  w["L3_W_in2"][:, 1024:1120].rearrange("(kc p) o -> p kc o",
                                                        p=128))
    prev_add = None
    for tc in range(NT):
        bank = 4 + tc % 2
        pbk = ctx.pb[bank]
        mm = None
        for kc in range(KC):
            mm = P.pe(lambda E, kc=kc, tc=tc, pbk=pbk: E.matmul(
                pbk[0:96, :], wtl[:, kc * 96:(kc + 1) * 96],
                xln[:, kc * T + tc * 512: kc * T + tc * 512 + 512],
                start=(kc == 0), stop=(kc == KC - 1)),
                deps=[twl] + ctx.bank_deps(bank), inc=(kc == KC - 1))
        a1 = P.act(lambda E, pbk=pbk: E.activation(
            u1[0:16, :], pbk[0:16, :], AF.Abs, bias=bpc[0:16, 9:10]),
            deps=[mm, tb, prev_add])
        a2 = P.act(lambda E: E.activation(u1[0:16, :], u1[0:16, :], AF.Exp,
                                          scale=-1.0), deps=[a1])
        a3 = P.act(lambda E: E.activation(u3[0:16, :], u1[0:16, :], AF.Ln,
                                          bias=1.0), deps=[a2])
        d1 = P.dve(lambda E, pbk=pbk: E.tensor_scalar(
            rr[0:16, :], pbk[0:16, :], bpc[0:16, 9:10], None, ALU.add),
            deps=[mm, tb, a1])
        d1 = P.dve(lambda E: E.tensor_single_scalar(rr[0:16, :], rr[0:16, :],
                                                    0.0, ALU.max), deps=[d1])
        prev_add = P.dve(lambda E, tc=tc: E.tensor_add(
            sDl[0:16, tc * 512:(tc + 1) * 512], rr[0:16, :], u3[0:16, :]),
            deps=[d1, a3])
        eB = P.dve(lambda E, tc=tc, pbk=pbk: E.tensor_scalar(
            sB[0:16, tc * 512:(tc + 1) * 512], pbk[32:48, :],
            bpc[0:16, 10:11], None, ALU.add), deps=[mm, tb, prev_add])
        eC = P.dve(lambda E, tc=tc, pbk=pbk: E.tensor_scalar(
            sC[0:16, tc * 512:(tc + 1) * 512], pbk[64:80, :],
            bpc[0:16, 11:12], None, ALU.add), deps=[eB])
        ctx.bank_use(bank, prev_add)
        ctx.bank_use_add(bank, a1)
        ctx.bank_use_add(bank, eC)
    ctx.flush()
    ctx.pop()
    ctx.pop()

    ctx.push()
    dA = ctx.f32(512, parts=16)
    hb = ctx.bf(512, parts=16)
    gg = ctx.bf(512, parts=16)
    yrow = ctx.f32(512, parts=1)
    yrep = ctx.f32(512)
    yz = ctx.bf(KC * 512)
    xio = ctx.f32(4 * 512)
    x_in_v = x_in.rearrange("(c p) t -> c p t", p=128)
    x_out_v = x_out.rearrange("(c p) t -> c p t", p=128)
    bout = ctx.f32(KC)
    tbo = P.load(bout, w["L3_b_out_pc"])
    for tc in range(NT):
        ts = tc * 512
        tda = P.act(lambda E, ts=ts: E.activation(
            dA[0:16, :], sDl[0:16, ts:ts + 512], AF.Exp,
            scale=A_sb[0:16, 0:1]), deps=[ta])
        init = 0.0 if tc == 0 else hl[0:16, 0:1]
        th = P.dve(lambda E, ts=ts, init=init: E.tensor_tensor_scan(
            hb[0:16, :], dA[0:16, :], sB[0:16, ts:ts + 512], init,
            ALU.mult, ALU.add), deps=[tda])
        th2 = P.dve(lambda E: E.tensor_copy(hl[0:16, 0:1], hb[0:16, 511:512]),
                    deps=[th])
        tg = P.dve(lambda E, ts=ts: E.tensor_mul(
            gg[0:16, :], hb[0:16, :], sC[0:16, ts:ts + 512]), deps=[th2])
        bank = 2
        mm = P.pe(lambda E: E.matmul(ctx.pb[bank][0:1, :], ctx.ones_col[0:16, :],
                                     gg[0:16, :], start=True, stop=True),
                  deps=[tg] + ctx.bank_deps(bank), inc=1)
        ty = P.dve(lambda E: E.tensor_copy(yrow, ctx.pb[bank][0:1, :]),
                   deps=[mm])
        ctx.bank_use(bank, ty)
        bank2 = 3
        mm2 = P.pe(lambda E: E.matmul(ctx.pb[bank2][:], ctx.ones_rowf[:],
                                      yrow, start=True, stop=True),
                   deps=[ty] + ctx.bank_deps(bank2), inc=1)
        tyr = P.dve(lambda E: E.tensor_copy(yrep, ctx.pb[bank2][:]),
                    deps=[mm2])
        ctx.bank_use(bank2, tyr)
        tzs = []
        for c in range(KC):
            tzs.append(P.dve(lambda E, c=c, ts=ts: E.tensor_mul(
                yz[:, c * 512:(c + 1) * 512],
                gz[:, c * T + ts: c * T + ts + 512], yrep), deps=[tyr]))
        store_prev = {}

        def ev_out(oc, tcc, pbk, mm, ts=ts, store_prev=store_prev):
            par = oc % 2
            xo = xio[:, par * 1024: par * 1024 + 512]
            xn = xio[:, par * 1024 + 512: par * 1024 + 1024]
            sprev = store_prev.get(par, (None, None))
            tx = P.load(xo, x_in_v[oc, :, ts:ts + 512], deps=[sprev[1]])
            tt_ = P.dve(lambda E, oc=oc, pbk=pbk, xo=xo, xn=xn:
                        E.scalar_tensor_tensor(
                xn, pbk[:], bout[:, oc:oc + 1], xo, ALU.add, ALU.add),
                deps=[mm, tx, sprev[0], tbo])
            st = P.load(x_out_v[oc, :, ts:ts + 512], xn, deps=[tt_])
            store_prev[par] = (st, tt_)
            return tt_
        ctx.push()
        mm_fm(ctx, w["L3_W_out"], D, D,
              lambda kc, tcc: yz[:, kc * 512:(kc + 1) * 512], ev_out,
              banks=(0, 1), t_lo=0, t_hi=1, extra_mm_deps=tzs)
        ctx.flush()
        ctx.pop()
    ctx.pop()
    ctx.pop()



def _bcast_col(ctx, val11, bank, deps=()):
    P = ctx.P
    col = ctx.f32(8)
    mm = P.pe(lambda E: E.matmul(ctx.pb[bank][:, 0:1], ctx.ones_rowf[:],
                                 val11, start=True, stop=True),
              deps=list(deps) + ctx.bank_deps(bank), inc=1)
    e = P.dve(lambda E: E.tensor_copy(col[:, 0:1], ctx.pb[bank][:, 0:1]),
              deps=[mm])
    ctx.bank_use(bank, e)
    return col[:, 0:1], e


def _global_amax_tile(ctx, acc, ncols, out11_dram, scr, deps=()):
    P = ctx.P
    a1 = ctx.f32(8)
    t1 = P.dve(lambda E: E.reduce_max(a1[:, 0:1], acc[:, 0:ncols], axis=AX.X),
               deps=deps)
    pp = ctx.f32(8)
    emit_pmax(ctx, a1[:, 0:1], 1, pp[:, 0:1], ("t", 6), ctx.pb[6], scr,
              deps=[t1])
    return P.load(out11_dram, pp[0:1, 0:1])


def emit_l3f1(ctx, x_in, xln8_out, lamax_out):
    P = ctx.P
    ctx.push()
    xln = ctx.bf(KC * T)
    ctx.push()
    xbf = ctx.bf(KC * T)
    sq = ctx.bf(KC * T)
    emit_ln(ctx, x_in, xln, xbf_scr=xbf, sq_scr=sq, f32_out_dram=xln8_out)
    ctx.pop()
    fb = ctx.f32(512)
    acc = ctx.f32(512)
    nb = ctx.f32(512)
    xv = xln8_out.rearrange("(c p) t -> c p t", p=128)
    tprev = None
    first = True
    for c in range(KC):
        for tc in range(NT):
            tl = P.load(fb, xv[c][:, tc * 512:(tc + 1) * 512], deps=[tprev])
            t1 = P.dve(lambda E: E.tensor_scalar(nb, fb, -1.0, None,
                                                 ALU.mult), deps=[tl])
            if first:
                t2 = P.dve(lambda E: E.tensor_max(acc, fb, nb), deps=[t1])
                first = False
            else:
                t2 = P.dve(lambda E: E.tensor_max(acc, acc, fb), deps=[t1])
                t2 = P.dve(lambda E: E.tensor_max(acc, acc, nb), deps=[t2])
            tprev = t2
    _global_amax_tile(ctx, acc, 512, lamax_out, nb, deps=[tprev])
    ctx.flush()
    ctx.pop()


def _quant_weight(ctx, wT_dram, K, O, wq, inv_col, deps=()):
    P = ctx.P
    KCd = K // 128
    nOC = O // 1024
    wv = wT_dram.rearrange("(kc p) o -> kc p o", p=128)
    rt = ctx.f32(512)
    ct = ctx.f32(512)
    fb = ctx.f32(1024)
    tprev = None
    for kc in range(KCd):
        for o4 in range(nOC):
            tl = P.load(fb, wv[kc][:, o4 * 1024:(o4 + 1) * 1024],
                        deps=[tprev] + list(deps))
            for h in range(2):
                t1 = P.dve(lambda E, h=h: E.tensor_scalar(
                    rt, fb[:, h * 512:(h + 1) * 512], inv_col, None,
                    ALU.mult), deps=[tl])
                tprev = emit_round(
                    ctx, wq[:, kc * O + o4 * 1024 + h * 512:
                            kc * O + o4 * 1024 + h * 512 + 512],
                    rt, ctx.i32t[:, 0:512], ct)
    return tprev


def _amax_weight(ctx, wT_dram, K, O, out11_dram, deps=()):
    P = ctx.P
    KCd = K // 128
    nOC = O // 1024
    wv = wT_dram.rearrange("(kc p) o -> kc p o", p=128)
    fb = ctx.f32(1024)
    nb = ctx.f32(1024)
    acc = ctx.f32(1024)
    tprev = None
    first = True
    for kc in range(KCd):
        for o4 in range(nOC):
            tl = P.load(fb, wv[kc][:, o4 * 1024:(o4 + 1) * 1024],
                        deps=[tprev] + list(deps))
            t1 = P.dve(lambda E: E.tensor_scalar(nb, fb, -1.0, None,
                                                 ALU.mult), deps=[tl])
            if first:
                t2 = P.dve(lambda E: E.tensor_max(acc, fb, nb), deps=[t1])
                first = False
            else:
                t2 = P.dve(lambda E: E.tensor_max(acc, acc, fb), deps=[t1])
                t2 = P.dve(lambda E: E.tensor_max(acc, acc, nb), deps=[t2])
            tprev = t2
    tk = _global_amax_tile(ctx, acc, 1024, out11_dram, nb, deps=[tprev])
    return tk


def _scale_from_amax8(ctx, amax8_dram, scr11_dram, minr):
    P = ctx.P
    row = ctx.f32(16, parts=1)
    t1 = P.load(row[:, 0:8], amax8_dram)
    t2 = P.dve(lambda E: E.reduce_max(row[:, 8:9], row[:, 0:8], axis=AX.X),
               deps=[t1])
    t2 = P.dve(lambda E: E.tensor_single_scalar(row[:, 8:9], row[:, 8:9],
                                                minr, ALU.max), deps=[t2])
    t3 = P.dve(lambda E: E.tensor_scalar(row[:, 9:10], row[:, 8:9],
                                         1.0 / 127.0, None, ALU.mult),
               deps=[t2])
    t4 = P.dve(lambda E: E.reciprocal(row[:, 10:11], row[:, 9:10]), deps=[t3])
    scol, e1 = _bcast_col(ctx, row[0:1, 9:10], 4, deps=[t4])
    icol, e2 = _bcast_col(ctx, row[0:1, 10:11], 5, deps=[t4, e1])
    return scol, icol, [e1, e2]


def _scale_from_11(ctx, val11_tok, minr):
    P = ctx.P
    val, tok = val11_tok
    r = ctx.f32(8, parts=1)
    t1 = P.dve(lambda E: E.tensor_single_scalar(r[:, 0:1], val, minr,
                                                ALU.max), deps=[tok])
    t2 = P.dve(lambda E: E.tensor_scalar(r[:, 1:2], r[:, 0:1], 1.0 / 127.0,
                                         None, ALU.mult), deps=[t1])
    t3 = P.dve(lambda E: E.reciprocal(r[:, 2:3], r[:, 1:2]), deps=[t2])
    return r[:, 1:2], r[:, 2:3], t3


def emit_l3f2(ctx, dram, outs):
    P, w = ctx.P, ctx.w
    HC = 32
    ctx.push()
    w1q = ctx.bf(KC * 4096)
    xq = ctx.bf(KC * T)
    b1s = ctx.f32(HC)
    tb = P.load(b1s, w["L3f_b1_pc"])
    scr_w = ctx.nc.dram_tensor("l3f_scrw", [1, 1], F32, kind="Internal").ap()
    ctx.push()
    tka = _amax_weight(ctx, w["L3f_w1T"], D, 4096, scr_w)
    ctx.flush()
    ctx.pop()
    wrow = ctx.f32(8, parts=1)
    tw1 = P.load(wrow[:, 0:1], scr_w, deps=[tka])
    sw11, iw11, twk = _scale_from_11(ctx, (wrow[:, 0:1], tw1), 1e-6)
    iwcol, ei = _bcast_col(ctx, iw11, 2, deps=[twk])
    swcol, ei2 = _bcast_col(ctx, sw11, 3, deps=[twk, ei])
    ctx.push()
    _quant_weight(ctx, w["L3f_w1T"], D, 4096, w1q, iwcol, deps=[ei])
    ctx.flush()
    ctx.pop()
    sxcol, ixcol, toks = _scale_from_amax8(ctx, dram["amax8"], None, 1e-6)
    swscol = ctx.f32(8)
    t5 = P.dve(lambda E: E.tensor_mul(swscol[:, 0:1], sxcol, swcol),
               deps=toks + [ei2])
    t6 = P.dve(lambda E: E.tensor_scalar(b1s, b1s, swscol[:, 0:1], None,
                                         ALU.mult), deps=[t5, tb])
    ctx.push()
    fb = ctx.f32(512)
    rt = ctx.f32(512)
    ct = ctx.f32(512)
    xv = dram["xln8"].rearrange("(c p) t -> c p t", p=128)
    tprev = None
    for c in range(KC):
        for tc in range(NT):
            tl = P.load(fb, xv[c][:, tc * 512:(tc + 1) * 512], deps=[tprev])
            t1 = P.dve(lambda E: E.tensor_scalar(rt, fb, ixcol, None,
                                                 ALU.mult),
                       deps=[tl] + toks)
            tprev = emit_round(ctx, xq[:, c * T + tc * 512:
                                       c * T + tc * 512 + 512],
                               rt, ctx.i32t[:, 0:512], ct)
    ctx.flush()
    ctx.pop()
    ctx.push()
    ht = ctx.f32(512)
    nb2 = ctx.f32(512)
    acc2 = ctx.f32(512)
    h_v = outs["h"].rearrange("(oc p) t -> oc p t", p=128)
    first = [True]
    aprev = None
    stprev = None
    for oc in range(HC):
        for tc in range(NT):
            bank = (oc * NT + tc) % 2
            pbk = ctx.pb[bank]
            mm = None
            for kc in range(KC):
                mm = P.pe(lambda E, oc=oc, kc=kc, tc=tc, pbk=pbk: E.matmul(
                    pbk[:],
                    w1q[:, kc * 4096 + oc * 128: kc * 4096 + oc * 128 + 128],
                    xq[:, kc * T + tc * 512: kc * T + tc * 512 + 512],
                    start=(kc == 0), stop=(kc == KC - 1)),
                    deps=ctx.bank_deps(bank), inc=(kc == KC - 1))
            e1 = P.act(lambda E, oc=oc, pbk=pbk: E.activation(
                ht, pbk[:], AF.Gelu, scale=swscol[:, 0:1],
                bias=b1s[:, oc:oc + 1]), deps=[mm, t6, aprev, stprev])
            ctx.bank_use(bank, e1)
            stprev = P.load(h_v[oc][:, tc * 512:(tc + 1) * 512], ht,
                            deps=[e1])
            t1 = P.dve(lambda E: E.tensor_scalar(nb2, ht, -1.0, None,
                                                 ALU.mult), deps=[e1])
            if first[0]:
                t2 = P.dve(lambda E: E.tensor_max(acc2, ht, nb2), deps=[t1])
                first[0] = False
            else:
                t2 = P.dve(lambda E: E.tensor_max(acc2, acc2, ht), deps=[t1])
                t2 = P.dve(lambda E: E.tensor_max(acc2, acc2, nb2),
                           deps=[t2])
            aprev = t2
    _global_amax_tile(ctx, acc2, 512, outs["lamax2"], nb2, deps=[aprev])
    ctx.flush()
    ctx.pop()
    ctx.pop()


def emit_l3f3(ctx, dram, outs):
    P, w = ctx.P, ctx.w
    HC = 32
    ctx.push()
    w2q = ctx.bf(HC * 1024)
    hq = ctx.bf(HC * 512)
    b2s = ctx.f32(KC)
    tb = P.load(b2s, w["L3f_b2_pc"])
    scr_w = ctx.nc.dram_tensor("l3f_scrw2", [1, 1], F32, kind="Internal").ap()
    ctx.push()
    tka = _amax_weight(ctx, w["L3f_w2T"], 4096, 1024, scr_w)
    ctx.flush()
    ctx.pop()
    wrow = ctx.f32(8, parts=1)
    tw1 = P.load(wrow[:, 0:1], scr_w, deps=[tka])
    sw11, iw11, twk = _scale_from_11(ctx, (wrow[:, 0:1], tw1), 1e-6)
    iwcol, ei = _bcast_col(ctx, iw11, 2, deps=[twk])
    swcol, ei2 = _bcast_col(ctx, sw11, 3, deps=[twk, ei])
    ctx.push()
    _quant_weight(ctx, w["L3f_w2T"], 4096, 1024, w2q, iwcol, deps=[ei])
    ctx.flush()
    ctx.pop()
    shcol, ihcol, toks = _scale_from_amax8(ctx, dram["hamax8"], None, 1e-6)
    swscol = ctx.f32(8)
    t5 = P.dve(lambda E: E.tensor_mul(swscol[:, 0:1], shcol, swcol),
               deps=toks + [ei2])
    t6 = P.dve(lambda E: E.tensor_scalar(b2s, b2s, swscol[:, 0:1], None,
                                         ALU.mult), deps=[t5, tb])
    ctx.push()
    fb = ctx.f32(512)
    rt = ctx.f32(512)
    ct = ctx.f32(512)
    otmp = ctx.f32(512)
    xio = ctx.f32(4 * 512)
    h_v = dram["h"].rearrange("(kc p) t -> kc p t", p=128)
    x_in_v = dram["x_st"].rearrange("(c p) t -> c p t", p=128)
    x_out_v = outs["x_out"].rearrange("(c p) t -> c p t", p=128)
    for tc in range(NT):
        ts = tc * 512
        tprev = None
        for kc in range(HC):
            tl = P.load(fb, h_v[kc][:, ts:ts + 512], deps=[tprev])
            t1 = P.dve(lambda E: E.tensor_scalar(rt, fb, ihcol, None,
                                                 ALU.mult),
                       deps=[tl] + toks)
            tprev = emit_round(ctx, hq[:, kc * 512:(kc + 1) * 512], rt,
                               ctx.i32t[:, 0:512], ct)
        store_prev = {}
        for oc in range(KC):
            par = oc % 2
            bank = par
            pbk = ctx.pb[bank]
            xo = xio[:, par * 1024: par * 1024 + 512]
            xn = xio[:, par * 1024 + 512: par * 1024 + 1024]
            sprev = store_prev.get(par, (None, None))
            tx = P.load(xo, x_in_v[oc, :, ts:ts + 512], deps=[sprev[1]])
            mm = None
            for kc in range(HC):
                mm = P.pe(lambda E, oc=oc, kc=kc, pbk=pbk: E.matmul(
                    pbk[:],
                    w2q[:, kc * 1024 + oc * 128: kc * 1024 + oc * 128 + 128],
                    hq[:, kc * 512:(kc + 1) * 512],
                    start=(kc == 0), stop=(kc == HC - 1)),
                    deps=[tprev] + ctx.bank_deps(bank), inc=(kc == HC - 1))
            t7 = P.dve(lambda E, pbk=pbk: E.tensor_scalar(
                otmp, pbk[:], swscol[:, 0:1], None, ALU.mult),
                deps=[mm, t6])
            tn = P.dve(lambda E, oc=oc, xo=xo, xn=xn: E.scalar_tensor_tensor(
                xn, otmp, b2s[:, oc:oc + 1], xo, ALU.add, ALU.add),
                deps=[t7, tx, sprev[0]])
            ctx.bank_use(bank, t7)
            st = P.load(x_out_v[oc, :, ts:ts + 512], xn, deps=[tn])
            store_prev[par] = (st, tn)
        ctx.flush()
    ctx.pop()
    ctx.pop()



def emit_l2f(ctx, x_in, x_out):
    P, w = ctx.P, ctx.w
    NH, HS, NS = 8, 64, 256
    ctx.push()
    qkvt = ctx.bf(16 * 1536)
    QK = ctx.bf(8 * T)
    Qf, Kf = QK[:, 0:4 * T], QK[:, 4 * T:8 * T]
    Vta = ctx.bf(16 * NH * 65)
    avd = ctx.bf(4 * T)
    msc = ctx.bf(4 * T)
    offi = ctx.i32t
    offd = ctx.nc.dram_tensor("l2f_offd", [8, T], F32, kind="Internal").ap()
    gated = ctx.nc.dram_tensor("l2f_gated", [8, T], F32, kind="Internal").ap()
    idxd = [ctx.nc.dram_tensor(f"l2f_idxd{i}", [NS + 1, 1], F32,
                               kind="Internal").ap() for i in range(NH)]
    xtokd = ctx.nc.dram_tensor("l2f_xtok", [T, D], BF16, kind="Internal").ap()
    Vt3 = Vta.rearrange("p (tt h d) -> p (tt h) d", h=NH, d=65)
    tves = P.dve(lambda E: E.memset(Vt3[:, :, 64:65], 1.0))

    ctx.push()
    xln = ctx.bf(KC * T)
    emit_ln(ctx, x_in, xln, xbf_scr=qkvt[:, 0:KC * T], sq_scr=QK[:, 0:KC * T])
    ctx.push()
    gtb = qkvt[0:8, 0:2048]
    selb = qkvt[0:8, 2048:4096]
    gates = qkvt[0:8, 4096:6144]
    offr = qkvt[0:8, 6144:8192]
    offtok = ctx.f32(16 * 8)
    lohi = ctx.f32(8, parts=8)
    postok = ctx.f32(16)
    Srow = ctx.f32(2048, parts=8)
    eqc = ctx.f32(2048, parts=8)

    ctx.push()
    rw = ctx.bf(KC * 8)
    stok = ctx.f32(8)
    trw = P.loadc(rw.rearrange("p (kc o) -> p kc o", kc=KC),
                  w["L2f_router"].rearrange("(kc p) o -> p kc o", p=128))
    for tt in range(16):
        bank = tt % 2
        pbk = ctx.pb[bank]
        mm = None
        for kc in range(KC):
            mm = P.pe(lambda E, tt=tt, kc=kc, pbk=pbk: E.matmul(
                pbk[:, 0:8],
                xln[:, kc * T + tt * 128: kc * T + (tt + 1) * 128],
                rw[:, kc * 8:(kc + 1) * 8], start=(kc == 0),
                stop=(kc == KC - 1)),
                deps=[trw] + ctx.bank_deps(bank), inc=(kc == KC - 1))
        e = P.dve(lambda E, pbk=pbk: E.tensor_copy(stok, pbk[:, 0:8]),
                  deps=[mm])
        ctx.bank_use(bank, e)
        mm2 = P.pe(lambda E: E.transpose(ctx.pb[2][0:8, 0:128], stok,
                                         ctx.identf[:]),
                   deps=[e] + ctx.bank_deps(2), inc=1)
        e2 = P.dve(lambda E, tt=tt: E.tensor_copy(
            Srow[0:8, tt * 128:(tt + 1) * 128], ctx.pb[2][0:8, 0:128]),
            deps=[mm2])
        ctx.bank_use(2, e2)
    ctx.flush()
    ctx.pop()

    lo, hi, mid, cnt, cc, sc1, sc2, need = (lohi[0:8, i:i + 1]
                                            for i in range(8))
    mkb = gtb
    o8 = _ones_row8(ctx)
    t1 = P.dve(lambda E: E.tensor_reduce(lo, Srow[0:8, :], AX.X, ALU.min))
    t1 = P.dve(lambda E: E.tensor_single_scalar(lo, lo, -1.0, ALU.add),
               deps=[t1])
    t1 = P.dve(lambda E: E.reduce_max(hi, Srow[0:8, :], axis=AX.X), deps=[t1])
    t1 = P.dve(lambda E: E.tensor_single_scalar(hi, hi, 1.0, ALU.add),
               deps=[t1])
    for it in range(72):
        t1 = P.dve(lambda E: E.tensor_add(mid, lo, hi), deps=[t1])
        t1 = P.dve(lambda E: E.tensor_single_scalar(mid, mid, 0.5, ALU.mult),
                   deps=[t1])
        t1 = P.dve(lambda E: E.tensor_scalar(
            mkb[0:8, :], Srow[0:8, :], mid, None, ALU.is_ge, ALU.add,
            accum_out=cnt), deps=[t1])
        t1 = P.dve(lambda E: E.tensor_single_scalar(cc, cnt, float(NS),
                                                    ALU.is_ge), deps=[t1])
        t1 = P.dve(lambda E: E.tensor_sub(sc1, mid, lo), deps=[t1])
        t1 = P.dve(lambda E: E.tensor_mul(sc1, sc1, cc), deps=[t1])
        t1 = P.dve(lambda E: E.tensor_add(lo, lo, sc1), deps=[t1])
        t1 = P.dve(lambda E: E.tensor_sub(sc2, hi, mid), deps=[t1])
        t1 = P.dve(lambda E: E.tensor_mul(sc2, sc2, cc), deps=[t1])
        t1 = P.dve(lambda E: E.tensor_add(hi, mid, sc2), deps=[t1])
    t1 = P.dve(lambda E: E.tensor_scalar(gtb[0:8, :], Srow[0:8, :], lo, None,
                                         ALU.is_gt, ALU.add,
                                         accum_out=cnt), deps=[t1])
    t1 = P.dve(lambda E: E.tensor_scalar(selb[0:8, :], Srow[0:8, :], lo, None,
                                         ALU.is_equal), deps=[t1])
    t1 = P.dve(lambda E: E.tensor_scalar(need, cnt, -1.0, float(NS), ALU.mult,
                                         ALU.add), deps=[t1])
    t1 = P.dve(lambda E, o8=o8: E.tensor_tensor_scan(eqc[0:8, :], o8[0:8, :],
                                                     selb[0:8, :], 0.0,
                                                     ALU.mult, ALU.add),
               deps=[t1])
    t1 = P.dve(lambda E: E.tensor_scalar(eqc[0:8, :], eqc[0:8, :], need, None,
                                         ALU.is_le), deps=[t1])
    t1 = P.dve(lambda E: E.tensor_mul(selb[0:8, :], selb[0:8, :], eqc[0:8, :]),
               deps=[t1])
    t1 = P.dve(lambda E: E.tensor_add(selb[0:8, :], gtb[0:8, :], selb[0:8, :]),
               deps=[t1])
    t1 = P.dve(lambda E, o8=o8: E.tensor_tensor_scan(eqc[0:8, :], o8[0:8, :],
                                                     selb[0:8, :], 0.0,
                                                     ALU.mult, ALU.add),
               deps=[t1])
    t1 = P.dve(lambda E: E.tensor_single_scalar(eqc[0:8, :], eqc[0:8, :],
                                                -1.0, ALU.add), deps=[t1])
    t1 = P.dve(lambda E: E.tensor_mul(eqc[0:8, :], eqc[0:8, :], selb[0:8, :]),
               deps=[t1])
    t1 = P.dve(lambda E: E.tensor_scalar(offr[0:8, :], selb[0:8, :], -256.0,
                                         256.0, ALU.mult, ALU.add), deps=[t1])
    t1 = P.dve(lambda E: E.tensor_add(offr[0:8, :], offr[0:8, :], eqc[0:8, :]),
               deps=[t1])
    tg = P.act(lambda E: E.activation(gates[0:8, :], Srow[0:8, :], AF.Sigmoid),
               deps=[t1])
    tg = P.dve(lambda E: E.tensor_mul(gates[0:8, :], gates[0:8, :],
                                      selb[0:8, :]), deps=[tg])
    st_off = P.loadc(offd, offr[0:8, :], deps=[t1])
    st_gat = P.loadc(gated, gates[0:8, :], deps=[tg])
    tio = P.gp(lambda E: E.iota(postok[:, 0:16], pattern=[[128, 16]], base=0,
                                channel_multiplier=1,
                                allow_small_or_imprecise_dtypes=True))
    for tt in range(16):
        mm = P.pe(lambda E, tt=tt: E.transpose(
            ctx.pt[0:128, 0:8], offr[0:8, tt * 128:(tt + 1) * 128],
            ctx.ident[0:8, 0:8]),
            deps=[t1] + ctx.bank_deps(("pt",)), inc=1)
        e = P.dve(lambda E, tt=tt: E.tensor_copy(
            offtok[:, tt * 8:(tt + 1) * 8], ctx.pt[0:128, 0:8]), deps=[mm])
        ctx.bank_use(("pt",), e)
        e2 = P.dve(lambda E, tt=tt: E.tensor_copy(
            offi[:, tt * 8:(tt + 1) * 8], offtok[:, tt * 8:(tt + 1) * 8]),
            deps=[e])
        for h in range(NH):
            P.gpdma(lambda E, tt=tt, h=h: E.indirect_dma_start(
                idxd[h], IndirectOffsetOnAxis(
                    offi[:, tt * 8 + h: tt * 8 + h + 1], 0),
                postok[:, tt:tt + 1], None), deps=[e2, tio])
    ctx.flush()
    ctx.pop()

    ctx.push()
    wq = ctx.bf(KC * 512)
    for oc3 in range(3):
        twq = P.loadc(
            wq.rearrange("p (kc o) -> p kc o", kc=KC),
            w["L2f_qkvT"][:, oc3 * 512:(oc3 + 1) * 512]
            .rearrange("(kc p) o -> p kc o", p=128),
            deps=[P.last_tok["pe"]] if oc3 else ())
        for tt in range(16):
            bank = tt % 2
            pbk = ctx.pb[bank]
            mm = None
            for kc in range(KC):
                mm = P.pe(lambda E, tt=tt, oc3=oc3, kc=kc, pbk=pbk: E.matmul(
                    pbk[:],
                    xln[:, kc * T + tt * 128: kc * T + (tt + 1) * 128],
                    wq[:, kc * 512:(kc + 1) * 512],
                    start=(kc == 0), stop=(kc == KC - 1)),
                    deps=[twq] + ctx.bank_deps(bank), inc=(kc == KC - 1))
            e = P.dve(lambda E, tt=tt, oc3=oc3, pbk=pbk: E.tensor_copy(
                qkvt[:, tt * 1536 + oc3 * 512: tt * 1536 + oc3 * 512 + 512],
                pbk[:]), deps=[mm])
            ctx.bank_use(bank, e)
    ctx.flush()
    ctx.pop()

    ctx.push()
    cs = ctx.f32(256)
    rp = ctx.f32(4 * 128)
    csv = w["L2f_costok"].rearrange("(tt p) k -> tt p k", p=128)
    prev = None
    for tt in range(16):
        tcs = P.load(cs, csv[tt], deps=[prev])
        qv3 = qkvt[:, tt * 1536:(tt + 1) * 1536].rearrange(
            "p (h o) -> p h o", o=192)
        cosv = cs[:, 0:128].rearrange("p (h k) -> p h k", k=16)
        sinv = cs[:, 128:256].rearrange("p (h k) -> p h k", k=16)
        for part in range(2):
            x1 = qv3[:, :, part * 64: part * 64 + 16]
            x2 = qv3[:, :, part * 64 + 16: part * 64 + 32]
            r1v = rp[:, 0:128].rearrange("p (h k) -> p h k", k=16)
            r2v = rp[:, 128:256].rearrange("p (h k) -> p h k", k=16)
            r3v = rp[:, 256:384].rearrange("p (h k) -> p h k", k=16)
            r4v = rp[:, 384:512].rearrange("p (h k) -> p h k", k=16)
            ta_ = P.dve(lambda E, x1=x1, cosv=cosv, r1v=r1v: E.tensor_mul(
                r1v, x1, cosv), deps=[tcs])
            ta_ = P.dve(lambda E, x2=x2, sinv=sinv, r2v=r2v: E.tensor_mul(
                r2v, x2, sinv), deps=[ta_])
            ta_ = P.dve(lambda E, x1=x1, sinv=sinv, r3v=r3v: E.tensor_mul(
                r3v, x1, sinv), deps=[ta_])
            ta_ = P.dve(lambda E, x2=x2, cosv=cosv, r4v=r4v: E.tensor_mul(
                r4v, x2, cosv), deps=[ta_])
            ta_ = P.dve(lambda E, x1=x1, r1v=r1v, r2v=r2v: E.tensor_sub(
                x1, r1v, r2v), deps=[ta_])
            prev = P.dve(lambda E, x2=x2, r3v=r3v, r4v=r4v: E.tensor_add(
                x2, r3v, r4v), deps=[ta_])
    ctx.flush()
    ctx.pop()

    ctx.push()
    xstage = ctx.bf(1024)
    qv4 = qkvt.rearrange("p (tt h o) -> p (tt h) o", h=NH, o=192)
    tprev = None
    for tt in range(16):
        for h in range(NH):
            for part, dst in ((0, Qf), (1, Kf)):
                mm = P.pe(lambda E, tt=tt, h=h, part=part: E.transpose(
                    ctx.pt[0:64, 0:128],
                    qv4[:, tt * NH + h, part * 64:(part + 1) * 64],
                    ctx.ident[:]),
                    deps=[tprev] + ctx.bank_deps(("pt",)), inc=1)
                tprev = P.dve(lambda E, tt=tt, h=h, dst=dst: E.tensor_copy(
                    dst[(h % 2) * 64:(h % 2) * 64 + 64,
                        (h // 2) * T + tt * 128:
                        (h // 2) * T + (tt + 1) * 128],
                    ctx.pt[0:64, 0:128]), deps=[mm])
                ctx.bank_use(("pt",), tprev)
        tprev = P.dve(lambda E, tt=tt: E.tensor_copy(
            Vt3[:, tt * NH:(tt + 1) * NH, 0:64],
            qv4[:, tt * NH:(tt + 1) * NH, 128:192]), deps=[tves, tprev])
    for tt in range(16):
        stw = None
        for kc in range(KC):
            mm = P.pe(lambda E, tt=tt, kc=kc: E.transpose(
                ctx.pt[:, 0:128],
                xln[:, kc * T + tt * 128: kc * T + (tt + 1) * 128],
                ctx.ident[:]), deps=[tprev] + ctx.bank_deps(("pt",)), inc=1)
            tprev = P.dve(lambda E, kc=kc: E.tensor_copy(
                xstage[:, kc * 128:(kc + 1) * 128], ctx.pt[:, 0:128]),
                deps=[mm, stw])
            ctx.bank_use(("pt",), tprev)
        stw = P.load(xtokd[tt * 128:(tt + 1) * 128, :], xstage, deps=[tprev])
        tprev = stw
    ctx.flush()
    ctx.pop()
    ctx.pop()

    ctx.push()
    Pt = ctx.bf(16 * 512)
    rl = ctx.f32(512, parts=1)
    rlrep = ctx.f32(512, parts=64)
    for tqc in range(NT):
        ts = tqc * 512
        nkt = (ts + 512) // 128
        for h in range(NH):
            hb = (h % 2) * 64
            hc = h // 2
            pts = []
            for kt in range(nkt):
                bank = kt % 2
                pbk = ctx.pb[bank]
                mm = P.pe(lambda E, kt=kt, hb=hb, hc=hc, ts=ts, pbk=pbk:
                          E.matmul(
                    pbk[:],
                    Kf[hb:hb + 64,
                       hc * T + kt * 128: hc * T + (kt + 1) * 128],
                    Qf[hb:hb + 64, hc * T + ts: hc * T + ts + 512],
                    start=True, stop=True), deps=ctx.bank_deps(bank), inc=1)
                e = P.act(lambda E, kt=kt, pbk=pbk: E.activation(
                    Pt[:, kt * 512:(kt + 1) * 512], pbk[:], AF.Exp,
                    scale=0.125, bias=ctx.cst(-12.0)), deps=[mm])
                ctx.bank_use(bank, e)
                if kt * 128 + 127 >= ts:
                    e = P.gp(lambda E, kt=kt, ts=ts: E.affine_select(
                        Pt[:, kt * 512:(kt + 1) * 512],
                        Pt[:, kt * 512:(kt + 1) * 512],
                        pattern=[[1, 512]], compare_op=ALU.is_ge, fill=0.0,
                        base=(ts - kt * 128), channel_multiplier=-1),
                        deps=[e])
                pts.append(e)
            bank = 2
            pbk = ctx.pb[bank]
            mm = None
            for kt in range(nkt):
                mm = P.pe(lambda E, kt=kt, h=h, pbk=pbk: E.matmul(
                    pbk[0:65, :],
                    Vt3[:, kt * NH + h: kt * NH + h + 1, :]
                    .rearrange("p a d -> p (a d)"),
                    Pt[:, kt * 512:(kt + 1) * 512],
                    start=(kt == 0), stop=(kt == nkt - 1)),
                    deps=pts + ctx.bank_deps(bank), inc=(kt == nkt - 1))
            t1_ = P.dve(lambda E, pbk=pbk: E.reciprocal(rl, pbk[64:65, :]),
                        deps=[mm])
            mm2 = P.pe(lambda E: E.matmul(
                ctx.pb[3][0:64, :], ctx.ones_rowf[0:1, 0:64], rl,
                start=True, stop=True), deps=[t1_] + ctx.bank_deps(3), inc=1)
            t2_ = P.dve(lambda E: E.tensor_copy(rlrep, ctx.pb[3][0:64, :]),
                        deps=[mm2])
            ctx.bank_use(3, t2_)
            dst = avd[hb:hb + 64, hc * T + ts: hc * T + ts + 512]
            t3_ = P.dve(lambda E, dst=dst, pbk=pbk: E.tensor_mul(
                dst, pbk[0:64, :], rlrep), deps=[t2_])
            ctx.bank_use(bank, t3_)
        ctx.flush()
    ctx.pop()

    ctx.push()
    Xgt = ctx.bf(2 * 1024)
    Xgf = ctx.bf(KC * 256)
    qkvg = ctx.bf(2 * 192)
    csg = ctx.f32(2 * 32)
    idxrow = ctx.f32(256, parts=1)
    idxvt2 = ctx.f32(2)
    idxg = ctx.f32(128, parts=2)
    qg = ctx.bf(256, parts=64)
    kg = ctx.bf(256, parts=64)
    vgt = ctx.bf(160)[:, 0:130]
    Ptg = ctx.bf(2 * 256)
    avm = ctx.bf(256, parts=64)
    avmT = ctx.bf(2 * 64)
    Gt = ctx.bf(2 * 512)
    orow = ctx.f32(512, parts=1)
    orep = ctx.f32(768)
    grow = ctx.f32(512, parts=1)
    grep = ctx.f32(512, parts=64)
    ropg = ctx.f32(64)
    iocol = ctx.f32(2)
    rlg = ctx.f32(256, parts=1)
    wqm = ctx.bf(KC * 192)
    tic = P.gp(lambda E: E.iota(iocol[:, 0:2], pattern=[[128, 2]], base=0,
                                channel_multiplier=1,
                                allow_small_or_imprecise_dtypes=True))
    vg3 = vgt.rearrange("p (g d) -> p g d", d=65)
    tvm = P.dve(lambda E: E.memset(vg3[:, :, 64:65], 1.0))
    prev_head = None
    for h in range(NH):
        tidx = P.load(idxg, idxd[h][0:NS].rearrange("(g p) o -> g (p o)",
                                                    p=128),
                      deps=[prev_head])
        mm = P.pe(lambda E: E.transpose(ctx.pb[4][0:128, 0:2], idxg,
                                        ctx.identf[0:2, 0:2]),
                  deps=[tidx] + ctx.bank_deps(4), inc=1)
        tvx = P.dve(lambda E: E.tensor_copy(idxvt2[:, 0:2],
                                            ctx.pb[4][0:128, 0:2]),
                    deps=[mm])
        ctx.bank_use(4, tvx)
        ti32 = P.dve(lambda E: E.tensor_copy(offi[:, 300:302], idxvt2[:, 0:2]),
                     deps=[tvx])
        g_toks = [ti32]
        for g in range(2):
            g_toks.append(P.gpdma(lambda E, g=g: E.indirect_dma_start(
                Xgt[:, g * 1024:(g + 1) * 1024], None, xtokd,
                IndirectOffsetOnAxis(offi[:, 300 + g: 301 + g], 0)),
                deps=[ti32]))
            g_toks.append(P.gpdma(lambda E, g=g: E.indirect_dma_start(
                csg.rearrange("p (g k) -> p g k", k=32)[:, g, :], None,
                w["L2f_csg"],
                IndirectOffsetOnAxis(offi[:, 300 + g: 301 + g], 0)),
                deps=[ti32]))
        tir = P.load(idxrow, idxd[h][0:NS].rearrange("a b -> (a b)")[None, :],
                     deps=[prev_head])
        tprev2 = prev_head
        for g in range(2):
            for kc in range(KC):
                mm = P.pe(lambda E, g=g, kc=kc: E.transpose(
                    ctx.pt[:, 0:128],
                    Xgt[:, g * 1024 + kc * 128: g * 1024 + (kc + 1) * 128],
                    ctx.ident[:]),
                    deps=g_toks + [tprev2] + ctx.bank_deps(("pt",)), inc=1)
                tprev2 = P.dve(lambda E, g=g, kc=kc: E.tensor_copy(
                    Xgf[:, kc * 256 + g * 128: kc * 256 + g * 128 + 128],
                    ctx.pt[:, 0:128]), deps=[mm])
                ctx.bank_use(("pt",), tprev2)
        twm = P.loadc(wqm.rearrange("p (kc o) -> p kc o", kc=KC),
                      w["L2f_wqkv_m"][:, h * 192:(h + 1) * 192]
                      .rearrange("(kc p) o -> p kc o", p=128),
                      deps=[prev_head])
        for g in range(2):
            bank = 4 + g
            pbk = ctx.pb[bank]
            mm = None
            for kc in range(KC):
                mm = P.pe(lambda E, g=g, kc=kc, pbk=pbk: E.matmul(
                    pbk[:, 0:192],
                    Xgf[:, kc * 256 + g * 128: kc * 256 + g * 128 + 128],
                    wqm[:, kc * 192:(kc + 1) * 192],
                    start=(kc == 0), stop=(kc == KC - 1)),
                    deps=[twm, tprev2] + ctx.bank_deps(bank),
                    inc=(kc == KC - 1))
            e = P.dve(lambda E, g=g, pbk=pbk: E.tensor_copy(
                qkvg[:, g * 192:(g + 1) * 192], pbk[:, 0:192]), deps=[mm])
            ctx.bank_use(bank, e)
        prevr = P.last_tok["dve"]
        for g in range(2):
            for part in range(2):
                x1 = qkvg[:, g * 192 + part * 64: g * 192 + part * 64 + 16]
                x2 = qkvg[:, g * 192 + part * 64 + 16:
                          g * 192 + part * 64 + 32]
                cosv = csg[:, g * 32: g * 32 + 16]
                sinv = csg[:, g * 32 + 16: g * 32 + 32]
                ta_ = P.dve(lambda E, x1=x1, cosv=cosv: E.tensor_mul(
                    ropg[:, 0:16], x1, cosv), deps=[prevr] + g_toks)
                ta_ = P.dve(lambda E, x2=x2, sinv=sinv: E.tensor_mul(
                    ropg[:, 16:32], x2, sinv), deps=[ta_])
                ta_ = P.dve(lambda E, x1=x1, sinv=sinv: E.tensor_mul(
                    ropg[:, 32:48], x1, sinv), deps=[ta_])
                ta_ = P.dve(lambda E, x2=x2, cosv=cosv: E.tensor_mul(
                    ropg[:, 48:64], x2, cosv), deps=[ta_])
                ta_ = P.dve(lambda E, x1=x1: E.tensor_sub(
                    x1, ropg[:, 0:16], ropg[:, 16:32]), deps=[ta_])
                prevr = P.dve(lambda E, x2=x2: E.tensor_add(
                    x2, ropg[:, 32:48], ropg[:, 48:64]), deps=[ta_])
        tprev3 = prevr
        for g in range(2):
            for part, dst in ((0, qg), (1, kg)):
                mm = P.pe(lambda E, g=g, part=part: E.transpose(
                    ctx.pt[0:64, 0:128],
                    qkvg[:, g * 192 + part * 64: g * 192 + (part + 1) * 64],
                    ctx.ident[:]),
                    deps=[tprev3] + ctx.bank_deps(("pt",)), inc=1)
                tprev3 = P.dve(lambda E, g=g, dst=dst: E.tensor_copy(
                    dst[0:64, g * 128:(g + 1) * 128], ctx.pt[0:64, 0:128]),
                    deps=[mm])
                ctx.bank_use(("pt",), tprev3)
            tprev3 = P.dve(lambda E, g=g: E.tensor_copy(
                vg3[:, g, 0:64], qkvg[:, g * 192 + 128: g * 192 + 192]),
                deps=[tprev3, tvm])
        mm = P.pe(lambda E: E.matmul(ctx.pb[6][:, 0:256], ctx.ones_rowf[:],
                                     idxrow, start=True, stop=True),
                  deps=[tir] + ctx.bank_deps(6), inc=1)
        tqr = P.dve(lambda E: E.tensor_copy(orep[:, 0:256],
                                            ctx.pb[6][:, 0:256]), deps=[mm])
        ctx.bank_use(6, tqr)
        spts = []
        for g in range(2):
            bank = g
            pbk = ctx.pb[bank]
            mm = P.pe(lambda E, g=g, pbk=pbk: E.matmul(
                pbk[:, 0:256], kg[0:64, g * 128:(g + 1) * 128],
                qg[0:64, 0:256], start=True, stop=True),
                deps=[tprev3] + ctx.bank_deps(bank), inc=1)
            e = P.act(lambda E, g=g, pbk=pbk: E.activation(
                Ptg[:, g * 256:(g + 1) * 256], pbk[:, 0:256], AF.Exp,
                scale=0.125, bias=ctx.cst(-12.0)), deps=[mm])
            ctx.bank_use(bank, e)
            e2 = P.dve(lambda E, g=g: E.tensor_scalar(
                orep[:, 256 + g * 256: 256 + (g + 1) * 256], orep[:, 0:256],
                idxvt2[:, g:g + 1], None, ALU.is_ge), deps=[tqr, tvx])
            spts.append(P.dve(lambda E, g=g: E.tensor_mul(
                Ptg[:, g * 256:(g + 1) * 256], Ptg[:, g * 256:(g + 1) * 256],
                orep[:, 256 + g * 256: 256 + (g + 1) * 256]), deps=[e, e2]))
        bank = 2
        pbk = ctx.pb[bank]
        mm = None
        for g in range(2):
            mm = P.pe(lambda E, g=g, pbk=pbk: E.matmul(
                pbk[0:65, 0:256], vg3[:, g, :],
                Ptg[:, g * 256:(g + 1) * 256],
                start=(g == 0), stop=(g == 1)),
                deps=spts + ctx.bank_deps(bank), inc=(g == 1))
        t1_ = P.dve(lambda E, pbk=pbk, rlg=rlg: E.reciprocal(
            rlg, pbk[64:65, 0:256]), deps=[mm])
        mm2 = P.pe(lambda E, rlg=rlg: E.matmul(ctx.pb[3][0:64, 0:256],
                                               ctx.ones_rowf[0:1, 0:64], rlg,
                                               start=True, stop=True),
                   deps=[t1_] + ctx.bank_deps(3), inc=1)
        t2_ = P.dve(lambda E: E.tensor_copy(grep[0:64, 0:256],
                                            ctx.pb[3][0:64, 0:256]),
                    deps=[mm2])
        ctx.bank_use(3, t2_)
        t3_ = P.dve(lambda E, pbk=pbk: E.tensor_mul(
            avm[0:64, :], pbk[0:64, 0:256], grep[0:64, 0:256]), deps=[t2_])
        ctx.bank_use(bank, t3_)
        tprev4 = t3_
        for g in range(2):
            mm = P.pe(lambda E, g=g: E.transpose(
                ctx.pt[:, 0:64], avm[0:64, g * 128:(g + 1) * 128],
                ctx.ident[0:64, 0:64]),
                deps=[tprev4] + ctx.bank_deps(("pt",)), inc=1)
            tprev4 = P.dve(lambda E, g=g: E.tensor_copy(
                avmT[:, g * 64:(g + 1) * 64], ctx.pt[:, 0:64]), deps=[mm])
            ctx.bank_use(("pt",), tprev4)
        hb2 = (h % 2) * 64
        hc2 = h // 2
        last_ms = None
        for tc in range(NT):
            ts2 = tc * 512
            tor = P.load(orow, offd[h][None, ts2:ts2 + 512],
                         deps=[st_off, last_ms])
            tgr = P.load(grow, gated[h][None, ts2:ts2 + 512],
                         deps=[st_gat, last_ms])
            mmo = P.pe(lambda E: E.matmul(
                ctx.pb[6][:], ctx.ones_rowf[:], orow, start=True, stop=True),
                deps=[tor] + ctx.bank_deps(6), inc=1)
            teo = P.dve(lambda E: E.tensor_copy(orep[:, 0:512],
                                                ctx.pb[6][:]), deps=[mmo])
            ctx.bank_use(6, teo)
            gts = []
            for g in range(2):
                gts.append(P.dve(lambda E, g=g: E.tensor_scalar(
                    Gt[:, g * 512:(g + 1) * 512], orep[:, 0:512],
                    iocol[:, g:g + 1], None, ALU.is_equal), deps=[teo, tic]))
            mmg = P.pe(lambda E: E.matmul(
                ctx.pb[5][0:64, :], ctx.ones_rowf[0:1, 0:64], grow,
                start=True, stop=True), deps=[tgr] + ctx.bank_deps(5), inc=1)
            teg = P.dve(lambda E: E.tensor_copy(grep[0:64, 0:512],
                                                ctx.pb[5][0:64, :]),
                        deps=[mmg])
            ctx.bank_use(5, teg)
            bank2 = 4
            pbk2 = ctx.pb[bank2]
            mm3 = None
            for g in range(2):
                mm3 = P.pe(lambda E, g=g, pbk2=pbk2: E.matmul(
                    pbk2[0:64, :], avmT[:, g * 64:(g + 1) * 64],
                    Gt[:, g * 512:(g + 1) * 512],
                    start=(g == 0), stop=(g == 1)),
                    deps=gts + [tprev4] + ctx.bank_deps(bank2),
                    inc=(g == 1))
            last_ms = P.dve(lambda E, hb2=hb2, hc2=hc2, ts2=ts2, pbk2=pbk2:
                            E.tensor_mul(
                msc[hb2:hb2 + 64, hc2 * T + ts2: hc2 * T + ts2 + 512],
                pbk2[0:64, :], grep[0:64, 0:512]), deps=[mm3, teg])
            ctx.bank_use(bank2, last_ms)
        prev_head = last_ms
    ctx.flush()
    ctx.pop()

    ctx.push()
    xio = ctx.f32(4 * 512)
    wod = ctx.bf(2 * 512)
    wom = ctx.bf(2 * 512)
    wodv = w["L2f_wo_d"].rearrange("(kc p) (oc j) -> oc p kc j", p=128, j=128)
    womv = w["L2f_wo_m"].rearrange("(kc p) (oc j) -> oc p kc j", p=128, j=128)
    wods = [wod[:, 0:512].rearrange("p (kc j) -> p kc j", j=128),
            wod[:, 512:1024].rearrange("p (kc j) -> p kc j", j=128)]
    woms = [wom[:, 0:512].rearrange("p (kc j) -> p kc j", j=128),
            wom[:, 512:1024].rearrange("p (kc j) -> p kc j", j=128)]
    twd = {0: P.loadc(wods[0], wodv[0]), 1: P.loadc(wods[1], wodv[1])}
    twm2 = {0: P.loadc(woms[0], womv[0]), 1: P.loadc(woms[1], womv[1])}
    x_in_v = x_in.rearrange("(c p) t -> c p t", p=128)
    x_out_v = x_out.rearrange("(c p) t -> c p t", p=128)
    store_prev = {}
    for oc in range(KC):
        par = oc % 2
        last_mm = None
        for tc in range(NT):
            ts2 = tc * 512
            bank = tc % 2
            pbk = ctx.pb[bank]
            mm = None
            for kc in range(4):
                mm = P.pe(lambda E, kc=kc, ts2=ts2, pbk=pbk, par=par:
                          E.matmul(
                    pbk[:], wods[par][:, kc, :],
                    avd[:, kc * T + ts2: kc * T + ts2 + 512],
                    start=(kc == 0), stop=False),
                    deps=[twd[oc]] + ctx.bank_deps(bank), inc=0)
            for kc in range(4):
                mm = P.pe(lambda E, kc=kc, ts2=ts2, pbk=pbk, par=par:
                          E.matmul(
                    pbk[:], woms[par][:, kc, :],
                    msc[:, kc * T + ts2: kc * T + ts2 + 512],
                    start=False, stop=(kc == 3)),
                    deps=[twm2[oc]], inc=(kc == 3))
            xo = xio[:, par * 1024: par * 1024 + 512]
            xn = xio[:, par * 1024 + 512: par * 1024 + 1024]
            sprev = store_prev.get(par, (None, None))
            tx = P.load(xo, x_in_v[oc, :, ts2:ts2 + 512], deps=[sprev[1]])
            tn = P.dve(lambda E, pbk=pbk, xo=xo, xn=xn: E.tensor_add(
                xn, pbk[:], xo), deps=[mm, tx, sprev[0]])
            ctx.bank_use(bank, tn)
            st = P.load(x_out_v[oc, :, ts2:ts2 + 512], xn, deps=[tn])
            store_prev[par] = (st, tn)
            last_mm = mm
        if oc + 2 < KC:
            twd[oc + 2] = P.loadc(wods[par], wodv[oc + 2], deps=[last_mm])
            twm2[oc + 2] = P.loadc(woms[par], womv[oc + 2], deps=[last_mm])
    ctx.flush()
    ctx.pop()
    ctx.pop()


def _ones_row8(ctx):
    if not hasattr(ctx, "_ones8"):
        t = ctx.nc.alloc_sbuf_tensor("c_ones8", [8, T], BF16)
        ctx._ones8 = t
        ctx.P.dve(lambda E: E.memset(t[:], 1.0))
    return ctx._ones8[:]





def mm_fm(ctx, wT_dram, K, O, rhs_fn, evac_fn, deps=(), banks=(0, 1),
          t_lo=0, t_hi=NT, extra_mm_deps=None):
    P = ctx.P
    KCd = K // 128
    OC = (O + 127) // 128
    wv4 = wT_dram.rearrange("(kc p) (oc j) -> oc p kc j", p=128, j=128)
    wv = [wv4[i] for i in range(OC)]
    wbuf = ctx.bf(2 * KCd * 128)
    wslot = [wbuf[:, 0:KCd * 128].rearrange("p (kc j) -> p kc j", j=128),
             wbuf[:, KCd * 128:2 * KCd * 128].rearrange("p (kc j) -> p kc j",
                                                        j=128)]
    tw = {0: P.loadc(wslot[0], wv[0], deps=deps)}
    if OC > 1:
        tw[1] = P.loadc(wslot[1], wv[1], deps=deps)
    toks = []
    for oc in range(OC):
        ow = min(128, O - oc * 128)
        last_mm = None
        for tc in range(t_lo, t_hi):
            bank = banks[tc % 2] if len(banks) == 2 else banks[0]
            pbk = ctx.pb[bank]
            mm = None
            for kc in range(KCd):
                mm = P.pe(lambda E, oc=oc, kc=kc, tc=tc, pbk=pbk, ow=ow:
                          E.matmul(
                    pbk[0:ow, :],
                    wbuf[:, (oc % 2) * KCd * 128 + kc * 128:
                         (oc % 2) * KCd * 128 + kc * 128 + ow],
                    rhs_fn(kc, tc), start=(kc == 0), stop=(kc == KCd - 1)),
                    deps=[tw[oc]] + ctx.bank_deps(bank) +
                         (extra_mm_deps or []), inc=(kc == KCd - 1))
            e = evac_fn(oc, tc, pbk, mm)
            ctx.bank_use(bank, e)
            toks.append(e)
            last_mm = mm
        if oc + 2 < OC:
            tw[oc + 2] = P.loadc(wslot[oc % 2], wv[oc + 2], deps=[last_mm])
    return toks


def _t(a):
    return np.ascontiguousarray(np.asarray(a).T)


def _prep_weights(inp):
    w = {}
    w["L0_w_inprojT"] = _t(inp["L0_attn_in_proj_w"])
    w["L0_conv_xw"] = np.ascontiguousarray(
        inp["L0_attn_conv_x_w"][:, 0, :].reshape(KC, 128, 4).transpose(1, 0, 2)
        .reshape(128, KC * 4))
    w["L0_conv_zw"] = np.ascontiguousarray(
        inp["L0_attn_conv_z_w"][:, 0, :].reshape(KC, 128, 4).transpose(1, 0, 2)
        .reshape(128, KC * 4))
    w["L0_w_xprojT"] = _t(inp["L0_attn_x_proj_w"])
    w["L0_w_dtprojT"] = _t(inp["L0_attn_dt_proj_w"])
    w["L0_dt_bias"] = np.ascontiguousarray(
        inp["L0_attn_dt_proj_b"].reshape(KC, 128).T)
    w["L0_A_log"] = np.ascontiguousarray(
        inp["L0_attn_A_log"].reshape(KC, 128, 16).transpose(1, 0, 2)
        .reshape(128, KC * 16))
    w["L0_D"] = np.ascontiguousarray(inp["L0_attn_D"].reshape(KC, 128).T)
    w["L0_w_outprojT"] = _t(inp["L0_attn_out_proj_w"])
    w["L0f_w1T"] = _t(inp["L0_ffn_w1_w"])
    w["L0f_w2T"] = _t(inp["L0_ffn_w2_w"])
    w["L0f_h1T"] = _t(inp["L0_ffn_h1_w"])
    w["L0f_h2T"] = _t(inp["L0_ffn_h2_w"])
    w["L0f_w3T"] = _t(inp["L0_ffn_w3_w"])
    w["L0f_s1"] = np.ascontiguousarray(inp["L0_ffn_w1_s"].reshape(32, 128).T)
    w["L0f_s2"] = np.ascontiguousarray(inp["L0_ffn_w2_s"].reshape(32, 128).T)
    w["L0f_sh1"] = np.ascontiguousarray(inp["L0_ffn_h1_s"].reshape(32, 128).T)
    w["L0f_sh2"] = np.ascontiguousarray(inp["L0_ffn_h2_s"].reshape(32, 128).T)
    w["L0f_s3"] = np.ascontiguousarray(inp["L0_ffn_w3_s"].reshape(8, 128).T)
    w["L1_wqT"] = _t(inp["L1_attn_Wq"])
    w["L1_wkT"] = _t(inp["L1_attn_Wk"])
    w["L1_wvT"] = _t(inp["L1_attn_Wv"])
    w["L1_bq"] = np.ascontiguousarray(inp["L1_attn_bq"].reshape(8, 128).T)
    w["L1_bk"] = np.ascontiguousarray(inp["L1_attn_bk"].reshape(8, 128).T)
    GAMMA = np.float64(0.96875)
    i = np.arange(T, dtype=np.float64)
    Dm = np.where(i[:, None] >= i[None, :], GAMMA ** (i[:, None] - i[None, :]),
                  0.0)
    w["L1_Dt"] = np.ascontiguousarray(Dm.T.astype(np.float32))
    w["L1_rhs"] = np.ascontiguousarray(inp["L1_ffn_rhs"])
    w["L2_w_attn"] = np.ascontiguousarray(inp["L2_attn_w_attn"])
    w["L2_b_attn"] = np.ascontiguousarray(inp["L2_attn_b_attn"])[None, :]
    w["L2_b_attn_pc"] = np.ascontiguousarray(
        inp["L2_attn_b_attn"][0:2048].reshape(16, 128).T)
    w["L2_b_attn_pc"] = np.concatenate(
        [w["L2_b_attn_pc"],
         inp["L2_attn_b_proj"].reshape(8, 128).T], axis=1)
    w["L2_w_proj"] = np.ascontiguousarray(inp["L2_attn_w_proj"])
    w["L2f_router"] = np.ascontiguousarray(inp["L2_ffn_mosa_router_w"])
    w["L2f_qkvT"] = np.ascontiguousarray(
        inp["L2_ffn_dense_wqkv"].reshape(1024, 1536))
    w["L2f_wqkv_m"] = np.ascontiguousarray(
        inp["L2_ffn_mosa_wqkv"].reshape(1024, 1536))
    w["L2f_wo_d"] = np.ascontiguousarray(inp["L2_ffn_dense_wo"])
    w["L2f_wo_m"] = np.ascontiguousarray(inp["L2_ffn_mosa_wo"])
    half = 16
    inv = (10000.0 ** (-np.arange(half, dtype=np.float64) / half))
    ang = np.arange(T, dtype=np.float64)[:, None] * inv[None, :]
    cosr = np.cos(ang).astype(np.float32)
    sinr = np.sin(ang).astype(np.float32)
    w["L2f_costok"] = np.ascontiguousarray(
        np.concatenate([np.tile(cosr, (1, 8)), np.tile(sinr, (1, 8))], 1))
    w["L2f_csg"] = np.ascontiguousarray(np.concatenate([cosr, sinr], 1))
    Wi = inp["L3_attn_W_in"]
    bi = inp["L3_attn_b_in"]
    W2 = np.zeros((1024, 1024 + 96), np.float32)
    W2[:, 0:1024] = Wi[:, 1024:2048]
    W2[:, 1024:1040] = Wi[:, 2080:2096]
    W2[:, 1056:1072] = Wi[:, 2048:2064]
    W2[:, 1088:1104] = Wi[:, 2064:2080]
    w["L3_W_in2"] = W2
    b2 = np.zeros((128, 9), np.float32)
    b2[:, 0:8] = bi[1024:2048].reshape(8, 128).T
    w["L3_b_in2"] = np.ascontiguousarray(b2)
    bt = np.zeros((16, 3), np.float32)
    bt[:, 0] = bi[2080:2096]
    bt[:, 1] = bi[2048:2064]
    bt[:, 2] = bi[2064:2080]
    w["L3_b_tail"] = bt
    w["L3_A_log"] = np.ascontiguousarray(inp["L3_attn_A_log"][:, None])
    w["L3_W_out"] = np.ascontiguousarray(inp["L3_attn_W_out"])
    w["L3_b_out_pc"] = np.ascontiguousarray(
        inp["L3_attn_b_out"].reshape(8, 128).T)
    w["L3f_w1T"] = _t(inp["L3_ffn_w1"])
    w["L3f_b1_pc"] = np.ascontiguousarray(inp["L3_ffn_b1"].reshape(32, 128).T)
    w["L3f_w2T"] = _t(inp["L3_ffn_w2"])
    w["L3f_b2_pc"] = np.ascontiguousarray(inp["L3_ffn_b2"].reshape(8, 128).T)
    return w


def _final_wait(ctx):
    P = ctx.P
    toks = []
    for q in _DMAQ:
        toks += [(f"{q}{i}", P.counts[f"{q}{i}"]) for i in range(NDMASEM)
                 if P.counts[f"{q}{i}"] > 0]
    P.op("sp0", lambda E: E.nop(), deps=toks, inc=0, chain=False)
    P.flush()


def _launch(build_fn, w_names, w, per_core_ins, out_shapes, nl):
    nc = bass.Bass("TRN2", target_bir_lowering=False)
    dram = {}
    for name in w_names:
        dram[name] = nc.dram_tensor(name, list(w[name].shape), F32,
                                    kind="ExternalInput").ap()
    for name, arr0 in per_core_ins[0].items():
        dram[name] = nc.dram_tensor(name, list(arr0.shape), F32,
                                    kind="ExternalInput").ap()
    outs = {}
    for name, shape in list(out_shapes.items()) + list(_DBG_TAPS.items()):
        outs[name] = nc.dram_tensor(name, list(shape), F32,
                                    kind="ExternalOutput").ap()
    build_fn(nc, dram, outs)
    nc.finalize()
    in_maps = []
    for c in range(NCORES):
        m = {n: w[n] for n in w_names}
        m.update(per_core_ins[c])
        in_maps.append(m)
    res = run_bass_kernel_spmd(nc, in_maps, core_ids=list(range(NCORES)))
    nl[0] += 1
    return res.results


L0A_W = ["L0_w_inprojT", "L0_conv_xw", "L0_conv_zw", "L0_w_xprojT",
         "L0_w_dtprojT", "L0_dt_bias", "L0_A_log", "L0_D", "L0_w_outprojT"]
L0F_W = ["L0f_w1T", "L0f_w2T", "L0f_h1T", "L0f_h2T", "L0f_w3T",
         "L0f_s1", "L0f_s2", "L0f_sh1", "L0f_sh2", "L0f_s3"]
L1A_W = ["L1_wqT", "L1_wkT", "L1_wvT", "L1_bq", "L1_bk", "L1_Dt"]


def _mk_build(emit_fn):
    def build(nc, dram, outs):
        ctx = Ctx(nc, dram)
        ctx.taps = outs
        try:
            ctx.emit_consts()
            emit_fn(ctx, dram["x_st"], outs["x_out"])
        except _StopEmit:
            ctx.P.pend = []
        _final_wait(ctx)
    return build


def build_l3f1(nc, dram, outs):
    ctx = Ctx(nc, dram)
    ctx.taps = outs
    try:
        ctx.emit_consts()
        emit_l3f1(ctx, dram["x_st"], outs["xln8"], outs["lamax"])
    except _StopEmit:
        ctx.P.pend = []
    _final_wait(ctx)


def build_l3f2(nc, dram, outs):
    ctx = Ctx(nc, dram)
    ctx.taps = outs
    try:
        ctx.emit_consts()
        emit_l3f2(ctx, dram, outs)
    except _StopEmit:
        ctx.P.pend = []
    _final_wait(ctx)


def build_l3f3(nc, dram, outs):
    ctx = Ctx(nc, dram)
    ctx.taps = outs
    try:
        ctx.emit_consts()
        emit_l3f3(ctx, dram, outs)
    except _StopEmit:
        ctx.P.pend = []
    _final_wait(ctx)


FUSED3 = True


def build_fusedA(nc, dram, outs):
    ctx = Ctx(nc, dram)
    ctx.taps = outs
    xa = nc.dram_tensor("xa", [D, T], F32, kind="Internal").ap()
    xm = outs["x_mid"]
    seq = [(emit_l0a, dram["x_st"], xm), (emit_l0f, xm, xa),
           (emit_l1a, xa, xm), (emit_l1f, xm, xa),
           (emit_l2a, xa, xm), (emit_l2f, xm, xa),
           (emit_l3a, xa, xm)]
    try:
        ctx.emit_consts()
        for fn, xi, xo in seq:
            fn(ctx, xi, xo)
            ctx.flush()
        emit_l3f1(ctx, xm, outs["xln8"], outs["lamax"])
    except _StopEmit:
        ctx.P.pend = []
    _final_wait(ctx)


def kernel(**inp):
    w = _prep_weights(inp)
    x0 = inp["L0_attn_hidden_states"]
    x_fm = [np.ascontiguousarray(x0[c].T.astype(np.float32))
            for c in range(NCORES)]
    nl = [0]

    if (FUSED3 and _N_SUB is None and _SUB_START == 0
            and not os.environ.get("BASS_SIM_FP64")):
        allw = [k for k in w if k not in
                ("L3f_w1T", "L3f_b1_pc", "L3f_w2T", "L3f_b2_pc")]
        r1 = _launch(build_fusedA, allw, w,
                     [dict(x_st=x_fm[c]) for c in range(NCORES)],
                     dict(x_mid=(D, T), xln8=(D, T), lamax=(1, 1)), nl)
        amax8 = np.concatenate([r1[c]["lamax"].reshape(1)
                                for c in range(NCORES)])[None, :]
        r2 = _launch(build_l3f2, ["L3f_w1T", "L3f_b1_pc"], w,
                     [dict(xln8=r1[c]["xln8"], amax8=amax8)
                      for c in range(NCORES)],
                     dict(h=(4096, T), lamax2=(1, 1)), nl)
        hamax8 = np.concatenate([r2[c]["lamax2"].reshape(1)
                                 for c in range(NCORES)])[None, :]
        r3 = _launch(build_l3f3, ["L3f_w2T", "L3f_b2_pc"], w,
                     [dict(h=r2[c]["h"], hamax8=hamax8,
                           x_st=r1[c]["x_mid"]) for c in range(NCORES)],
                     dict(x_out=(D, T)), nl)
        y = np.stack([r3[c]["x_out"].T for c in range(NCORES)], axis=0)
        return np.ascontiguousarray(y.astype(np.float32))

    chain = [(emit_l0a, L0A_W), (emit_l0f, L0F_W), (emit_l1a, L1A_W),
             (emit_l1f, ["L1_rhs"]),
             (emit_l2a, ["L2_w_attn", "L2_b_attn", "L2_b_attn_pc",
                         "L2_w_proj"]),
             (emit_l2f, ["L2f_router", "L2f_qkvT", "L2f_wqkv_m", "L2f_wo_d",
                         "L2f_wo_m", "L2f_costok", "L2f_csg"]),
             (emit_l3a, ["L3_W_in2", "L3_b_in2", "L3_b_tail", "L3_A_log",
                         "L3_W_out", "L3_b_out_pc"])]
    if _N_SUB is not None:
        chain = chain[:_N_SUB]
    chain = chain[_SUB_START:]
    if _X_OVERRIDE is not None:
        x_fm = [np.ascontiguousarray(np.asarray(_X_OVERRIDE)[c].T.astype(
            np.float32)) for c in range(NCORES)]
    for emit_fn, wn in chain:
        res = _launch(_mk_build(emit_fn), wn, w,
                      [dict(x_st=x_fm[c]) for c in range(NCORES)],
                      dict(x_out=(D, T)), nl)
        x_fm = [res[c]["x_out"] for c in range(NCORES)]

    if _N_SUB is None or _N_SUB > 7:
        r1 = _launch(build_l3f1, [], w,
                     [dict(x_st=x_fm[c]) for c in range(NCORES)],
                     dict(xln8=(D, T), lamax=(1, 1)), nl)
        amax8 = np.concatenate([r1[c]["lamax"].reshape(1)
                                for c in range(NCORES)])[None, :]
        r2 = _launch(build_l3f2, ["L3f_w1T", "L3f_b1_pc"], w,
                     [dict(xln8=r1[c]["xln8"], amax8=amax8)
                      for c in range(NCORES)],
                     dict(h=(4096, T), lamax2=(1, 1)), nl)
        hamax8 = np.concatenate([r2[c]["lamax2"].reshape(1)
                                 for c in range(NCORES)])[None, :]
        r3 = _launch(build_l3f3, ["L3f_w2T", "L3f_b2_pc"], w,
                     [dict(h=r2[c]["h"], hamax8=hamax8, x_st=x_fm[c])
                      for c in range(NCORES)],
                     dict(x_out=(D, T)), nl)
        x_fm = [r3[c]["x_out"] for c in range(NCORES)]

    y = np.stack([x_fm[c].T for c in range(NCORES)], axis=0)
    return np.ascontiguousarray(y.astype(np.float32))
